# version 21
# baseline (speedup 1.0000x reference)
import os
import numpy as np
from contextlib import ExitStack

import concourse.bass as bass
import concourse.mybir as mybir
from concourse.bass_utils import run_bass_kernel_spmd

F32 = mybir.dt.float32
BF16 = mybir.dt.bfloat16
AF = mybir.ActivationFunctionType
ALU = mybir.AluOpType
AX = mybir.AxisListType

NEG = -1.0e30


class Cfg:
    def __init__(self, C, T, INTER, RINTER):
        self.C, self.T, self.INTER, self.RINTER = C, T, INTER, RINTER
        self.PC = C // 128
        self.H = self.PC
        self.HAUG = self.H * 129
        self.OWN = T // 2
        self.A = self.OWN // 2
        self.HALF = self.OWN // 2
        self.NQT = self.OWN // 128
        self.QTH = self.NQT // 2
        self.KB = 512
        self.NKB = T // self.KB
        self.CHK = 512
        self.SLAB = 512
        self.CH2 = C // 2
        self.eps = 1e-5
        self.qscale = 1.0 / np.sqrt(128.0)


FULL = Cfg(C=2048, T=4096, INTER=8192, RINTER=1024)
MINI = Cfg(C=256, T=1024, INTER=1024, RINTER=512)


def qpos_of(cfg, cls, i):
    return 256 * i + 128 * cls


def cov_of(cfg, i):
    mx = max(qpos_of(cfg, 0, i), qpos_of(cfg, 1, i))
    return (mx + 128 + cfg.KB - 1) // cfg.KB


def open_of(cfg, i):
    mn = min(qpos_of(cfg, 0, i), qpos_of(cfg, 1, i))
    return mn // cfg.KB


def mask_pairs(cfg):
    out = []
    for half in range(2):
        qts = range(half * cfg.QTH, (half + 1) * cfg.QTH)
        maxcov = max(cov_of(cfg, i) for i in qts)
        for kb in range(maxcov):
            for i in qts:
                if open_of(cfg, i) <= kb < cov_of(cfg, i):
                    out.append((half, kb, i))
    return out


def build_masks(cfg, cls):
    pairs = mask_pairs(cfg)
    m = np.zeros((len(pairs), 128, cfg.KB), np.float32)
    kr = np.arange(cfg.KB)
    qr = np.arange(128)
    for n, (_h, kb, i) in enumerate(pairs):
        qg = qpos_of(cfg, cls, i) + qr[:, None]
        kg = kb * cfg.KB + kr[None, :]
        m[n] = np.where(kg <= qg, 0.0, NEG).astype(np.float32)
    return m



ENGS = ("sync", "tensor", "vector", "scalar", "gpsimd")
_ESEM = {"sync": "SY", "gpsimd": "G", "scalar": "S", "vector": "V", "tensor": "T"}


def _flat(reads):
    out = []
    for x in reads:
        if x is None:
            continue
        if isinstance(x, (list, tuple)) and x and isinstance(x[0], (list, tuple)):
            out.extend(_flat(x))
        elif isinstance(x, (list, tuple)) and len(x) == 2 and isinstance(x[0], str):
            out.append(x)
        elif isinstance(x, (list, tuple)):
            out.extend(_flat(x))
    return out


NSYNC_LANES = 6
NGP_LANES = 4
SYNC_LANES = [f"DS{i}" for i in range(NSYNC_LANES)]
GP_LANES = [f"DG{i}" for i in range(NGP_LANES)]


class Rec:

    def __init__(self):
        self.items = {e: [] for e in ENGS}
        self.cnt = {}
        self.pending = {e: [] for e in ENGS}
        self.lane_rr = {"sync": 0, "gpsimd": 0}

    def op(self, eng, fn, reads=(), sem=None, inc=1):
        reads = _flat(list(reads))
        if self.pending[eng]:
            reads = self.pending[eng] + reads
            self.pending[eng] = []
        tok = None
        if sem is not None:
            self.cnt[sem] = self.cnt.get(sem, 0) + inc
            tok = (sem, self.cnt[sem])
        self.items[eng].append({"reads": reads, "fn": fn, "sem": sem,
                                "inc": inc if sem else 0})
        return tok

    def dma_batch(self, eng, pairs, reads=(), sem=None):
        if sem is None:
            lanes = SYNC_LANES if eng == "sync" else GP_LANES
            i = self.lane_rr[eng]
            sem = lanes[i % len(lanes)]
            self.lane_rr[eng] = i + 1
        gate = (sem, self.cnt.get(sem, 0))
        tok = None
        for n, (out, in_) in enumerate(pairs):
            tok = self.op(eng, (lambda o, i_: lambda e: e.dma_start(out=o, in_=i_))(out, in_),
                          reads=(list(reads) + [gate]) if n == 0 else [],
                          sem=sem, inc=16)
        return tok

    def dma(self, eng, out, in_, reads=(), sem=None):
        return self.dma_batch(eng, [(out, in_)], reads=reads, sem=sem)

    def phase_end(self):
        toks = []
        for e in ENGS:
            if not self.items[e]:
                continue
            it = self.items[e][-1]
            if it["sem"] is None:
                it["sem"] = _ESEM[e]
                it["inc"] = 1
                self.cnt[it["sem"]] = self.cnt.get(it["sem"], 0) + it["inc"]
            toks.append((it["sem"], self.cnt[it["sem"]]))
        for lane in SYNC_LANES + GP_LANES:
            if self.cnt.get(lane, 0):
                toks.append((lane, self.cnt[lane]))
        toks = list({s: (s, v) for s, v in toks}.values())
        for e in ENGS:
            self.pending[e] = list(toks)
        return toks

    def emit(self, blk, sems):
        for ename in ENGS:
            stream = self.items[ename]
            if not stream:
                continue

            def runner(e, stream=stream):
                waited = {}
                for it in stream:
                    for s, v in it["reads"]:
                        if waited.get(s, 0) < v:
                            e.wait_ge(sems[s], v)
                            waited[s] = v
                    ins = it["fn"](e)
                    if it["sem"] is not None:
                        ins.then_inc(sems[it["sem"]], it["inc"])
            getattr(blk, ename)(runner)



def build(cfg, dbg=False, lim=99):
    PH_LVL = int(os.environ.get("PH_LVL", "99"))
    nc = bass.Bass("TRN2", target_bir_lowering=False)
    C, T, PC, H = cfg.C, cfg.T, cfg.PC, cfg.H
    OWN, HALF, KB, CHK, SLAB = cfg.OWN, cfg.HALF, cfg.KB, cfg.CHK, cfg.SLAB
    HAUG, CH2 = cfg.HAUG, cfg.CH2
    QTH = cfg.QTH
    assert CHK == KB
    assert cfg.INTER % SLAB == 0 and cfg.RINTER % SLAB == 0
    TW = min(512, HALF)
    NTK = HALF // TW
    YW = min(1024, C)
    NYC = C // YW
    QD = min(4, PC)
    NQD = PC // QD
    VDW = min(512, C)
    NVD = C // VDW
    CCW = min(512, CH2)
    NCC = CH2 // CCW
    NST = max(C // 512, 1)
    STW = min(512, C)

    pairs = mask_pairs(cfg)
    NMASK = len(pairs)

    class _Lazy:
        def __init__(self, name, shape, dtype):
            self.args = (name, shape, dtype)
            self._ap = None

        def __getitem__(self, key):
            if self._ap is None:
                self._ap = nc.dram_tensor(self.args[0], self.args[1], self.args[2],
                                          kind="ExternalInput").ap()
            return self._ap[key]

    x_seq = _Lazy("x_seq", [T, C], F32)
    x_own = _Lazy("x_own", [OWN, C], F32)
    qkvT = _Lazy("qkvT", [C, 3 * C], F32)
    qkvb_cols = _Lazy("qkvb_cols", [128, 3 * PC], F32)
    qkvb_flat = _Lazy("qkvb_flat", [3 * C], F32)
    outwT = _Lazy("outwT", [C, C], F32)
    outb = _Lazy("outb", [C], F32)
    gT = _Lazy("gT", [C, cfg.INTER], F32)
    uT = _Lazy("uT", [C, cfg.INTER], F32)
    dT = _Lazy("dT", [cfg.INTER, C], F32)
    rgT = _Lazy("rgT", [C, cfg.RINTER], F32)
    ruT = _Lazy("ruT", [C, cfg.RINTER], F32)
    rdT = _Lazy("rdT", [cfg.RINTER, C], F32)
    routerT = _Lazy("routerT", [C, 8], F32)
    masks = _Lazy("masks", [max(NMASK, 1), 128, KB], F32)
    identin = _Lazy("identin", [128, 128], BF16)
    out_own = nc.dram_tensor("out_own", [OWN, C], F32, kind="ExternalOutput").ap()

    skind = "ExternalOutput" if dbg else "Internal"
    ln1s = nc.dram_tensor("ln1s", [C, T], BF16, kind=skind).ap()
    Ks = nc.dram_tensor("Ks", [C, T], BF16, kind=skind).ap()
    Vs = nc.dram_tensor("Vs", [T, HAUG], BF16, kind=skind).ap()
    Qs = nc.dram_tensor("Qs", [C, OWN], BF16, kind=skind).ap()
    dbg_y = nc.dram_tensor("dbg_y", [OWN, C], F32, kind="ExternalOutput").ap() if dbg else None
    dbg_ot = nc.dram_tensor("dbg_ot", [2, C, HALF], BF16, kind="ExternalOutput").ap() if dbg else None
    dbg_l2 = nc.dram_tensor("dbg_l2", [2, C, HALF], BF16, kind="ExternalOutput").ap() if dbg else None

    es = ExitStack()
    with es:
        LN2F0 = max(2 * PC * HALF, PC * HALF + PC * CH2, 4 * PC * SLAB)
        ARB_W = max(PC * C + 3 * PC * CHK,
                    PC * C + 2 * PC * CHK + 2 * HAUG + C,
                    PC * HALF + 2 * PC * KB + 2 * (KB // 128) * HAUG + 8 * KB + 512,
                    LN2F0 + PC * HALF + C,
                    LN2F0 + PC * HALF + (SLAB // 128) * C)
        ARF_W = max(QTH * HAUG + 2 * KB, QTH * C + 2 * C, QTH * C + HALF)
        ARB = es.enter_context(nc.sbuf_tensor("ARB", [128, ARB_W], BF16))
        ARF = es.enter_context(nc.sbuf_tensor("ARF", [128, ARF_W], F32))
        TMPBF = es.enter_context(nc.sbuf_tensor("TMPBF", [128, 2 * C + CHK], BF16))
        SMB = es.enter_context(nc.sbuf_tensor("SMB", [128, 256 + PC * 8 + H + CHK], BF16))
        SMF = es.enter_context(nc.sbuf_tensor("SMF", [128, 256 + QTH * H * 2 + 16 + QTH + cfg.NQT * H], F32))

        ar, af, tb, smb, smf = ARB.ap(), ARF.ap(), TMPBF.ap(), SMB.ap(), SMF.ap()

        ident = smb[:, 0:128]
        ones_bf = smb[:, 128:256]
        router_sb = smb[:, 256:256 + PC * 8]
        kmx2row = smb[0:1, 256 + PC * 8:256 + PC * 8 + H]
        rowtmp = smb[0:1, 256 + PC * 8 + H:256 + PC * 8 + H + CHK]

        qkvb_sb = smf[:, 0:3 * PC]
        qb_scaled = smf[:, 3 * PC:4 * PC]
        bnst = lambda p: smf[:, 64 + (p % 2) * 24: 64 + (p % 2) * 24 + 24]
        aggr = lambda p: smf[:, 112 + (p % 2) * 2: 112 + (p % 2) * 2 + 2]
        sdv = lambda p: smf[:, 116 + (p % 2): 117 + (p % 2)]
        rstd = lambda p: smf[:, 118 + (p % 2): 119 + (p % 2)]
        nmr = lambda p: smf[:, 120 + (p % 2): 121 + (p % 2)]
        nmcol = lambda p: smf[:, 128 + (p % 4): 129 + (p % 4)]
        dcol = lambda p: smf[:, 132 + (p % 4): 133 + (p % 4)]
        acol = lambda p: smf[:, 136 + (p % 4): 137 + (p % 4)]
        lrec = lambda p: smf[:, 140 + (p % 4):141 + (p % 4)]
        rlog = lambda p: smf[:, 144 + (p % 2) * 8: 144 + (p % 2) * 8 + 8]
        rtmp = smf[:, 160:168]
        rst = lambda j: smf[:, 168 + j:169 + j]
        MB0 = 192
        mcol = lambda qt, hd, par: smf[:, MB0 + ((qt % QTH) * H + hd) * 2 + (par % 2):
                                       MB0 + ((qt % QTH) * H + hd) * 2 + (par % 2) + 1]
        CC0 = MB0 + QTH * H * 2
        ccol = lambda qt: smf[:, CC0 + (qt % QTH): CC0 + (qt % QTH) + 1]
        NQ0 = CC0 + QTH
        nq2col = lambda qt, hd: smf[:, NQ0 + qt * H + hd: NQ0 + qt * H + hd + 1]

        XB0 = QTH * C
        xslot = lambda p: af[:, XB0 + (p % 2) * C: XB0 + (p % 2) * C + C]
        lnbf = lambda p: tb[:, (p % 2) * C:(p % 2) * C + C]
        sqA = tb[:, 2 * C:2 * C + CHK]

        semnames = ["T", "V", "S", "G", "SY"] + SYNC_LANES + GP_LANES
        sems = {n: es.enter_context(nc.semaphore(n)) for n in semnames}
        r = Rec()

        def ln_chain(xtile, par, outbf, reads0):
            t = None
            for j in range(NST):
                t = r.op("vector", lambda e, j=j: e.bn_stats(
                    out=bnst(par)[:, 6 * j:6 * j + 6], in_=xtile[:, j * STW:(j + 1) * STW]),
                    reads=(reads0 if j == 0 else []), sem="V" if j == NST - 1 else None)
            t_ag = r.op("vector", lambda e: e.bn_aggr(
                out=aggr(par), in_=bnst(par)[:, 0:6 * NST]), reads=[t], sem="V")
            t_vp = r.op("vector", lambda e: e.tensor_scalar_add(
                out=sdv(par), in0=aggr(par)[:, 1:2], scalar1=float(cfg.eps)),
                reads=[t_ag], sem="V")
            t_sd = r.op("scalar", lambda e: e.activation(
                out=sdv(par), in_=sdv(par), func=AF.Sqrt),
                reads=[t_vp], sem="S")
            t_rs = r.op("vector", lambda e: e.reciprocal(
                out=rstd(par), in_=sdv(par)), reads=[t_sd], sem="V")
            t_nm = r.op("vector", lambda e: e.tensor_scalar(
                out=nmr(par), in0=aggr(par)[:, 0:1], scalar1=rstd(par), scalar2=-1.0,
                op0=ALU.mult, op1=ALU.mult), reads=[t_rs], sem="V")
            t_norm = r.op("scalar", lambda e: e.activation(
                out=outbf, in_=xtile, func=AF.Identity, bias=nmr(par), scale=rstd(par)),
                reads=[t_nm], sem="S")
            return t_norm

        psA = ExitStack()
        pk = [psA.enter_context(nc.psum_tensor(f"pkA{i}", [128, CHK], F32)) for i in range(QD)]
        ptr = [psA.enter_context(nc.psum_tensor(f"ptA{i}", [128, 128], BF16)) for i in range(3)]
        pnA = psA.enter_context(nc.psum_tensor("pnA", [1, CHK], F32)).ap()
        pcol = [ptr[2].ap()[:, 0:1]]

        wk = ar[:, 0:PC * C]
        ln1f = lambda p: ar[:, PC * C + (p % 2) * PC * CHK: PC * C + (p % 2 + 1) * PC * CHK]
        kch = ar[:, PC * C + 2 * PC * CHK: PC * C + 3 * PC * CHK]

        t_ident = t_qb = r.dma_batch("sync", [(ident, identin[:, :]),
                                              (qkvb_sb, qkvb_cols[:, :])])
        t_ones = r.op("vector", lambda e: e.memset(ones_bf, 1.0), sem="V")
        t_qbs = r.op("vector", lambda e: e.tensor_scalar_mul(
            out=qb_scaled, in0=qkvb_sb[:, 0:PC], scalar1=float(cfg.qscale)),
            reads=[t_qb], sem="V")

        def phase_AC(src_x, nch, wsec, bias_cols, scale, dst, lnspill, knorm=False,
                     qnorm=False):
            hp = max(PC // 2, 1)
            t_w0 = r.dma_batch("gpsimd", [
                (wk[:, p * C:(p + 1) * C],
                 qkvT[p * 128:(p + 1) * 128, wsec * C:(wsec + 1) * C])
                for p in range(hp)])
            t_w1 = r.dma_batch("gpsimd", [
                (wk[:, p * C:(p + 1) * C],
                 qkvT[p * 128:(p + 1) * 128, wsec * C:(wsec + 1) * C])
                for p in range(hp, PC)]) if PC > 1 else t_w0
            guard = {}
            for ch in range(nch):
                lf = ln1f(ch)
                ev = None
                for tl in range(CHK // 128):
                    t0 = ch * CHK + tl * 128
                    xp = (ch * (CHK // 128) + tl) % 2
                    xtok = r.dma("sync", xslot(xp), src_x[t0:t0 + 128, :],
                                 reads=[guard.get(("xfree", xp))])
                    t_norm = ln_chain(xslot(xp), xp, lnbf(xp),
                                      [xtok, guard.get(("lnbf_free", xp))])
                    guard[("xfree", xp)] = t_norm
                    if PH_LVL < 2:
                        guard[("lnbf_free", xp)] = t_norm
                        continue
                    for p in range(PC):
                        pp = p % 3
                        t_tr = r.op("tensor", lambda e, p=p, pp=pp, xp=xp: e.transpose(
                            out=ptr[pp].ap()[:, :], in_=lnbf(xp)[:, p * 128:(p + 1) * 128],
                            identity=ident),
                            reads=[t_norm, t_ident, guard.get(("trev", pp)),
                                   (guard.get("pcolfree") if pp == 2 else None)], sem="T")
                        ev = r.op("vector", lambda e, p=p, pp=pp, tl=tl, lf=lf: e.tensor_copy(
                            out=lf[:, p * CHK + tl * 128: p * CHK + tl * 128 + 128],
                            in_=ptr[pp].ap()[:, :]),
                            reads=[t_tr, guard.get(("lfw", ch % 2))], sem="V")
                        guard[("trev", pp)] = ev
                    guard[("lnbf_free", xp)] = ev
                lf_ready = ev
                mm_last = None
                kevac = None
                if PH_LVL < 3:
                    continue
                for quad in range(NQD):
                    for ci in range(PC):
                        for j in range(QD):
                            od = quad * QD + j
                            mm_last = r.op("tensor", lambda e, ci=ci, od=od, j=j, lf=lf: e.matmul(
                                out=pk[j][:, :],
                                lhsT=wk[:, ci * C + od * 128: ci * C + od * 128 + 128],
                                rhs=lf[:, ci * CHK: ci * CHK + CHK],
                                start=(ci == 0), stop=(ci == PC - 1)),
                                reads=([lf_ready, t_w0, guard.get("kevac_all"), t_qbs]
                                       if (ci == 0 and j == 0) else
                                       ([t_w1] if (ci == hp and j == 0) else [])),
                                sem="T" if (ci == PC - 1 and j == QD - 1) else None)
                    for j in range(QD):
                        od = quad * QD + j
                        kevac = r.op("scalar", lambda e, od=od, j=j: e.activation(
                            out=kch[:, od * CHK:(od + 1) * CHK], in_=pk[j][:, :],
                            func=AF.Identity, bias=bias_cols[:, od:od + 1],
                            scale=float(scale)),
                            reads=[mm_last, guard.get("spill")], sem="S")
                        if knorm:
                            t_sq = r.op("gpsimd", lambda e, od=od: e.tensor_mul(
                                out=sqA, in0=kch[:, od * CHK:(od + 1) * CHK],
                                in1=kch[:, od * CHK:(od + 1) * CHK]),
                                reads=[kevac, guard.get("sqfree")], sem="G")
                            t_pn = r.op("tensor", lambda e: e.matmul(
                                out=pnA[:, :], lhsT=ones_bf[:, 0:1], rhs=sqA,
                                start=True, stop=True),
                                reads=[t_sq, t_ones, guard.get("pnfree")], sem="T")
                            guard["sqfree"] = t_pn
                            t_mx = r.op("vector", lambda e: e.reduce_max(
                                out=rowtmp[0:1, 0:1], in_=pnA[0:1, :], axis=AX.X),
                                reads=[t_pn, guard.get("rowfree")], sem="V")
                            guard["pnfree"] = t_mx
                            if guard.get(("km", od)) is None:
                                t_km = r.op("vector", lambda e, od=od: e.tensor_copy(
                                    out=kmx2row[0:1, od:od + 1], in_=rowtmp[0:1, 0:1]),
                                    reads=[t_mx], sem="V")
                            else:
                                t_km = r.op("vector", lambda e, od=od: e.tensor_tensor(
                                    out=kmx2row[0:1, od:od + 1], in0=kmx2row[0:1, od:od + 1],
                                    in1=rowtmp[0:1, 0:1], op=ALU.max),
                                    reads=[t_mx], sem="V")
                            guard[("km", od)] = t_km
                            guard["rowfree"] = t_km
                            r.kmtok = t_km
                    guard["kevac_all"] = kevac
                if PH_LVL >= 4:
                    sp = r.dma_batch("sync", [
                        (dst[od * 128:(od + 1) * 128, ch * CHK:(ch + 1) * CHK],
                         kch[:, od * CHK:(od + 1) * CHK]) for od in range(PC)],
                        reads=[kevac])
                    guard["spill"] = sp
                if qnorm:
                    for od in range(PC):
                        t_sq = r.op("gpsimd", lambda e, od=od: e.tensor_mul(
                            out=sqA, in0=kch[:, od * CHK:(od + 1) * CHK],
                            in1=kch[:, od * CHK:(od + 1) * CHK]),
                            reads=[kevac, guard.get("sqfree")], sem="G")
                        t_pn = r.op("tensor", lambda e: e.matmul(
                            out=pnA[:, :], lhsT=ones_bf[:, 0:1], rhs=sqA,
                            start=True, stop=True),
                            reads=[t_sq, t_ones, guard.get("pnfree")], sem="T")
                        guard["sqfree"] = t_pn
                        t_rw = r.op("vector", lambda e: e.tensor_copy(
                            out=rowtmp[0:1, 0:CHK], in_=pnA[0:1, :]),
                            reads=[t_pn, guard.get("rowfree")], sem="V")
                        guard["pnfree"] = t_rw
                        t_tc = None
                        for tl in range(CHK // 128):
                            qt_g = (ch * CHK) // 128 + tl
                            t_tc = r.op("tensor", lambda e, tl=tl: e.transpose(
                                out=pcol[0][:, :],
                                in_=rowtmp[0:1, tl * 128:(tl + 1) * 128],
                                identity=ident[0:1, 0:1]),
                                reads=[t_rw, guard.get("pcolfree"),
                                       guard.get(("trev", 2))], sem="T")
                            t_ev = r.op("scalar", lambda e, qt_g=qt_g, od=od, tl=tl: e.copy(
                                out=nq2col(qt_g, od), in_=pcol[0][:, :]),
                                reads=[t_tc], sem="S")
                            guard["pcolfree"] = t_ev
                            guard[("nq2", qt_g, od)] = t_ev
                        guard["rowfree"] = t_tc
                    r.nq2 = {k: v for k, v in guard.items() if isinstance(k, tuple) and k[0] == "nq2"}                         if not hasattr(r, "nq2") else {**r.nq2, **{k: v for k, v in guard.items() if isinstance(k, tuple) and k[0] == "nq2"}}
                if lnspill and PH_LVL >= 4:
                    sp2 = r.dma_batch("sync", [
                        (ln1s[p * 128:(p + 1) * 128, ch * CHK:(ch + 1) * CHK],
                         lf[:, p * CHK:(p + 1) * CHK]) for p in range(PC)],
                        reads=[lf_ready])
                    guard[("lfw", ch % 2)] = [mm_last, sp2]
                else:
                    guard[("lfw", ch % 2)] = [mm_last, lf_ready]
            r.phase_end()

        if lim >= 1:
            phase_AC(x_seq, T // CHK, 1, qkvb_sb[:, PC:2 * PC], 1.0, Ks, lnspill=True,
                     knorm=True)

        wv = wk
        VST0 = PC * C + 2 * PC * CHK
        vst = lambda p: ar[:, VST0 + (p % 2) * HAUG: VST0 + (p % 2 + 1) * HAUG]
        bv_row = ar[0:1, VST0 + 2 * HAUG: VST0 + 2 * HAUG + C]

        if lim >= 2:
            hpv = max(PC // 2, 1)
            t_wv0 = r.dma_batch("gpsimd", [
                (wv[:, p * C:(p + 1) * C], qkvT[p * 128:(p + 1) * 128, 2 * C:3 * C])
                for p in range(hpv)])
            t_wv = r.dma_batch("gpsimd", [
                (wv[:, p * C:(p + 1) * C], qkvT[p * 128:(p + 1) * 128, 2 * C:3 * C])
                for p in range(hpv, PC)]) if PC > 1 else t_wv0
            t_bv = r.dma("gpsimd", bv_row, qkvb_flat[None, 2 * C:3 * C])
            t_vi0 = r.op("vector", lambda e: e.memset(
                vst(0).rearrange("p (h x) -> p h x", x=129)[:, :, 128:129], 1.0), sem="V")
            t_vi1 = r.op("vector", lambda e: e.memset(
                vst(1).rearrange("p (h x) -> p h x", x=129)[:, :, 128:129], 1.0), sem="V")

            gB = {}
            for ch in range(T // CHK):
                lf = ln1f(ch)
                t_lf = r.dma_batch("sync", [
                    (lf[:, p * CHK:(p + 1) * CHK],
                     ln1s[p * 128:(p + 1) * 128, ch * CHK:(ch + 1) * CHK])
                    for p in range(PC)], reads=[gB.get(("lfree", ch % 2))])
                mm_last = None
                for tl in range(CHK // 128):
                    nh = VDW // 128
                    for vd in range(NVD):
                        mm_last = r.op("tensor", lambda e, vd=vd: e.matmul(
                            out=pk[vd][:, 0:VDW], lhsT=ones_bf[0:1, 0:128],
                            rhs=bv_row[0:1, vd * VDW:(vd + 1) * VDW], start=True, stop=False),
                            reads=[t_lf, t_bv, t_ones, t_vi0, t_vi1,
                                   gB.get(("pvfree", vd))], sem=None)
                    for ci in range(PC):
                        for vd in range(NVD):
                            mm_last = r.op("tensor", lambda e, ci=ci, vd=vd, tl=tl, lf=lf: e.matmul(
                                out=pk[vd][:, 0:VDW],
                                lhsT=lf[:, ci * CHK + tl * 128: ci * CHK + tl * 128 + 128],
                                rhs=wv[:, ci * C + vd * VDW: ci * C + vd * VDW + VDW],
                                start=False, stop=(ci == PC - 1)),
                                reads=([t_wv0] if (ci == 0 and vd == 0) else
                                       ([t_wv] if (ci == hpv and vd == 0) else [])),
                                sem="T" if (ci == PC - 1 and vd == NVD - 1) else None)
                    tvev = None
                    for vd in range(NVD):
                        tvev = r.op("vector", lambda e, vd=vd, tl=tl, nh=nh: e.tensor_copy(
                            out=vst(tl).rearrange("p (h x) -> p h x", x=129)[:, vd * nh:(vd + 1) * nh, 0:128],
                            in_=pk[vd][:, 0:VDW].rearrange("p (h x) -> p h x", x=128)),
                            reads=[mm_last, gB.get(("vsfree", tl % 2))], sem="V")
                        gB[("pvfree", vd)] = tvev
                    t0 = ch * CHK + tl * 128
                    gB[("vsfree", tl % 2)] = r.dma("sync", Vs[t0:t0 + 128, :], vst(tl),
                                                   reads=[tvev])
                gB[("lfree", ch % 2)] = mm_last
            r.phase_end()

        if lim >= 3:
            phase_AC(x_own, OWN // CHK, 0, qb_scaled, cfg.qscale, Qs, lnspill=False,
                     qnorm=True)

        psA.close()

        qbuf = ar[:, 0:PC * HALF]
        KB0 = PC * HALF
        kbuf = lambda p: ar[:, KB0 + (p % 2) * PC * KB: KB0 + (p % 2 + 1) * PC * KB]
        VB0 = KB0 + 2 * PC * KB
        vbuf = lambda p: ar[:, VB0 + (p % 2) * (KB // 128) * HAUG:
                            VB0 + (p % 2 + 1) * (KB // 128) * HAUG]
        PB0 = VB0 + 2 * (KB // 128) * HAUG
        pbuf = lambda p: ar[:, PB0 + (p % 4) * KB: PB0 + (p % 4 + 1) * KB]
        PTB0 = PB0 + 4 * KB
        ptb = lambda p: ar[:, PTB0 + (p % 4) * KB: PTB0 + (p % 4 + 1) * KB]
        OFB0 = PTB0 + 4 * KB
        ofbf = lambda p: ar[:, OFB0 + (p % 4) * 128: OFB0 + (p % 4 + 1) * 128]

        obuf = lambda qt: af[:, (qt % QTH) * HAUG: (qt % QTH + 1) * HAUG]
        MKOFF = QTH * HAUG
        mslot = lambda p: af[:, MKOFF + (p % 2) * KB: MKOFF + (p % 2 + 1) * KB]

        WO0 = PC * HALF
        wo = ar[:, WO0: WO0 + PC * CH2]
        ln2f = ar[:, LN2F0: LN2F0 + PC * HALF]
        OBR0 = LN2F0 + PC * HALF
        ob_row = ar[0:1, OBR0: OBR0 + C]
        ybuf = lambda qt: af[:, (qt % QTH) * C: (qt % QTH + 1) * C]

        gsb = lambda p: ar[:, (p % 2) * PC * SLAB: (p % 2 + 1) * PC * SLAB]
        usb = lambda p: ar[:, 2 * PC * SLAB + (p % 2) * PC * SLAB:
                           2 * PC * SLAB + (p % 2 + 1) * PC * SLAB]
        DS0 = LN2F0 + PC * HALF
        dsb = ar[:, DS0: DS0 + (SLAB // 128) * C]
        hsb = tb[:, 0:(SLAB // 128) * HALF]
        silu_t = af[:, XB0: XB0 + HALF]

        t_rt = r.dma_batch("gpsimd", [
            (router_sb[:, p * 8:(p + 1) * 8], routerT[p * 128:(p + 1) * 128, 0:8])
            for p in range(PC)])

        mask_idx = 0
        out_toks = []
        t_ob = None
        for half in range(2):
            if lim < 4 + 3 * half:
                break
            qts = list(range(half * QTH, (half + 1) * QTH))
            maxcov = max(cov_of(cfg, i) for i in qts)

            psD = ExitStack()
            ps_s = [psD.enter_context(nc.psum_tensor(f"psS{half}{i}", [128, KB], F32)) for i in range(3)]
            ptp = [psD.enter_context(nc.psum_tensor(f"ptD{half}{i}", [128, KB], BF16)).ap()[:, :] for i in range(3)]
            pav = [psD.enter_context(nc.psum_tensor(f"pav{half}{i}", [128, 129], F32)).ap()[:, :] for i in range(2)]

            t_qbuf = r.dma_batch("sync", [
                (qbuf[:, p * HALF:(p + 1) * HALF],
                 Qs[p * 128:(p + 1) * 128, half * HALF:(half + 1) * HALF])
                for p in range(PC)])

            bset = {}
            for qt in qts:
                for hd in range(H):
                    bb = (qt % QTH) * H + hd
                    bpp = bb % 2
                    t_bc = r.op("tensor", lambda e, hd=hd, bpp=bpp: e.matmul(
                        out=pav[bpp][:, 0:1], lhsT=ones_bf[0:1, 0:128],
                        rhs=kmx2row[0:1, hd:hd + 1], start=True, stop=True),
                        reads=[bset.get(("pf", bpp)), r.kmtok, t_ones], sem="T")
                    t_m = r.op("vector", lambda e, qt=qt, hd=hd, bpp=bpp: e.tensor_mul(
                        out=mcol(qt, hd, 1), in0=nq2col(qt, hd), in1=pav[bpp][:, 0:1]),
                        reads=[t_bc, r.nq2[("nq2", qt, hd)], bset.get(("mf", qt, hd))], sem="V")
                    bset[("pf", bpp)] = t_m
                    t_sq = r.op("scalar", lambda e, qt=qt, hd=hd: e.activation(
                        out=mcol(qt, hd, 1), in_=mcol(qt, hd, 1), func=AF.Sqrt),
                        reads=[t_m], sem="S")
                    t_bn = r.op("vector", lambda e, qt=qt, hd=hd: e.tensor_scalar_mul(
                        out=mcol(qt, hd, 0), in0=mcol(qt, hd, 1), scalar1=-1.0),
                        reads=[t_sq], sem="V")
                    bset[(qt, hd)] = t_bn

            kv_free, o_tok, s_free, p_free, pt_free, av_free, m_free = {}, {}, {}, {}, {}, {}, {}
            tup = 0
            last_s_mm = None
            kv_done = {}
            pend = []
            pend2 = []
            bkc = [0]
            avc = [0]

            def flush_mid():
                st = pend.pop(0)
                bp = bkc[0] % 3
                ap_ = bkc[0] % 4
                bkc[0] += 1
                pp = st["pp"]
                t_tr = None
                for j in range(KB // 128):
                    t_tr = r.op("tensor", lambda e, j=j, pp=pp, bp=bp: e.transpose(
                        out=ptp[bp][:, j * 128:(j + 1) * 128],
                        in_=pbuf(pp)[:, j * 128:(j + 1) * 128], identity=ident),
                        reads=([st["t_p"], pt_free.get(bp)] if j == 0 else []),
                        sem="T" if j == KB // 128 - 1 else None)
                t_ptc = r.op("vector", lambda e, ap_=ap_, bp=bp: e.tensor_copy(
                    out=ptb(ap_), in_=ptp[bp][:, :]),
                    reads=[t_tr, av_free.get((ap_, "ptb"))], sem="V")
                p_free[pp] = t_ptc
                pt_free[bp] = t_ptc
                st["ap_"] = ap_
                st["t_ptc"] = t_ptc
                pend2.append(st)

            def flush_back():
                st = pend2.pop(0)
                ap_ = st["ap_"]
                vp_ = avc[0] % 2
                avc[0] += 1
                qt, hd, kp = st["qt"], st["hd"], st["kp"]
                t_av = None
                for j in range(KB // 128):
                    t_av = r.op("tensor", lambda e, j=j, ap_=ap_, vp_=vp_, hd=hd, kp=kp: e.matmul(
                        out=pav[vp_][:, :],
                        lhsT=ptb(ap_)[:, j * 128:(j + 1) * 128],
                        rhs=vbuf(kp)[:, j * HAUG + hd * 129: j * HAUG + hd * 129 + 129],
                        start=(j == 0), stop=(j == KB // 128 - 1)),
                        reads=([st["t_ptc"], av_free.get((vp_, "pav"))] if j == 0 else []),
                        sem="T" if j == KB // 128 - 1 else None)
                av_free[(ap_, "ptb")] = t_av
                osl = obuf(qt).rearrange("p (h x) -> p h x", x=129)[:, hd:hd + 1, :].squeeze(1)
                if st["first"]:
                    t_o = r.op("vector", lambda e, osl=osl, vp_=vp_: e.tensor_copy(
                        out=osl, in_=pav[vp_][:, :]),
                        reads=[t_av, o_tok.get((qt, hd))], sem="V")
                else:
                    t_o = r.op("vector", lambda e, osl=osl, vp_=vp_: e.tensor_add(
                        out=osl, in0=osl, in1=pav[vp_][:, :]),
                        reads=[t_av, o_tok.get((qt, hd))], sem="V")
                o_tok[(qt, hd)] = t_o
                o_tok[(qt, hd, "mf")] = t_o
                av_free[(vp_, "pav")] = t_o
                if st["last_of_kb"]:
                    kv_done[st["kb"]] = t_o
                return t_o

            LAG = 3
            LAG2 = 2
            for kb in range(maxcov):
                kp = kb % 2
                kvpairs = [(kbuf(kp)[:, p * KB:(p + 1) * KB],
                            Ks[p * 128:(p + 1) * 128, kb * KB:(kb + 1) * KB])
                           for p in range(PC)]
                kvpairs += [(vbuf(kp)[:, j * HAUG:(j + 1) * HAUG],
                             Vs[kb * KB + j * 128: kb * KB + (j + 1) * 128, :])
                            for j in range(KB // 128)]
                t_kv = r.dma_batch("sync", kvpairs, reads=[kv_done.get(kb - 2)])
                active = [q for q in qts if kb < cov_of(cfg, q)]
                for qn, qt in enumerate(active):
                    qrel = qt % QTH
                    need_mask = open_of(cfg, qt) <= kb
                    t_mask = None
                    mp = mask_idx % 2
                    if need_mask:
                        t_mask = r.dma("sync", mslot(mp), masks[mask_idx, :, :],
                                       reads=[m_free.get(mp)])
                        mask_idx += 1
                    first = (kb == 0)
                    for hd in range(H):
                        pp = tup % 4
                        ps = tup % 3
                        t_smm = r.op("tensor", lambda e, hd=hd, qrel=qrel, kp=kp, ps=ps: e.matmul(
                            out=ps_s[ps][:, :],
                            lhsT=qbuf[:, hd * HALF + qrel * 128: hd * HALF + qrel * 128 + 128],
                            rhs=kbuf(kp)[:, hd * KB:(hd + 1) * KB],
                            start=True, stop=True),
                            reads=[t_qbuf, t_kv, s_free.get(ps)], sem="T")
                        last_s_mm = t_smm
                        t_sm = t_smm
                        if need_mask:
                            t_sm = r.op("vector", lambda e, ps=ps, mp=mp: e.tensor_add(
                                out=ps_s[ps][:, :], in0=ps_s[ps][:, :], in1=mslot(mp)),
                                reads=[t_smm, t_mask], sem="V")
                            m_free[mp] = t_sm
                        t_p = r.op("scalar", lambda e, pp=pp, ps=ps, qt=qt, hd=hd: e.activation(
                            out=pbuf(pp), in_=ps_s[ps][:, :], func=AF.Exp,
                            bias=mcol(qt, hd, 0)),
                            reads=[t_sm, bset[(qt, hd)], p_free.get(pp)], sem="S")
                        s_free[ps] = t_p
                        pend.append({"pp": pp, "qt": qt, "hd": hd, "kp": kp,
                                     "first": first, "t_p": t_p,
                                     "kb": kb,
                                     "last_of_kb": (qn == len(active) - 1 and hd == H - 1)})
                        if len(pend) > LAG:
                            flush_mid()
                        if len(pend2) > LAG2:
                            flush_back()
                        tup += 1
            while pend:
                flush_mid()
            while pend2:
                flush_back()

            fin_ev = {}
            t_fin_last = None
            fpend = []
            fbk = [0]

            def flush_fin():
                stf = fpend.pop(0)
                fp2 = fbk[0] % 3
                fbk[0] += 1
                t_tr = r.op("tensor", lambda e, fp=stf["fp"], fp2=fp2: e.transpose(
                    out=ptp[fp2][:, 0:128], in_=ofbf(fp), identity=ident),
                    reads=[stf["t_of"], fin_ev.get(("ev", fp2)), last_s_mm], sem="T")
                t_ev = r.op("scalar", lambda e, fp2=fp2, hd=stf["hd"], qrel=stf["qrel"]: e.copy(
                    out=qbuf[:, hd * HALF + qrel * 128: hd * HALF + qrel * 128 + 128],
                    in_=ptp[fp2][:, 0:128]),
                    reads=[t_tr], sem="S")
                fin_ev[("ev", fp2)] = t_ev
                fin_ev[("trdone", stf["fp"])] = t_ev
                return t_ev

            fi = 0
            for qt in qts:
                qrel = qt % QTH
                for hd in range(H):
                    osl = obuf(qt).rearrange("p (h x) -> p h x", x=129)[:, hd:hd + 1, :].squeeze(1)
                    fp = fi % 4
                    t_rl = r.op("vector", lambda e, osl=osl, fp=fp: e.reciprocal(
                        out=lrec(fp), in_=osl[:, 128:129]),
                        reads=[o_tok[(qt, hd)], fin_ev.get(("lfree", fp))], sem="V")
                    t_of = r.op("vector", lambda e, osl=osl, fp=fp: e.tensor_scalar_mul(
                        out=ofbf(fp), in0=osl[:, 0:128], scalar1=lrec(fp)),
                        reads=[t_rl, fin_ev.get(("trdone", fp))], sem="V")
                    fin_ev[("lfree", fp)] = t_of
                    fpend.append({"fp": fp, "t_of": t_of, "hd": hd, "qrel": qrel})
                    if len(fpend) > LAG:
                        t_fin_last = flush_fin()
                    fi += 1
            while fpend:
                t_fin_last = flush_fin()
            if dbg:
                r.dma_batch("sync", [
                    (dbg_ot[half, p * 128:(p + 1) * 128, :], qbuf[:, p * HALF:(p + 1) * HALF])
                    for p in range(PC)], reads=[t_fin_last])
            r.phase_end()

            if lim < 5 + 3 * half:
                psD.close()
                break
            y_tok, x_used, xfree, ln2_toks = {}, {}, {}, {}
            mm = None
            for coh in range(2):
                wt = r.dma_batch("gpsimd", [
                    (wo[:, p * CH2:(p + 1) * CH2],
                     outwT[p * 128:(p + 1) * 128, coh * CH2:(coh + 1) * CH2])
                    for p in range(PC)], reads=[y_tok.get("wofree")])
                if coh == 0:
                    t_ob = r.dma("gpsimd", ob_row, outb[None, :])
                unit = 0
                for qt in qts:
                    qrel = qt % QTH
                    xp = qt % 2
                    x_used[(qt, coh)] = r.dma("sync", xslot(xp),
                                              x_own[qt * 128:(qt + 1) * 128, :],
                                              reads=[xfree.get(xp)])
                    for cc in range(NCC):
                        up = unit % 2
                        mm = r.op("tensor", lambda e, coh=coh, cc=cc, up=up: e.matmul(
                            out=ps_s[up][:, 0:CCW], lhsT=ones_bf[0:1, 0:128],
                            rhs=ob_row[0:1, coh * CH2 + cc * CCW: coh * CH2 + cc * CCW + CCW],
                            start=True, stop=False),
                            reads=[wt, t_ob, y_tok.get(("pofree", up)), t_fin_last], sem=None)
                        for ci in range(PC):
                            mm = r.op("tensor", lambda e, ci=ci, cc=cc, coh=coh, qrel=qrel, up=up: e.matmul(
                                out=ps_s[up][:, 0:CCW],
                                lhsT=qbuf[:, ci * HALF + qrel * 128: ci * HALF + qrel * 128 + 128],
                                rhs=wo[:, ci * CH2 + cc * CCW: ci * CH2 + cc * CCW + CCW],
                                start=False, stop=(ci == PC - 1)),
                                sem="T" if ci == PC - 1 else None)
                        t_ev = r.op("vector", lambda e, qt=qt, coh=coh, cc=cc, xp=xp, up=up: e.scalar_tensor_tensor(
                            out=ybuf(qt)[:, coh * CH2 + cc * CCW: coh * CH2 + cc * CCW + CCW],
                            in0=xslot(xp)[:, coh * CH2 + cc * CCW: coh * CH2 + cc * CCW + CCW],
                            scalar=1.0, in1=ps_s[up][:, 0:CCW], op0=ALU.mult, op1=ALU.add),
                            reads=[mm, x_used[(qt, coh)], y_tok.get(("yfree", qt))], sem="V")
                        y_tok[("pofree", up)] = t_ev
                        xfree[xp] = t_ev
                        unit += 1
                    y_tok[(qt, coh)] = t_ev
                y_tok["wofree"] = mm
            t_l2last = None
            for qt in qts:
                qrel = qt % QTH
                xp = qt % 2
                t_norm = ln_chain(ybuf(qt), xp, lnbf(xp),
                                  [y_tok[(qt, 0)], y_tok[(qt, 1)], ln2_toks.get(("lnbf_free", xp))])
                ev = None
                for p in range(PC):
                    pp = p % 2
                    t_tr = r.op("tensor", lambda e, p=p, pp=pp, xp=xp: e.transpose(
                        out=ptp[pp][:, 0:128], in_=lnbf(xp)[:, p * 128:(p + 1) * 128],
                        identity=ident),
                        reads=[t_norm, ln2_toks.get(("ev", pp))], sem="T")
                    ev = r.op("scalar", lambda e, p=p, pp=pp, qrel=qrel: e.copy(
                        out=ln2f[:, p * HALF + qrel * 128: p * HALF + qrel * 128 + 128],
                        in_=ptp[pp][:, 0:128]),
                        reads=[t_tr], sem="S")
                    ln2_toks[("ev", pp)] = ev
                ln2_toks[("lnbf_free", xp)] = ev
                ln2_toks[qt] = ev
                t_l2last = ev
            if dbg:
                for qt in qts:
                    r.dma("sync", dbg_y[qt * 128:(qt + 1) * 128, :], ybuf(qt),
                          reads=[y_tok[(qt, 0)], y_tok[(qt, 1)]])
                r.dma_batch("sync", [
                    (dbg_l2[half, p * 128:(p + 1) * 128, :], ln2f[:, p * HALF:(p + 1) * HALF])
                    for p in range(PC)], reads=[t_l2last])
            r.phase_end()
            psD.close()

            if lim < 6 + 3 * half:
                break
            psF = ExitStack()
            pgF = psF.enter_context(nc.psum_tensor(f"pgF{half}", [128, HALF], F32))
            puF = psF.enter_context(nc.psum_tensor(f"puF{half}", [128, HALF], F32))
            pyF = [psF.enter_context(nc.psum_tensor(f"pyF{half}{i}", [128, YW], F32)) for i in range(2)]

            rt_ev = {}
            for qt in qts:
                qrel = qt % QTH
                up = qt % 2
                mm = None
                for ci in range(PC):
                    mm = r.op("tensor", lambda e, ci=ci, qrel=qrel: e.matmul(
                        out=pgF[:, 0:8],
                        lhsT=ln2f[:, ci * HALF + qrel * 128: ci * HALF + qrel * 128 + 128],
                        rhs=router_sb[:, ci * 8:(ci + 1) * 8],
                        start=(ci == 0), stop=(ci == PC - 1)),
                        reads=([t_rt, t_l2last, rt_ev.get("prfree")] if ci == 0 else []),
                        sem="T" if ci == PC - 1 else None)
                lg = rlog(up)
                t_lg = r.op("vector", lambda e, lg=lg: e.tensor_copy(out=lg, in_=pgF[:, 0:8]),
                            reads=[mm, rt_ev.get(("lgfree", up))], sem="V")
                rt_ev["prfree"] = t_lg
                t_nmx = r.op("vector", lambda e, lg=lg: e.reduce_max(
                    out=rst(0), in_=lg, axis=AX.X, negate=True), reads=[t_lg, rt_ev.get("st")], sem="V")
                t_d1 = r.op("vector", lambda e, lg=lg: e.tensor_scalar_add(
                    out=lg, in0=lg, scalar1=rst(0)), reads=[t_nmx], sem="V")
                t_eq = r.op("vector", lambda e, lg=lg: e.tensor_single_scalar(
                    out=rtmp, in_=lg, scalar=0.0, op=ALU.is_ge), reads=[t_d1], sem="V")
                t_cnt = r.op("vector", lambda e: e.reduce_sum(
                    out=rst(1), in_=rtmp, axis=AX.X), reads=[t_eq], sem="V")
                t_msk = r.op("vector", lambda e, lg=lg: e.scalar_tensor_tensor(
                    out=rtmp, in0=rtmp, scalar=NEG, in1=lg, op0=ALU.mult, op1=ALU.add),
                    reads=[t_eq, t_cnt], sem="V")
                t_n2 = r.op("vector", lambda e: e.reduce_max(
                    out=rst(2), in_=rtmp, axis=AX.X, negate=True), reads=[t_msk], sem="V")
                t_tie = r.op("vector", lambda e: e.tensor_single_scalar(
                    out=rst(3), in_=rst(1), scalar=2.0, op=ALU.is_ge), reads=[t_cnt], sem="V")
                t_de = r.op("vector", lambda e: e.tensor_scalar(
                    out=rst(4), in0=rst(3), scalar1=-1.0, scalar2=1.0,
                    op0=ALU.mult, op1=ALU.add), reads=[t_tie], sem="V")
                t_de2 = r.op("vector", lambda e: e.tensor_mul(
                    out=rst(5), in0=rst(2), in1=rst(4)), reads=[t_n2, t_de], sem="V")
                t_u2 = r.op("scalar", lambda e: e.activation(
                    out=rst(6), in_=rst(5), func=AF.Exp, scale=-1.0), reads=[t_de2], sem="S")
                t_ss = r.op("vector", lambda e: e.tensor_scalar_add(
                    out=rst(7), in0=rst(6), scalar1=1.0), reads=[t_u2], sem="V")
                t_rc = r.op("vector", lambda e: e.reciprocal(
                    out=rst(8), in_=rst(7)), reads=[t_ss], sem="V")
                t_c = r.op("vector", lambda e, qt=qt: e.tensor_mul(
                    out=ccol(qt), in0=rst(7), in1=rst(8)), reads=[t_rc], sem="V")
                rt_ev[qt] = t_c
                rt_ev[("lgfree", up)] = t_d1
                rt_ev["st"] = t_c

            ff = {}
            ff["pgfree"] = rt_ev["prfree"]
            slabs = [("r", js) for js in range(cfg.RINTER // SLAB)] + \
                    [("s", js) for js in range(cfg.INTER // SLAB)]
            for sn, (stype, js) in enumerate(slabs):
                gsrc, usrc, dsrc = (rgT, ruT, rdT) if stype == "r" else (gT, uT, dT)
                sp = sn % 2
                tg = r.dma_batch("gpsimd", [
                    (gsb(sp)[:, p * SLAB:(p + 1) * SLAB],
                     gsrc[p * 128:(p + 1) * 128, js * SLAB:(js + 1) * SLAB])
                    for p in range(PC)], reads=[ff.get(("gfree", sp))])
                tu = r.dma_batch("gpsimd", [
                    (usb(sp)[:, p * SLAB:(p + 1) * SLAB],
                     usrc[p * 128:(p + 1) * 128, js * SLAB:(js + 1) * SLAB])
                    for p in range(PC)], reads=[ff.get(("ufree", sp))])
                td = r.dma_batch("gpsimd", [
                    (dsb[:, j * C:(j + 1) * C],
                     dsrc[js * SLAB + j * 128: js * SLAB + (j + 1) * 128, :])
                    for j in range(SLAB // 128)], reads=[ff.get("dfree")])
                dn_last = None
                for j in range(SLAB // 128):
                    mmg = None
                    for tk in range(NTK):
                        for ci in range(PC):
                            mmg = r.op("tensor", lambda e, ci=ci, j=j, tk=tk, sp=sp: e.matmul(
                                out=pgF[:, tk * TW:(tk + 1) * TW],
                                lhsT=gsb(sp)[:, ci * SLAB + j * 128: ci * SLAB + j * 128 + 128],
                                rhs=ln2f[:, ci * HALF + tk * TW: ci * HALF + tk * TW + TW],
                                start=(ci == 0), stop=(ci == PC - 1)),
                                reads=([tg, ff.get("pgfree")] if ci == 0 and tk == 0 else []),
                                sem="T" if (ci == PC - 1 and tk == NTK - 1) else None)
                    mmu = None
                    for tk in range(NTK):
                        for ci in range(PC):
                            mmu = r.op("tensor", lambda e, ci=ci, j=j, tk=tk, sp=sp: e.matmul(
                                out=puF[:, tk * TW:(tk + 1) * TW],
                                lhsT=usb(sp)[:, ci * SLAB + j * 128: ci * SLAB + j * 128 + 128],
                                rhs=ln2f[:, ci * HALF + tk * TW: ci * HALF + tk * TW + TW],
                                start=(ci == 0), stop=(ci == PC - 1)),
                                reads=([tu, ff.get("pufree")] if ci == 0 and tk == 0 else []),
                                sem="T" if (ci == PC - 1 and tk == NTK - 1) else None)
                    t_si = r.op("scalar", lambda e: e.activation(
                        out=silu_t, in_=pgF[:, :], func=AF.Silu),
                        reads=[mmg, ff.get("silufree")], sem="S")
                    ff["pgfree"] = t_si
                    t_h = r.op("vector", lambda e, j=j: e.tensor_tensor(
                        out=hsb[:, j * HALF:(j + 1) * HALF], in0=silu_t, in1=puF[:, :],
                        op=ALU.mult),
                        reads=[t_si, mmu, ff.get(("hfree", j))], sem="V")
                    ff["pufree"] = t_h
                    ff["silufree"] = t_h
                    ff[("h", j)] = t_h
                unit = 0
                for qt in qts:
                    qrel = qt % QTH
                    for yc in range(NYC):
                        up = unit % 2
                        mm = None
                        ncc2 = max(YW // 512, 1)
                        ccw2 = min(512, YW)
                        for j in range(SLAB // 128):
                            for cc in range(ncc2):
                                mm = r.op("tensor", lambda e, j=j, cc=cc, yc=yc, qrel=qrel, up=up, ccw2=ccw2: e.matmul(
                                    out=pyF[up][:, cc * ccw2:(cc + 1) * ccw2],
                                    lhsT=hsb[:, j * HALF + qrel * 128: j * HALF + qrel * 128 + 128],
                                    rhs=dsb[:, j * C + yc * YW + cc * ccw2: j * C + yc * YW + cc * ccw2 + ccw2],
                                    start=(j == 0), stop=(j == SLAB // 128 - 1)),
                                    reads=([td, ff.get(("pyfree", up)), ff[("h", j)]]
                                           if cc == 0 else [ff[("h", j)]]),
                                    sem="T" if (j == SLAB // 128 - 1 and cc == ncc2 - 1) else None)
                        dn_last = mm
                        if stype == "r":
                            t_ev = r.op("vector", lambda e, qt=qt, yc=yc, up=up: e.scalar_tensor_tensor(
                                out=ybuf(qt)[:, yc * YW:(yc + 1) * YW],
                                in0=pyF[up][:, :], scalar=ccol(qt),
                                in1=ybuf(qt)[:, yc * YW:(yc + 1) * YW],
                                op0=ALU.mult, op1=ALU.add),
                                reads=[mm, rt_ev[qt], ff.get(("ylast", qt, yc)),
                                       ln2_toks[qt]], sem="V")
                        else:
                            t_ev = r.op("vector", lambda e, qt=qt, yc=yc, up=up: e.tensor_add(
                                out=ybuf(qt)[:, yc * YW:(yc + 1) * YW],
                                in0=ybuf(qt)[:, yc * YW:(yc + 1) * YW],
                                in1=pyF[up][:, :]),
                                reads=[mm, ff.get(("ylast", qt, yc)), ln2_toks[qt]], sem="V")
                        ff[("pyfree", up)] = t_ev
                        ff[("ylast", qt, yc)] = t_ev
                        unit += 1
                for j in range(SLAB // 128):
                    ff[("hfree", j)] = dn_last
                ff[("gfree", sp)] = dn_last
                ff[("ufree", sp)] = dn_last
                ff["dfree"] = dn_last
            for qt in qts:
                reads = [ff[("ylast", qt, yc)] for yc in range(NYC)]
                tko = r.dma("sync", out_own[qt * 128:(qt + 1) * 128, :], ybuf(qt),
                            reads=reads)
                out_toks.append(tko)
            r.phase_end()
            psF.close()

        if out_toks:
            r.op("sync", lambda e: e.nop(), reads=out_toks)
        else:
            r.op("sync", lambda e: e.nop(), reads=r.phase_end())

        blk = es.enter_context(nc.Block())
        r.emit(blk, sems)

    return nc



def _host_inputs(cfg, x, qkv_w, qkv_b, out_w, out_b, router_w,
                 sh_gate, sh_up, sh_down, r_gate, r_up, r_down):
    C, T, OWN, A = cfg.C, cfg.T, cfg.OWN, cfg.A
    import ml_dtypes
    shared = {
        "qkvT": np.ascontiguousarray(np.asarray(qkv_w, np.float32).T),
        "qkvb_cols": np.ascontiguousarray(np.asarray(qkv_b, np.float32).reshape(3 * C // 128, 128).T),
        "qkvb_flat": np.ascontiguousarray(np.asarray(qkv_b, np.float32)),
        "outwT": np.ascontiguousarray(np.asarray(out_w, np.float32).T),
        "outb": np.ascontiguousarray(np.asarray(out_b, np.float32)),
        "gT": np.ascontiguousarray(np.asarray(sh_gate, np.float32).T),
        "uT": np.ascontiguousarray(np.asarray(sh_up, np.float32).T),
        "dT": np.ascontiguousarray(np.asarray(sh_down, np.float32).T),
        "rgT": np.ascontiguousarray(np.asarray(r_gate, np.float32).T),
        "ruT": np.ascontiguousarray(np.asarray(r_up, np.float32).T),
        "rdT": np.ascontiguousarray(np.asarray(r_down, np.float32).T),
        "routerT": np.ascontiguousarray(np.asarray(router_w, np.float32).T),
        "identin": np.eye(128, dtype=ml_dtypes.bfloat16),
    }
    masks_by_cls = [build_masks(cfg, 0), build_masks(cfg, 1)]
    B = x.shape[0]
    in_maps = []
    for c in range(2 * B):
        b, cls = c // 2, c % 2
        xs = np.ascontiguousarray(x[b])
        xo = np.ascontiguousarray(x[b].reshape(T // 128, 128, C)[cls::2].reshape(OWN, C))
        in_maps.append({"x_seq": xs, "x_own": xo, "masks": masks_by_cls[cls], **shared})
    return in_maps


def _scatter(cfg, outs, B):
    T, OWN, C = cfg.T, cfg.OWN, cfg.C
    full = np.empty((B, T, C), np.float32)
    for c, o in enumerate(outs):
        b, cls = c // 2, c % 2
        full[b].reshape(T // 128, 128, C)[cls::2] = o.reshape(OWN // 128, 128, C)
    return full


_NC_CACHE = {}


def kernel(L0_attn_x, L0_attn_qkv_w, L0_attn_qkv_b, L0_attn_out_w, L0_attn_out_b,
           L0_ffn_router_weight, L0_ffn_sh_gate, L0_ffn_sh_up, L0_ffn_sh_down,
           L0_ffn_r_gate, L0_ffn_r_up, L0_ffn_r_down):
    cfg = FULL
    x = np.asarray(L0_attn_x, np.float32)
    in_maps = _host_inputs(cfg, x, L0_attn_qkv_w, L0_attn_qkv_b, L0_attn_out_w,
                           L0_attn_out_b, L0_ffn_router_weight, L0_ffn_sh_gate,
                           L0_ffn_sh_up, L0_ffn_sh_down, L0_ffn_r_gate,
                           L0_ffn_r_up, L0_ffn_r_down)
    if "nc" not in _NC_CACHE:
        _NC_CACHE["nc"] = build(cfg)
    nc = _NC_CACHE["nc"]
    res = run_bass_kernel_spmd(nc, in_maps, core_ids=list(range(8)))
    outs = [r["out_own"] for r in res.results]
    return _scatter(cfg, outs, x.shape[0])


# version 22
# speedup vs baseline: 1.1080x; 1.1080x over previous
import os
import numpy as np
from contextlib import ExitStack

import concourse.bass as bass
import concourse.mybir as mybir
from concourse.bass_utils import run_bass_kernel_spmd

F32 = mybir.dt.float32
BF16 = mybir.dt.bfloat16
AF = mybir.ActivationFunctionType
ALU = mybir.AluOpType
AX = mybir.AxisListType

NEG = -1.0e30


class Cfg:
    def __init__(self, C, T, INTER, RINTER):
        self.C, self.T, self.INTER, self.RINTER = C, T, INTER, RINTER
        self.PC = C // 128
        self.H = self.PC
        self.HAUG = self.H * 129
        self.OWN = T // 2
        self.A = self.OWN // 2
        self.HALF = self.OWN // 2
        self.NQT = self.OWN // 128
        self.QTH = self.NQT // 2
        self.KB = 512
        self.NKB = T // self.KB
        self.CHK = 512
        self.SLAB = 512
        self.CH2 = C // 2
        self.eps = 1e-5
        self.qscale = 1.0 / np.sqrt(128.0)


FULL = Cfg(C=2048, T=4096, INTER=8192, RINTER=1024)
MINI = Cfg(C=256, T=1024, INTER=1024, RINTER=512)


def qpos_of(cfg, cls, i):
    return 256 * i + 128 * cls


def cov_of(cfg, i):
    mx = max(qpos_of(cfg, 0, i), qpos_of(cfg, 1, i))
    return (mx + 128 + cfg.KB - 1) // cfg.KB


def open_of(cfg, i):
    mn = min(qpos_of(cfg, 0, i), qpos_of(cfg, 1, i))
    return mn // cfg.KB


def mask_pairs(cfg):
    out = []
    for half in range(2):
        qts = range(half * cfg.QTH, (half + 1) * cfg.QTH)
        maxcov = max(cov_of(cfg, i) for i in qts)
        for kb in range(maxcov):
            for i in qts:
                if open_of(cfg, i) <= kb < cov_of(cfg, i):
                    out.append((half, kb, i))
    return out


def build_masks(cfg, cls):
    pairs = mask_pairs(cfg)
    m = np.zeros((len(pairs), 128, cfg.KB), np.float32)
    kr = np.arange(cfg.KB)
    qr = np.arange(128)
    for n, (_h, kb, i) in enumerate(pairs):
        qg = qpos_of(cfg, cls, i) + qr[:, None]
        kg = kb * cfg.KB + kr[None, :]
        m[n] = np.where(kg <= qg, 0.0, NEG).astype(np.float32)
    return m



ENGS = ("sync", "tensor", "vector", "scalar", "gpsimd")
_ESEM = {"sync": "SY", "gpsimd": "G", "scalar": "S", "vector": "V", "tensor": "T"}


def _flat(reads):
    out = []
    for x in reads:
        if x is None:
            continue
        if isinstance(x, (list, tuple)) and x and isinstance(x[0], (list, tuple)):
            out.extend(_flat(x))
        elif isinstance(x, (list, tuple)) and len(x) == 2 and isinstance(x[0], str):
            out.append(x)
        elif isinstance(x, (list, tuple)):
            out.extend(_flat(x))
    return out


NSYNC_LANES = 6
NGP_LANES = 4
SYNC_LANES = [f"DS{i}" for i in range(NSYNC_LANES)]
GP_LANES = [f"DG{i}" for i in range(NGP_LANES)]


class Rec:

    def __init__(self):
        self.items = {e: [] for e in ENGS}
        self.cnt = {}
        self.pending = {e: [] for e in ENGS}
        self.lane_rr = {"sync": 0, "gpsimd": 0}

    def op(self, eng, fn, reads=(), sem=None, inc=1):
        reads = _flat(list(reads))
        if self.pending[eng]:
            reads = self.pending[eng] + reads
            self.pending[eng] = []
        tok = None
        if sem is not None:
            self.cnt[sem] = self.cnt.get(sem, 0) + inc
            tok = (sem, self.cnt[sem])
        self.items[eng].append({"reads": reads, "fn": fn, "sem": sem,
                                "inc": inc if sem else 0})
        return tok

    def dma_batch(self, eng, pairs, reads=(), sem=None):
        if sem is None:
            lanes = SYNC_LANES if eng == "sync" else GP_LANES
            i = self.lane_rr[eng]
            sem = lanes[i % len(lanes)]
            self.lane_rr[eng] = i + 1
        gate = (sem, self.cnt.get(sem, 0))
        tok = None
        for n, (out, in_) in enumerate(pairs):
            tok = self.op(eng, (lambda o, i_: lambda e: e.dma_start(out=o, in_=i_))(out, in_),
                          reads=(list(reads) + [gate]) if n == 0 else [],
                          sem=sem, inc=16)
        return tok

    def dma(self, eng, out, in_, reads=(), sem=None):
        return self.dma_batch(eng, [(out, in_)], reads=reads, sem=sem)

    def phase_end(self):
        toks = []
        for e in ENGS:
            if not self.items[e]:
                continue
            it = self.items[e][-1]
            if it["sem"] is None:
                it["sem"] = _ESEM[e]
                it["inc"] = 1
                self.cnt[it["sem"]] = self.cnt.get(it["sem"], 0) + it["inc"]
            toks.append((it["sem"], self.cnt[it["sem"]]))
        for lane in SYNC_LANES + GP_LANES:
            if self.cnt.get(lane, 0):
                toks.append((lane, self.cnt[lane]))
        toks = list({s: (s, v) for s, v in toks}.values())
        for e in ENGS:
            self.pending[e] = list(toks)
        return toks

    def emit(self, blk, sems):
        for ename in ENGS:
            stream = self.items[ename]
            if not stream:
                continue

            def runner(e, stream=stream):
                waited = {}
                for it in stream:
                    for s, v in it["reads"]:
                        if waited.get(s, 0) < v:
                            e.wait_ge(sems[s], v)
                            waited[s] = v
                    ins = it["fn"](e)
                    if it["sem"] is not None:
                        ins.then_inc(sems[it["sem"]], it["inc"])
            getattr(blk, ename)(runner)



def build(cfg, dbg=False, lim=99):
    PH_LVL = int(os.environ.get("PH_LVL", "99"))
    nc = bass.Bass("TRN2", target_bir_lowering=False)
    C, T, PC, H = cfg.C, cfg.T, cfg.PC, cfg.H
    OWN, HALF, KB, CHK, SLAB = cfg.OWN, cfg.HALF, cfg.KB, cfg.CHK, cfg.SLAB
    HAUG, CH2 = cfg.HAUG, cfg.CH2
    QTH = cfg.QTH
    assert CHK == KB
    assert cfg.INTER % SLAB == 0 and cfg.RINTER % SLAB == 0
    TW = min(512, HALF)
    NTK = HALF // TW
    YW = min(1024, C)
    NYC = C // YW
    QD = min(4, PC)
    NQD = PC // QD
    VDW = min(512, C)
    NVD = C // VDW
    CCW = min(512, CH2)
    NCC = CH2 // CCW
    NST = max(C // 512, 1)
    STW = min(512, C)

    pairs = mask_pairs(cfg)
    NMASK = len(pairs)

    class _Lazy:
        def __init__(self, name, shape, dtype):
            self.args = (name, shape, dtype)
            self._ap = None

        def __getitem__(self, key):
            if self._ap is None:
                self._ap = nc.dram_tensor(self.args[0], self.args[1], self.args[2],
                                          kind="ExternalInput").ap()
            return self._ap[key]

    x_seq = _Lazy("x_seq", [T, C], F32)
    x_own = _Lazy("x_own", [OWN, C], F32)
    qkvT = _Lazy("qkvT", [C, 3 * C], F32)
    qkvb_cols = _Lazy("qkvb_cols", [128, 3 * PC], F32)
    qkvb_flat = _Lazy("qkvb_flat", [3 * C], F32)
    outwT = _Lazy("outwT", [C, C], F32)
    outb = _Lazy("outb", [C], F32)
    gT = _Lazy("gT", [C, cfg.INTER], F32)
    uT = _Lazy("uT", [C, cfg.INTER], F32)
    dT = _Lazy("dT", [cfg.INTER, C], F32)
    rgT = _Lazy("rgT", [C, cfg.RINTER], F32)
    ruT = _Lazy("ruT", [C, cfg.RINTER], F32)
    rdT = _Lazy("rdT", [cfg.RINTER, C], F32)
    routerT = _Lazy("routerT", [C, 8], F32)
    masks = _Lazy("masks", [max(NMASK, 1), 128, KB], F32)
    identin = _Lazy("identin", [128, 128], BF16)
    out_own = nc.dram_tensor("out_own", [OWN, C], F32, kind="ExternalOutput").ap()

    skind = "ExternalOutput" if dbg else "Internal"
    ln1s = nc.dram_tensor("ln1s", [C, T], BF16, kind=skind).ap()
    Ks = nc.dram_tensor("Ks", [C, T], BF16, kind=skind).ap()
    Vs = nc.dram_tensor("Vs", [T, HAUG], BF16, kind=skind).ap()
    Qs = nc.dram_tensor("Qs", [C, OWN], BF16, kind=skind).ap()
    dbg_y = nc.dram_tensor("dbg_y", [OWN, C], F32, kind="ExternalOutput").ap() if dbg else None
    dbg_ot = nc.dram_tensor("dbg_ot", [2, C, HALF], BF16, kind="ExternalOutput").ap() if dbg else None
    dbg_l2 = nc.dram_tensor("dbg_l2", [2, C, HALF], BF16, kind="ExternalOutput").ap() if dbg else None

    es = ExitStack()
    with es:
        LN2F0 = max(2 * PC * HALF, PC * HALF + PC * CH2, 4 * PC * SLAB)
        ARB_W = max(PC * C + 3 * PC * CHK,
                    PC * C + 2 * PC * CHK + 2 * HAUG + C,
                    PC * HALF + 2 * PC * KB + 2 * (KB // 128) * HAUG + 8 * KB + 512,
                    LN2F0 + PC * HALF + C,
                    LN2F0 + PC * HALF + (SLAB // 128) * C)
        ARF_W = max(QTH * HAUG + 2 * KB, QTH * C + 2 * C, QTH * C + HALF)
        ARB = es.enter_context(nc.sbuf_tensor("ARB", [128, ARB_W], BF16))
        ARF = es.enter_context(nc.sbuf_tensor("ARF", [128, ARF_W], F32))
        TMPBF = es.enter_context(nc.sbuf_tensor("TMPBF", [128, 2 * C + CHK], BF16))
        SMB = es.enter_context(nc.sbuf_tensor("SMB", [128, 256 + PC * 8 + H + CHK], BF16))
        SMF = es.enter_context(nc.sbuf_tensor("SMF", [128, 256 + QTH * H * 2 + 16 + QTH + cfg.NQT * H], F32))

        ar, af, tb, smb, smf = ARB.ap(), ARF.ap(), TMPBF.ap(), SMB.ap(), SMF.ap()

        ident = smb[:, 0:128]
        ones_bf = smb[:, 128:256]
        router_sb = smb[:, 256:256 + PC * 8]
        kmx2row = smb[0:1, 256 + PC * 8:256 + PC * 8 + H]
        rowtmp = smb[0:1, 256 + PC * 8 + H:256 + PC * 8 + H + CHK]

        qkvb_sb = smf[:, 0:3 * PC]
        qb_scaled = smf[:, 3 * PC:4 * PC]
        bnst = lambda p: smf[:, 64 + (p % 2) * 24: 64 + (p % 2) * 24 + 24]
        aggr = lambda p: smf[:, 112 + (p % 2) * 2: 112 + (p % 2) * 2 + 2]
        sdv = lambda p: smf[:, 116 + (p % 2): 117 + (p % 2)]
        rstd = lambda p: smf[:, 118 + (p % 2): 119 + (p % 2)]
        nmr = lambda p: smf[:, 120 + (p % 2): 121 + (p % 2)]
        nmcol = lambda p: smf[:, 128 + (p % 4): 129 + (p % 4)]
        dcol = lambda p: smf[:, 132 + (p % 4): 133 + (p % 4)]
        acol = lambda p: smf[:, 136 + (p % 4): 137 + (p % 4)]
        lrec = lambda p: smf[:, 140 + (p % 4):141 + (p % 4)]
        rlog = lambda p: smf[:, 144 + (p % 2) * 8: 144 + (p % 2) * 8 + 8]
        rtmp = smf[:, 160:168]
        rst = lambda j: smf[:, 168 + j:169 + j]
        MB0 = 192
        mcol = lambda qt, hd, par: smf[:, MB0 + ((qt % QTH) * H + hd) * 2 + (par % 2):
                                       MB0 + ((qt % QTH) * H + hd) * 2 + (par % 2) + 1]
        CC0 = MB0 + QTH * H * 2
        ccol = lambda qt: smf[:, CC0 + (qt % QTH): CC0 + (qt % QTH) + 1]
        NQ0 = CC0 + QTH
        nq2col = lambda qt, hd: smf[:, NQ0 + qt * H + hd: NQ0 + qt * H + hd + 1]

        XB0 = QTH * C
        xslot = lambda p: af[:, XB0 + (p % 2) * C: XB0 + (p % 2) * C + C]
        lnbf = lambda p: tb[:, (p % 2) * C:(p % 2) * C + C]
        sqA = tb[:, 2 * C:2 * C + CHK]

        semnames = ["T", "V", "S", "G", "SY"] + SYNC_LANES + GP_LANES
        sems = {n: es.enter_context(nc.semaphore(n)) for n in semnames}
        r = Rec()

        def ln_chain(xtile, par, outbf, reads0):
            t = None
            for j in range(NST):
                t = r.op("vector", lambda e, j=j: e.bn_stats(
                    out=bnst(par)[:, 6 * j:6 * j + 6], in_=xtile[:, j * STW:(j + 1) * STW]),
                    reads=(reads0 if j == 0 else []), sem="V" if j == NST - 1 else None)
            t_ag = r.op("vector", lambda e: e.bn_aggr(
                out=aggr(par), in_=bnst(par)[:, 0:6 * NST]), reads=[t], sem="V")
            t_vp = r.op("vector", lambda e: e.tensor_scalar_add(
                out=sdv(par), in0=aggr(par)[:, 1:2], scalar1=float(cfg.eps)),
                reads=[t_ag], sem="V")
            t_sd = r.op("scalar", lambda e: e.activation(
                out=sdv(par), in_=sdv(par), func=AF.Sqrt),
                reads=[t_vp], sem="S")
            t_rs = r.op("vector", lambda e: e.reciprocal(
                out=rstd(par), in_=sdv(par)), reads=[t_sd], sem="V")
            t_nm = r.op("vector", lambda e: e.tensor_scalar(
                out=nmr(par), in0=aggr(par)[:, 0:1], scalar1=rstd(par), scalar2=-1.0,
                op0=ALU.mult, op1=ALU.mult), reads=[t_rs], sem="V")
            t_norm = r.op("scalar", lambda e: e.activation(
                out=outbf, in_=xtile, func=AF.Identity, bias=nmr(par), scale=rstd(par)),
                reads=[t_nm], sem="S")
            return t_norm

        psA = ExitStack()
        pk = [psA.enter_context(nc.psum_tensor(f"pkA{i}", [128, CHK], F32)) for i in range(QD)]
        ptr = [psA.enter_context(nc.psum_tensor(f"ptA{i}", [128, 128], BF16)) for i in range(3)]
        pnA = psA.enter_context(nc.psum_tensor("pnA", [1, CHK], F32)).ap()
        pcol = [ptr[2].ap()[:, 0:1]]

        wk = ar[:, 0:PC * C]
        ln1f = lambda p: ar[:, PC * C + (p % 2) * PC * CHK: PC * C + (p % 2 + 1) * PC * CHK]
        kch = ar[:, PC * C + 2 * PC * CHK: PC * C + 3 * PC * CHK]

        t_ident = t_qb = r.dma_batch("sync", [(ident, identin[:, :]),
                                              (qkvb_sb, qkvb_cols[:, :])])
        t_ones = r.op("vector", lambda e: e.memset(ones_bf, 1.0), sem="V")
        t_qbs = r.op("vector", lambda e: e.tensor_scalar_mul(
            out=qb_scaled, in0=qkvb_sb[:, 0:PC], scalar1=float(cfg.qscale)),
            reads=[t_qb], sem="V")

        def phase_AC(src_x, nch, wsec, bias_cols, scale, dst, lnspill, knorm=False,
                     qnorm=False):
            hp = max(PC // 2, 1)
            t_w0 = r.dma_batch("gpsimd", [
                (wk[:, p * C:(p + 1) * C],
                 qkvT[p * 128:(p + 1) * 128, wsec * C:(wsec + 1) * C])
                for p in range(hp)])
            t_w1 = r.dma_batch("gpsimd", [
                (wk[:, p * C:(p + 1) * C],
                 qkvT[p * 128:(p + 1) * 128, wsec * C:(wsec + 1) * C])
                for p in range(hp, PC)]) if PC > 1 else t_w0
            guard = {}
            for ch in range(nch):
                lf = ln1f(ch)
                ev = None
                for tl in range(CHK // 128):
                    t0 = ch * CHK + tl * 128
                    xp = (ch * (CHK // 128) + tl) % 2
                    xtok = r.dma("sync", xslot(xp), src_x[t0:t0 + 128, :],
                                 reads=[guard.get(("xfree", xp))])
                    t_norm = ln_chain(xslot(xp), xp, lnbf(xp),
                                      [xtok, guard.get(("lnbf_free", xp))])
                    guard[("xfree", xp)] = t_norm
                    if PH_LVL < 2:
                        guard[("lnbf_free", xp)] = t_norm
                        continue
                    for p in range(PC):
                        pp = p % 3
                        t_tr = r.op("tensor", lambda e, p=p, pp=pp, xp=xp: e.transpose(
                            out=ptr[pp].ap()[:, :], in_=lnbf(xp)[:, p * 128:(p + 1) * 128],
                            identity=ident),
                            reads=[t_norm, t_ident, guard.get(("trev", pp)),
                                   (guard.get("pcolfree") if pp == 2 else None)], sem="T")
                        ev = r.op("vector", lambda e, p=p, pp=pp, tl=tl, lf=lf: e.tensor_copy(
                            out=lf[:, p * CHK + tl * 128: p * CHK + tl * 128 + 128],
                            in_=ptr[pp].ap()[:, :]),
                            reads=[t_tr, guard.get(("lfw", ch % 2))], sem="V")
                        guard[("trev", pp)] = ev
                    guard[("lnbf_free", xp)] = ev
                lf_ready = ev
                mm_last = None
                kevac = None
                if PH_LVL < 3:
                    continue
                for quad in range(NQD):
                    for ci in range(PC):
                        for j in range(QD):
                            od = quad * QD + j
                            mm_last = r.op("tensor", lambda e, ci=ci, od=od, j=j, lf=lf: e.matmul(
                                out=pk[j][:, :],
                                lhsT=wk[:, ci * C + od * 128: ci * C + od * 128 + 128],
                                rhs=lf[:, ci * CHK: ci * CHK + CHK],
                                start=(ci == 0), stop=(ci == PC - 1)),
                                reads=([lf_ready, t_w0, guard.get("kevac_all"), t_qbs]
                                       if (ci == 0 and j == 0) else
                                       ([t_w1] if (ci == hp and j == 0) else [])),
                                sem="T" if (ci == PC - 1 and j == QD - 1) else None)
                    for j in range(QD):
                        od = quad * QD + j
                        kevac = r.op("scalar", lambda e, od=od, j=j: e.activation(
                            out=kch[:, od * CHK:(od + 1) * CHK], in_=pk[j][:, :],
                            func=AF.Identity, bias=bias_cols[:, od:od + 1],
                            scale=float(scale)),
                            reads=[mm_last, guard.get("spill")], sem="S")
                        if knorm:
                            t_sq = r.op("gpsimd", lambda e, od=od: e.tensor_mul(
                                out=sqA, in0=kch[:, od * CHK:(od + 1) * CHK],
                                in1=kch[:, od * CHK:(od + 1) * CHK]),
                                reads=[kevac, guard.get("sqfree")], sem="G")
                            t_pn = r.op("tensor", lambda e: e.matmul(
                                out=pnA[:, :], lhsT=ones_bf[:, 0:1], rhs=sqA,
                                start=True, stop=True),
                                reads=[t_sq, t_ones, guard.get("pnfree")], sem="T")
                            guard["sqfree"] = t_pn
                            t_mx = r.op("vector", lambda e: e.reduce_max(
                                out=rowtmp[0:1, 0:1], in_=pnA[0:1, :], axis=AX.X),
                                reads=[t_pn, guard.get("rowfree")], sem="V")
                            guard["pnfree"] = t_mx
                            if guard.get(("km", od)) is None:
                                t_km = r.op("vector", lambda e, od=od: e.tensor_copy(
                                    out=kmx2row[0:1, od:od + 1], in_=rowtmp[0:1, 0:1]),
                                    reads=[t_mx], sem="V")
                            else:
                                t_km = r.op("vector", lambda e, od=od: e.tensor_tensor(
                                    out=kmx2row[0:1, od:od + 1], in0=kmx2row[0:1, od:od + 1],
                                    in1=rowtmp[0:1, 0:1], op=ALU.max),
                                    reads=[t_mx], sem="V")
                            guard[("km", od)] = t_km
                            guard["rowfree"] = t_km
                            r.kmtok = t_km
                    guard["kevac_all"] = kevac
                if PH_LVL >= 4:
                    sp = r.dma_batch("sync", [
                        (dst[od * 128:(od + 1) * 128, ch * CHK:(ch + 1) * CHK],
                         kch[:, od * CHK:(od + 1) * CHK]) for od in range(PC)],
                        reads=[kevac])
                    guard["spill"] = sp
                if qnorm:
                    for od in range(PC):
                        t_sq = r.op("gpsimd", lambda e, od=od: e.tensor_mul(
                            out=sqA, in0=kch[:, od * CHK:(od + 1) * CHK],
                            in1=kch[:, od * CHK:(od + 1) * CHK]),
                            reads=[kevac, guard.get("sqfree")], sem="G")
                        t_pn = r.op("tensor", lambda e: e.matmul(
                            out=pnA[:, :], lhsT=ones_bf[:, 0:1], rhs=sqA,
                            start=True, stop=True),
                            reads=[t_sq, t_ones, guard.get("pnfree")], sem="T")
                        guard["sqfree"] = t_pn
                        t_rw = r.op("vector", lambda e: e.tensor_copy(
                            out=rowtmp[0:1, 0:CHK], in_=pnA[0:1, :]),
                            reads=[t_pn, guard.get("rowfree")], sem="V")
                        guard["pnfree"] = t_rw
                        t_tc = None
                        for tl in range(CHK // 128):
                            qt_g = (ch * CHK) // 128 + tl
                            t_tc = r.op("tensor", lambda e, tl=tl: e.transpose(
                                out=pcol[0][:, :],
                                in_=rowtmp[0:1, tl * 128:(tl + 1) * 128],
                                identity=ident[0:1, 0:1]),
                                reads=[t_rw, guard.get("pcolfree"),
                                       guard.get(("trev", 2))], sem="T")
                            t_ev = r.op("scalar", lambda e, qt_g=qt_g, od=od, tl=tl: e.copy(
                                out=nq2col(qt_g, od), in_=pcol[0][:, :]),
                                reads=[t_tc], sem="S")
                            guard["pcolfree"] = t_ev
                            guard[("nq2", qt_g, od)] = t_ev
                        guard["rowfree"] = t_tc
                    r.nq2 = {k: v for k, v in guard.items() if isinstance(k, tuple) and k[0] == "nq2"}                         if not hasattr(r, "nq2") else {**r.nq2, **{k: v for k, v in guard.items() if isinstance(k, tuple) and k[0] == "nq2"}}
                if lnspill and PH_LVL >= 4:
                    sp2 = r.dma_batch("sync", [
                        (ln1s[p * 128:(p + 1) * 128, ch * CHK:(ch + 1) * CHK],
                         lf[:, p * CHK:(p + 1) * CHK]) for p in range(PC)],
                        reads=[lf_ready])
                    guard[("lfw", ch % 2)] = [mm_last, sp2]
                else:
                    guard[("lfw", ch % 2)] = [mm_last, lf_ready]
            r.phase_end()

        if lim >= 1:
            phase_AC(x_seq, T // CHK, 1, qkvb_sb[:, PC:2 * PC], 1.0, Ks, lnspill=True,
                     knorm=True)

        wv = wk
        VST0 = PC * C + 2 * PC * CHK
        vst = lambda p: ar[:, VST0 + (p % 2) * HAUG: VST0 + (p % 2 + 1) * HAUG]
        bv_row = ar[0:1, VST0 + 2 * HAUG: VST0 + 2 * HAUG + C]

        if lim >= 2:
            hpv = max(PC // 2, 1)
            t_wv0 = r.dma_batch("gpsimd", [
                (wv[:, p * C:(p + 1) * C], qkvT[p * 128:(p + 1) * 128, 2 * C:3 * C])
                for p in range(hpv)])
            t_wv = r.dma_batch("gpsimd", [
                (wv[:, p * C:(p + 1) * C], qkvT[p * 128:(p + 1) * 128, 2 * C:3 * C])
                for p in range(hpv, PC)]) if PC > 1 else t_wv0
            t_bv = r.dma("gpsimd", bv_row, qkvb_flat[None, 2 * C:3 * C])
            t_vi0 = r.op("vector", lambda e: e.memset(
                vst(0).rearrange("p (h x) -> p h x", x=129)[:, :, 128:129], 1.0), sem="V")
            t_vi1 = r.op("vector", lambda e: e.memset(
                vst(1).rearrange("p (h x) -> p h x", x=129)[:, :, 128:129], 1.0), sem="V")

            gB = {}
            for ch in range(T // CHK):
                lf = ln1f(ch)
                t_lf = r.dma_batch("sync", [
                    (lf[:, p * CHK:(p + 1) * CHK],
                     ln1s[p * 128:(p + 1) * 128, ch * CHK:(ch + 1) * CHK])
                    for p in range(PC)], reads=[gB.get(("lfree", ch % 2))])
                mm_last = None
                for tl in range(CHK // 128):
                    nh = VDW // 128
                    for vd in range(NVD):
                        mm_last = r.op("tensor", lambda e, vd=vd: e.matmul(
                            out=pk[vd][:, 0:VDW], lhsT=ones_bf[0:1, 0:128],
                            rhs=bv_row[0:1, vd * VDW:(vd + 1) * VDW], start=True, stop=False),
                            reads=[t_lf, t_bv, t_ones, t_vi0, t_vi1,
                                   gB.get(("pvfree", vd))], sem=None)
                    for ci in range(PC):
                        for vd in range(NVD):
                            mm_last = r.op("tensor", lambda e, ci=ci, vd=vd, tl=tl, lf=lf: e.matmul(
                                out=pk[vd][:, 0:VDW],
                                lhsT=lf[:, ci * CHK + tl * 128: ci * CHK + tl * 128 + 128],
                                rhs=wv[:, ci * C + vd * VDW: ci * C + vd * VDW + VDW],
                                start=False, stop=(ci == PC - 1)),
                                reads=([t_wv0] if (ci == 0 and vd == 0) else
                                       ([t_wv] if (ci == hpv and vd == 0) else [])),
                                sem="T" if (ci == PC - 1 and vd == NVD - 1) else None)
                    tvev = None
                    for vd in range(NVD):
                        tvev = r.op("vector", lambda e, vd=vd, tl=tl, nh=nh: e.tensor_copy(
                            out=vst(tl).rearrange("p (h x) -> p h x", x=129)[:, vd * nh:(vd + 1) * nh, 0:128],
                            in_=pk[vd][:, 0:VDW].rearrange("p (h x) -> p h x", x=128)),
                            reads=[mm_last, gB.get(("vsfree", tl % 2))], sem="V")
                        gB[("pvfree", vd)] = tvev
                    t0 = ch * CHK + tl * 128
                    gB[("vsfree", tl % 2)] = r.dma("sync", Vs[t0:t0 + 128, :], vst(tl),
                                                   reads=[tvev])
                gB[("lfree", ch % 2)] = mm_last
            r.phase_end()

        if lim >= 3:
            phase_AC(x_own, OWN // CHK, 0, qb_scaled, cfg.qscale, Qs, lnspill=False,
                     qnorm=True)

        psA.close()

        qbuf = ar[:, 0:PC * HALF]
        KB0 = PC * HALF
        kbuf = lambda p: ar[:, KB0 + (p % 2) * PC * KB: KB0 + (p % 2 + 1) * PC * KB]
        VB0 = KB0 + 2 * PC * KB
        vbuf = lambda p: ar[:, VB0 + (p % 2) * (KB // 128) * HAUG:
                            VB0 + (p % 2 + 1) * (KB // 128) * HAUG]
        PB0 = VB0 + 2 * (KB // 128) * HAUG
        pbuf = lambda p: ar[:, PB0 + (p % 4) * KB: PB0 + (p % 4 + 1) * KB]
        PTB0 = PB0 + 4 * KB
        ptb = lambda p: ar[:, PTB0 + (p % 4) * KB: PTB0 + (p % 4 + 1) * KB]
        OFB0 = PTB0 + 4 * KB
        ofbf = lambda p: ar[:, OFB0 + (p % 4) * 128: OFB0 + (p % 4 + 1) * 128]

        obuf = lambda qt: af[:, (qt % QTH) * HAUG: (qt % QTH + 1) * HAUG]
        MKOFF = QTH * HAUG
        mslot = lambda p: af[:, MKOFF + (p % 2) * KB: MKOFF + (p % 2 + 1) * KB]

        WO0 = PC * HALF
        wo = ar[:, WO0: WO0 + PC * CH2]
        ln2f = ar[:, LN2F0: LN2F0 + PC * HALF]
        OBR0 = LN2F0 + PC * HALF
        ob_row = ar[0:1, OBR0: OBR0 + C]
        ybuf = lambda qt: af[:, (qt % QTH) * C: (qt % QTH + 1) * C]

        gsb = lambda p: ar[:, (p % 2) * PC * SLAB: (p % 2 + 1) * PC * SLAB]
        usb = lambda p: ar[:, 2 * PC * SLAB + (p % 2) * PC * SLAB:
                           2 * PC * SLAB + (p % 2 + 1) * PC * SLAB]
        DS0 = LN2F0 + PC * HALF
        dsb = ar[:, DS0: DS0 + (SLAB // 128) * C]
        hsb = tb[:, 0:(SLAB // 128) * HALF]
        silu_t = af[:, XB0: XB0 + HALF]

        t_rt = r.dma_batch("gpsimd", [
            (router_sb[:, p * 8:(p + 1) * 8], routerT[p * 128:(p + 1) * 128, 0:8])
            for p in range(PC)])

        mask_idx = 0
        out_toks = []
        t_ob = None
        for half in range(2):
            if lim < 4 + 3 * half:
                break
            qts = list(range(half * QTH, (half + 1) * QTH))
            maxcov = max(cov_of(cfg, i) for i in qts)

            psD = ExitStack()
            ps_s = [psD.enter_context(nc.psum_tensor(f"psS{half}{i}", [128, KB], F32)) for i in range(3)]
            ptp = [psD.enter_context(nc.psum_tensor(f"ptD{half}{i}", [128, KB], BF16)).ap()[:, :] for i in range(3)]
            pav = [psD.enter_context(nc.psum_tensor(f"pav{half}{i}", [128, 129], F32)).ap()[:, :] for i in range(2)]

            t_qbuf = r.dma_batch("sync", [
                (qbuf[:, p * HALF:(p + 1) * HALF],
                 Qs[p * 128:(p + 1) * 128, half * HALF:(half + 1) * HALF])
                for p in range(PC)])

            bset = {}
            for qt in qts:
                for hd in range(H):
                    bb = (qt % QTH) * H + hd
                    bpp = bb % 2
                    t_bc = r.op("tensor", lambda e, hd=hd, bpp=bpp: e.matmul(
                        out=pav[bpp][:, 0:1], lhsT=ones_bf[0:1, 0:128],
                        rhs=kmx2row[0:1, hd:hd + 1], start=True, stop=True),
                        reads=[bset.get(("pf", bpp)), r.kmtok, t_ones], sem="T")
                    t_m = r.op("vector", lambda e, qt=qt, hd=hd, bpp=bpp: e.tensor_mul(
                        out=mcol(qt, hd, 1), in0=nq2col(qt, hd), in1=pav[bpp][:, 0:1]),
                        reads=[t_bc, r.nq2[("nq2", qt, hd)], bset.get(("mf", qt, hd))], sem="V")
                    bset[("pf", bpp)] = t_m
                    t_sq = r.op("scalar", lambda e, qt=qt, hd=hd: e.activation(
                        out=mcol(qt, hd, 1), in_=mcol(qt, hd, 1), func=AF.Sqrt),
                        reads=[t_m], sem="S")
                    t_bn = r.op("vector", lambda e, qt=qt, hd=hd: e.tensor_scalar_mul(
                        out=mcol(qt, hd, 0), in0=mcol(qt, hd, 1), scalar1=-1.0),
                        reads=[t_sq], sem="V")
                    bset[(qt, hd)] = t_bn

            kv_free, o_tok, s_free, p_free, pt_free, av_free, m_free = {}, {}, {}, {}, {}, {}, {}
            tup = 0
            last_s_mm = None
            kv_done = {}
            pend = []
            pend2 = []
            bkc = [0]
            avc = [0]

            def flush_mid():
                st = pend.pop(0)
                bp = bkc[0] % 3
                ap_ = bkc[0] % 4
                bkc[0] += 1
                pp = st["pp"]
                t_tr = None
                for j in range(KB // 128):
                    t_tr = r.op("tensor", lambda e, j=j, pp=pp, bp=bp: e.transpose(
                        out=ptp[bp][:, j * 128:(j + 1) * 128],
                        in_=pbuf(pp)[:, j * 128:(j + 1) * 128], identity=ident),
                        reads=([st["t_p"], pt_free.get(bp)] if j == 0 else []),
                        sem="T" if j == KB // 128 - 1 else None)
                t_ptc = r.op("vector", lambda e, ap_=ap_, bp=bp: e.tensor_copy(
                    out=ptb(ap_), in_=ptp[bp][:, :]),
                    reads=[t_tr, av_free.get((ap_, "ptb"))], sem="V")
                p_free[pp] = t_ptc
                pt_free[bp] = t_ptc
                st["ap_"] = ap_
                st["t_ptc"] = t_ptc
                pend2.append(st)

            def flush_back():
                st = pend2.pop(0)
                ap_ = st["ap_"]
                vp_ = avc[0] % 2
                avc[0] += 1
                qt, hd, kp = st["qt"], st["hd"], st["kp"]
                t_av = None
                for j in range(KB // 128):
                    t_av = r.op("tensor", lambda e, j=j, ap_=ap_, vp_=vp_, hd=hd, kp=kp: e.matmul(
                        out=pav[vp_][:, :],
                        lhsT=ptb(ap_)[:, j * 128:(j + 1) * 128],
                        rhs=vbuf(kp)[:, j * HAUG + hd * 129: j * HAUG + hd * 129 + 129],
                        start=(j == 0), stop=(j == KB // 128 - 1)),
                        reads=([st["t_ptc"], av_free.get((vp_, "pav"))] if j == 0 else []),
                        sem="T" if j == KB // 128 - 1 else None)
                av_free[(ap_, "ptb")] = t_av
                osl = obuf(qt).rearrange("p (h x) -> p h x", x=129)[:, hd:hd + 1, :].squeeze(1)
                if st["first"]:
                    t_o = r.op("vector", lambda e, osl=osl, vp_=vp_: e.tensor_copy(
                        out=osl, in_=pav[vp_][:, :]),
                        reads=[t_av, o_tok.get((qt, hd))], sem="V")
                else:
                    t_o = r.op("vector", lambda e, osl=osl, vp_=vp_: e.tensor_add(
                        out=osl, in0=osl, in1=pav[vp_][:, :]),
                        reads=[t_av, o_tok.get((qt, hd))], sem="V")
                o_tok[(qt, hd)] = t_o
                o_tok[(qt, hd, "mf")] = t_o
                av_free[(vp_, "pav")] = t_o
                if st["last_of_kb"]:
                    kv_done[st["kb"]] = t_o
                return t_o

            LAG = 2
            LAG2 = 2
            for kb in range(maxcov):
                kp = kb % 2
                kvpairs = [(kbuf(kp)[:, p * KB:(p + 1) * KB],
                            Ks[p * 128:(p + 1) * 128, kb * KB:(kb + 1) * KB])
                           for p in range(PC)]
                kvpairs += [(vbuf(kp)[:, j * HAUG:(j + 1) * HAUG],
                             Vs[kb * KB + j * 128: kb * KB + (j + 1) * 128, :])
                            for j in range(KB // 128)]
                t_kv = r.dma_batch("sync", kvpairs, reads=[kv_done.get(kb - 2)])
                active = [q for q in qts if kb < cov_of(cfg, q)]
                for qn, qt in enumerate(active):
                    qrel = qt % QTH
                    need_mask = open_of(cfg, qt) <= kb
                    t_mask = None
                    mp = mask_idx % 2
                    if need_mask:
                        t_mask = r.dma("sync", mslot(mp), masks[mask_idx, :, :],
                                       reads=[m_free.get(mp)])
                        mask_idx += 1
                    first = (kb == 0)
                    for hd in range(H):
                        pp = tup % 4
                        ps = tup % 3
                        t_smm = r.op("tensor", lambda e, hd=hd, qrel=qrel, kp=kp, ps=ps: e.matmul(
                            out=ps_s[ps][:, :],
                            lhsT=qbuf[:, hd * HALF + qrel * 128: hd * HALF + qrel * 128 + 128],
                            rhs=kbuf(kp)[:, hd * KB:(hd + 1) * KB],
                            start=True, stop=True),
                            reads=[t_qbuf, t_kv, s_free.get(ps)], sem="T")
                        last_s_mm = t_smm
                        t_sm = t_smm
                        if need_mask:
                            t_sm = r.op("vector", lambda e, ps=ps, mp=mp: e.tensor_add(
                                out=ps_s[ps][:, :], in0=ps_s[ps][:, :], in1=mslot(mp)),
                                reads=[t_smm, t_mask], sem="V")
                            m_free[mp] = t_sm
                        t_p = r.op("scalar", lambda e, pp=pp, ps=ps, qt=qt, hd=hd: e.activation(
                            out=pbuf(pp), in_=ps_s[ps][:, :], func=AF.Exp,
                            bias=mcol(qt, hd, 0)),
                            reads=[t_sm, bset[(qt, hd)], p_free.get(pp)], sem="S")
                        s_free[ps] = t_p
                        pend.append({"pp": pp, "qt": qt, "hd": hd, "kp": kp,
                                     "first": first, "t_p": t_p,
                                     "kb": kb,
                                     "last_of_kb": (qn == len(active) - 1 and hd == H - 1)})
                        if len(pend) > LAG:
                            flush_mid()
                        if len(pend2) > LAG2:
                            flush_back()
                        tup += 1
            while pend:
                flush_mid()
            while pend2:
                flush_back()

            fin_ev = {}
            t_fin_last = None
            fpend = []
            fbk = [0]

            def flush_fin():
                stf = fpend.pop(0)
                fp2 = fbk[0] % 3
                fbk[0] += 1
                t_tr = r.op("tensor", lambda e, fp=stf["fp"], fp2=fp2: e.transpose(
                    out=ptp[fp2][:, 0:128], in_=ofbf(fp), identity=ident),
                    reads=[stf["t_of"], fin_ev.get(("ev", fp2)), last_s_mm], sem="T")
                t_ev = r.op("scalar", lambda e, fp2=fp2, hd=stf["hd"], qrel=stf["qrel"]: e.copy(
                    out=qbuf[:, hd * HALF + qrel * 128: hd * HALF + qrel * 128 + 128],
                    in_=ptp[fp2][:, 0:128]),
                    reads=[t_tr], sem="S")
                fin_ev[("ev", fp2)] = t_ev
                fin_ev[("trdone", stf["fp"])] = t_ev
                return t_ev

            fi = 0
            for qt in qts:
                qrel = qt % QTH
                for hd in range(H):
                    osl = obuf(qt).rearrange("p (h x) -> p h x", x=129)[:, hd:hd + 1, :].squeeze(1)
                    fp = fi % 4
                    t_rl = r.op("vector", lambda e, osl=osl, fp=fp: e.reciprocal(
                        out=lrec(fp), in_=osl[:, 128:129]),
                        reads=[o_tok[(qt, hd)], fin_ev.get(("lfree", fp))], sem="V")
                    t_of = r.op("vector", lambda e, osl=osl, fp=fp: e.tensor_scalar_mul(
                        out=ofbf(fp), in0=osl[:, 0:128], scalar1=lrec(fp)),
                        reads=[t_rl, fin_ev.get(("trdone", fp))], sem="V")
                    fin_ev[("lfree", fp)] = t_of
                    fpend.append({"fp": fp, "t_of": t_of, "hd": hd, "qrel": qrel})
                    if len(fpend) > LAG:
                        t_fin_last = flush_fin()
                    fi += 1
            while fpend:
                t_fin_last = flush_fin()
            if dbg:
                r.dma_batch("sync", [
                    (dbg_ot[half, p * 128:(p + 1) * 128, :], qbuf[:, p * HALF:(p + 1) * HALF])
                    for p in range(PC)], reads=[t_fin_last])
            r.phase_end()

            if lim < 5 + 3 * half:
                psD.close()
                break
            y_tok, x_used, xfree, ln2_toks = {}, {}, {}, {}
            mm = None
            for coh in range(2):
                wt = r.dma_batch("gpsimd", [
                    (wo[:, p * CH2:(p + 1) * CH2],
                     outwT[p * 128:(p + 1) * 128, coh * CH2:(coh + 1) * CH2])
                    for p in range(PC)], reads=[y_tok.get("wofree")])
                if coh == 0:
                    t_ob = r.dma("gpsimd", ob_row, outb[None, :])
                unit = 0
                for qt in qts:
                    qrel = qt % QTH
                    xp = qt % 2
                    x_used[(qt, coh)] = r.dma("sync", xslot(xp),
                                              x_own[qt * 128:(qt + 1) * 128, :],
                                              reads=[xfree.get(xp)])
                    for cc in range(NCC):
                        up = unit % 2
                        mm = r.op("tensor", lambda e, coh=coh, cc=cc, up=up: e.matmul(
                            out=ps_s[up][:, 0:CCW], lhsT=ones_bf[0:1, 0:128],
                            rhs=ob_row[0:1, coh * CH2 + cc * CCW: coh * CH2 + cc * CCW + CCW],
                            start=True, stop=False),
                            reads=[wt, t_ob, y_tok.get(("pofree", up)), t_fin_last], sem=None)
                        for ci in range(PC):
                            mm = r.op("tensor", lambda e, ci=ci, cc=cc, coh=coh, qrel=qrel, up=up: e.matmul(
                                out=ps_s[up][:, 0:CCW],
                                lhsT=qbuf[:, ci * HALF + qrel * 128: ci * HALF + qrel * 128 + 128],
                                rhs=wo[:, ci * CH2 + cc * CCW: ci * CH2 + cc * CCW + CCW],
                                start=False, stop=(ci == PC - 1)),
                                sem="T" if ci == PC - 1 else None)
                        t_ev = r.op("vector", lambda e, qt=qt, coh=coh, cc=cc, xp=xp, up=up: e.scalar_tensor_tensor(
                            out=ybuf(qt)[:, coh * CH2 + cc * CCW: coh * CH2 + cc * CCW + CCW],
                            in0=xslot(xp)[:, coh * CH2 + cc * CCW: coh * CH2 + cc * CCW + CCW],
                            scalar=1.0, in1=ps_s[up][:, 0:CCW], op0=ALU.mult, op1=ALU.add),
                            reads=[mm, x_used[(qt, coh)], y_tok.get(("yfree", qt))], sem="V")
                        y_tok[("pofree", up)] = t_ev
                        xfree[xp] = t_ev
                        unit += 1
                    y_tok[(qt, coh)] = t_ev
                y_tok["wofree"] = mm
            t_l2last = None
            for qt in qts:
                qrel = qt % QTH
                xp = qt % 2
                t_norm = ln_chain(ybuf(qt), xp, lnbf(xp),
                                  [y_tok[(qt, 0)], y_tok[(qt, 1)], ln2_toks.get(("lnbf_free", xp))])
                ev = None
                for p in range(PC):
                    pp = p % 2
                    t_tr = r.op("tensor", lambda e, p=p, pp=pp, xp=xp: e.transpose(
                        out=ptp[pp][:, 0:128], in_=lnbf(xp)[:, p * 128:(p + 1) * 128],
                        identity=ident),
                        reads=[t_norm, ln2_toks.get(("ev", pp))], sem="T")
                    ev = r.op("scalar", lambda e, p=p, pp=pp, qrel=qrel: e.copy(
                        out=ln2f[:, p * HALF + qrel * 128: p * HALF + qrel * 128 + 128],
                        in_=ptp[pp][:, 0:128]),
                        reads=[t_tr], sem="S")
                    ln2_toks[("ev", pp)] = ev
                ln2_toks[("lnbf_free", xp)] = ev
                ln2_toks[qt] = ev
                t_l2last = ev
            if dbg:
                for qt in qts:
                    r.dma("sync", dbg_y[qt * 128:(qt + 1) * 128, :], ybuf(qt),
                          reads=[y_tok[(qt, 0)], y_tok[(qt, 1)]])
                r.dma_batch("sync", [
                    (dbg_l2[half, p * 128:(p + 1) * 128, :], ln2f[:, p * HALF:(p + 1) * HALF])
                    for p in range(PC)], reads=[t_l2last])
            r.phase_end()
            psD.close()

            if lim < 6 + 3 * half:
                break
            psF = ExitStack()
            pgF = psF.enter_context(nc.psum_tensor(f"pgF{half}", [128, HALF], F32))
            puF = psF.enter_context(nc.psum_tensor(f"puF{half}", [128, HALF], F32))
            pyF = [psF.enter_context(nc.psum_tensor(f"pyF{half}{i}", [128, YW], F32)) for i in range(2)]

            rt_ev = {}
            for qt in qts:
                qrel = qt % QTH
                up = qt % 2
                mm = None
                for ci in range(PC):
                    mm = r.op("tensor", lambda e, ci=ci, qrel=qrel: e.matmul(
                        out=pgF[:, 0:8],
                        lhsT=ln2f[:, ci * HALF + qrel * 128: ci * HALF + qrel * 128 + 128],
                        rhs=router_sb[:, ci * 8:(ci + 1) * 8],
                        start=(ci == 0), stop=(ci == PC - 1)),
                        reads=([t_rt, t_l2last, rt_ev.get("prfree")] if ci == 0 else []),
                        sem="T" if ci == PC - 1 else None)
                lg = rlog(up)
                t_lg = r.op("vector", lambda e, lg=lg: e.tensor_copy(out=lg, in_=pgF[:, 0:8]),
                            reads=[mm, rt_ev.get(("lgfree", up))], sem="V")
                rt_ev["prfree"] = t_lg
                t_nmx = r.op("vector", lambda e, lg=lg: e.reduce_max(
                    out=rst(0), in_=lg, axis=AX.X, negate=True), reads=[t_lg, rt_ev.get("st")], sem="V")
                t_d1 = r.op("vector", lambda e, lg=lg: e.tensor_scalar_add(
                    out=lg, in0=lg, scalar1=rst(0)), reads=[t_nmx], sem="V")
                t_eq = r.op("vector", lambda e, lg=lg: e.tensor_single_scalar(
                    out=rtmp, in_=lg, scalar=0.0, op=ALU.is_ge), reads=[t_d1], sem="V")
                t_cnt = r.op("vector", lambda e: e.reduce_sum(
                    out=rst(1), in_=rtmp, axis=AX.X), reads=[t_eq], sem="V")
                t_msk = r.op("vector", lambda e, lg=lg: e.scalar_tensor_tensor(
                    out=rtmp, in0=rtmp, scalar=NEG, in1=lg, op0=ALU.mult, op1=ALU.add),
                    reads=[t_eq, t_cnt], sem="V")
                t_n2 = r.op("vector", lambda e: e.reduce_max(
                    out=rst(2), in_=rtmp, axis=AX.X, negate=True), reads=[t_msk], sem="V")
                t_tie = r.op("vector", lambda e: e.tensor_single_scalar(
                    out=rst(3), in_=rst(1), scalar=2.0, op=ALU.is_ge), reads=[t_cnt], sem="V")
                t_de = r.op("vector", lambda e: e.tensor_scalar(
                    out=rst(4), in0=rst(3), scalar1=-1.0, scalar2=1.0,
                    op0=ALU.mult, op1=ALU.add), reads=[t_tie], sem="V")
                t_de2 = r.op("vector", lambda e: e.tensor_mul(
                    out=rst(5), in0=rst(2), in1=rst(4)), reads=[t_n2, t_de], sem="V")
                t_u2 = r.op("scalar", lambda e: e.activation(
                    out=rst(6), in_=rst(5), func=AF.Exp, scale=-1.0), reads=[t_de2], sem="S")
                t_ss = r.op("vector", lambda e: e.tensor_scalar_add(
                    out=rst(7), in0=rst(6), scalar1=1.0), reads=[t_u2], sem="V")
                t_rc = r.op("vector", lambda e: e.reciprocal(
                    out=rst(8), in_=rst(7)), reads=[t_ss], sem="V")
                t_c = r.op("vector", lambda e, qt=qt: e.tensor_mul(
                    out=ccol(qt), in0=rst(7), in1=rst(8)), reads=[t_rc], sem="V")
                rt_ev[qt] = t_c
                rt_ev[("lgfree", up)] = t_d1
                rt_ev["st"] = t_c

            ff = {}
            ff["pgfree"] = rt_ev["prfree"]
            slabs = [("r", js) for js in range(cfg.RINTER // SLAB)] + \
                    [("s", js) for js in range(cfg.INTER // SLAB)]
            for sn, (stype, js) in enumerate(slabs):
                gsrc, usrc, dsrc = (rgT, ruT, rdT) if stype == "r" else (gT, uT, dT)
                sp = sn % 2
                tg = r.dma_batch("gpsimd", [
                    (gsb(sp)[:, p * SLAB:(p + 1) * SLAB],
                     gsrc[p * 128:(p + 1) * 128, js * SLAB:(js + 1) * SLAB])
                    for p in range(PC)], reads=[ff.get(("gfree", sp))])
                tu = r.dma_batch("gpsimd", [
                    (usb(sp)[:, p * SLAB:(p + 1) * SLAB],
                     usrc[p * 128:(p + 1) * 128, js * SLAB:(js + 1) * SLAB])
                    for p in range(PC)], reads=[ff.get(("ufree", sp))])
                td = r.dma_batch("gpsimd", [
                    (dsb[:, j * C:(j + 1) * C],
                     dsrc[js * SLAB + j * 128: js * SLAB + (j + 1) * 128, :])
                    for j in range(SLAB // 128)], reads=[ff.get("dfree")])
                dn_last = None
                for j in range(SLAB // 128):
                    mmg = None
                    for tk in range(NTK):
                        for ci in range(PC):
                            mmg = r.op("tensor", lambda e, ci=ci, j=j, tk=tk, sp=sp: e.matmul(
                                out=pgF[:, tk * TW:(tk + 1) * TW],
                                lhsT=gsb(sp)[:, ci * SLAB + j * 128: ci * SLAB + j * 128 + 128],
                                rhs=ln2f[:, ci * HALF + tk * TW: ci * HALF + tk * TW + TW],
                                start=(ci == 0), stop=(ci == PC - 1)),
                                reads=([tg, ff.get("pgfree")] if ci == 0 and tk == 0 else []),
                                sem="T" if (ci == PC - 1 and tk == NTK - 1) else None)
                    mmu = None
                    for tk in range(NTK):
                        for ci in range(PC):
                            mmu = r.op("tensor", lambda e, ci=ci, j=j, tk=tk, sp=sp: e.matmul(
                                out=puF[:, tk * TW:(tk + 1) * TW],
                                lhsT=usb(sp)[:, ci * SLAB + j * 128: ci * SLAB + j * 128 + 128],
                                rhs=ln2f[:, ci * HALF + tk * TW: ci * HALF + tk * TW + TW],
                                start=(ci == 0), stop=(ci == PC - 1)),
                                reads=([tu, ff.get("pufree")] if ci == 0 and tk == 0 else []),
                                sem="T" if (ci == PC - 1 and tk == NTK - 1) else None)
                    t_si = r.op("scalar", lambda e: e.activation(
                        out=silu_t, in_=pgF[:, :], func=AF.Silu),
                        reads=[mmg, ff.get("silufree")], sem="S")
                    ff["pgfree"] = t_si
                    t_h = r.op("vector", lambda e, j=j: e.tensor_tensor(
                        out=hsb[:, j * HALF:(j + 1) * HALF], in0=silu_t, in1=puF[:, :],
                        op=ALU.mult),
                        reads=[t_si, mmu, ff.get(("hfree", j))], sem="V")
                    ff["pufree"] = t_h
                    ff["silufree"] = t_h
                    ff[("h", j)] = t_h
                unit = 0
                for qt in qts:
                    qrel = qt % QTH
                    for yc in range(NYC):
                        up = unit % 2
                        mm = None
                        ncc2 = max(YW // 512, 1)
                        ccw2 = min(512, YW)
                        for j in range(SLAB // 128):
                            for cc in range(ncc2):
                                mm = r.op("tensor", lambda e, j=j, cc=cc, yc=yc, qrel=qrel, up=up, ccw2=ccw2: e.matmul(
                                    out=pyF[up][:, cc * ccw2:(cc + 1) * ccw2],
                                    lhsT=hsb[:, j * HALF + qrel * 128: j * HALF + qrel * 128 + 128],
                                    rhs=dsb[:, j * C + yc * YW + cc * ccw2: j * C + yc * YW + cc * ccw2 + ccw2],
                                    start=(j == 0), stop=(j == SLAB // 128 - 1)),
                                    reads=([td, ff.get(("pyfree", up)), ff[("h", j)]]
                                           if cc == 0 else [ff[("h", j)]]),
                                    sem="T" if (j == SLAB // 128 - 1 and cc == ncc2 - 1) else None)
                        dn_last = mm
                        if stype == "r":
                            t_ev = r.op("vector", lambda e, qt=qt, yc=yc, up=up: e.scalar_tensor_tensor(
                                out=ybuf(qt)[:, yc * YW:(yc + 1) * YW],
                                in0=pyF[up][:, :], scalar=ccol(qt),
                                in1=ybuf(qt)[:, yc * YW:(yc + 1) * YW],
                                op0=ALU.mult, op1=ALU.add),
                                reads=[mm, rt_ev[qt], ff.get(("ylast", qt, yc)),
                                       ln2_toks[qt]], sem="V")
                        else:
                            t_ev = r.op("vector", lambda e, qt=qt, yc=yc, up=up: e.tensor_add(
                                out=ybuf(qt)[:, yc * YW:(yc + 1) * YW],
                                in0=ybuf(qt)[:, yc * YW:(yc + 1) * YW],
                                in1=pyF[up][:, :]),
                                reads=[mm, ff.get(("ylast", qt, yc)), ln2_toks[qt]], sem="V")
                        ff[("pyfree", up)] = t_ev
                        ff[("ylast", qt, yc)] = t_ev
                        unit += 1
                for j in range(SLAB // 128):
                    ff[("hfree", j)] = dn_last
                ff[("gfree", sp)] = dn_last
                ff[("ufree", sp)] = dn_last
                ff["dfree"] = dn_last
            for qt in qts:
                reads = [ff[("ylast", qt, yc)] for yc in range(NYC)]
                tko = r.dma("sync", out_own[qt * 128:(qt + 1) * 128, :], ybuf(qt),
                            reads=reads)
                out_toks.append(tko)
            r.phase_end()
            psF.close()

        if out_toks:
            r.op("sync", lambda e: e.nop(), reads=out_toks)
        else:
            r.op("sync", lambda e: e.nop(), reads=r.phase_end())

        blk = es.enter_context(nc.Block())
        r.emit(blk, sems)

    return nc



def _host_inputs(cfg, x, qkv_w, qkv_b, out_w, out_b, router_w,
                 sh_gate, sh_up, sh_down, r_gate, r_up, r_down):
    C, T, OWN, A = cfg.C, cfg.T, cfg.OWN, cfg.A
    import ml_dtypes
    shared = {
        "qkvT": np.ascontiguousarray(np.asarray(qkv_w, np.float32).T),
        "qkvb_cols": np.ascontiguousarray(np.asarray(qkv_b, np.float32).reshape(3 * C // 128, 128).T),
        "qkvb_flat": np.ascontiguousarray(np.asarray(qkv_b, np.float32)),
        "outwT": np.ascontiguousarray(np.asarray(out_w, np.float32).T),
        "outb": np.ascontiguousarray(np.asarray(out_b, np.float32)),
        "gT": np.ascontiguousarray(np.asarray(sh_gate, np.float32).T),
        "uT": np.ascontiguousarray(np.asarray(sh_up, np.float32).T),
        "dT": np.ascontiguousarray(np.asarray(sh_down, np.float32).T),
        "rgT": np.ascontiguousarray(np.asarray(r_gate, np.float32).T),
        "ruT": np.ascontiguousarray(np.asarray(r_up, np.float32).T),
        "rdT": np.ascontiguousarray(np.asarray(r_down, np.float32).T),
        "routerT": np.ascontiguousarray(np.asarray(router_w, np.float32).T),
        "identin": np.eye(128, dtype=ml_dtypes.bfloat16),
    }
    masks_by_cls = [build_masks(cfg, 0), build_masks(cfg, 1)]
    B = x.shape[0]
    in_maps = []
    for c in range(2 * B):
        b, cls = c // 2, c % 2
        xs = np.ascontiguousarray(x[b])
        xo = np.ascontiguousarray(x[b].reshape(T // 128, 128, C)[cls::2].reshape(OWN, C))
        in_maps.append({"x_seq": xs, "x_own": xo, "masks": masks_by_cls[cls], **shared})
    return in_maps


def _scatter(cfg, outs, B):
    T, OWN, C = cfg.T, cfg.OWN, cfg.C
    full = np.empty((B, T, C), np.float32)
    for c, o in enumerate(outs):
        b, cls = c // 2, c % 2
        full[b].reshape(T // 128, 128, C)[cls::2] = o.reshape(OWN // 128, 128, C)
    return full


_NC_CACHE = {}


def kernel(L0_attn_x, L0_attn_qkv_w, L0_attn_qkv_b, L0_attn_out_w, L0_attn_out_b,
           L0_ffn_router_weight, L0_ffn_sh_gate, L0_ffn_sh_up, L0_ffn_sh_down,
           L0_ffn_r_gate, L0_ffn_r_up, L0_ffn_r_down):
    cfg = FULL
    x = np.asarray(L0_attn_x, np.float32)
    in_maps = _host_inputs(cfg, x, L0_attn_qkv_w, L0_attn_qkv_b, L0_attn_out_w,
                           L0_attn_out_b, L0_ffn_router_weight, L0_ffn_sh_gate,
                           L0_ffn_sh_up, L0_ffn_sh_down, L0_ffn_r_gate,
                           L0_ffn_r_up, L0_ffn_r_down)
    if "nc" not in _NC_CACHE:
        _NC_CACHE["nc"] = build(cfg)
    nc = _NC_CACHE["nc"]
    res = run_bass_kernel_spmd(nc, in_maps, core_ids=list(range(8)))
    outs = [r["out_own"] for r in res.results]
    return _scatter(cfg, outs, x.shape[0])


# version 23
# speedup vs baseline: 1.1300x; 1.0199x over previous
import os
import numpy as np
from contextlib import ExitStack

import concourse.bass as bass
import concourse.mybir as mybir
from concourse.bass_utils import run_bass_kernel_spmd

F32 = mybir.dt.float32
BF16 = mybir.dt.bfloat16
AF = mybir.ActivationFunctionType
ALU = mybir.AluOpType
AX = mybir.AxisListType

NEG = -1.0e30


class Cfg:
    def __init__(self, C, T, INTER, RINTER):
        self.C, self.T, self.INTER, self.RINTER = C, T, INTER, RINTER
        self.PC = C // 128
        self.H = self.PC
        self.HAUG = self.H * 129
        self.OWN = T // 2
        self.A = self.OWN // 2
        self.HALF = self.OWN // 2
        self.NQT = self.OWN // 128
        self.QTH = self.NQT // 2
        self.KB = 512
        self.NKB = T // self.KB
        self.CHK = 512
        self.SLAB = 512
        self.CH2 = C // 2
        self.eps = 1e-5
        self.qscale = 1.0 / np.sqrt(128.0)


FULL = Cfg(C=2048, T=4096, INTER=8192, RINTER=1024)
MINI = Cfg(C=256, T=1024, INTER=1024, RINTER=512)


def qpos_of(cfg, cls, i):
    return 256 * i + 128 * cls


def cov_of(cfg, i):
    mx = max(qpos_of(cfg, 0, i), qpos_of(cfg, 1, i))
    return (mx + 128 + cfg.KB - 1) // cfg.KB


def open_of(cfg, i):
    mn = min(qpos_of(cfg, 0, i), qpos_of(cfg, 1, i))
    return mn // cfg.KB


def mask_pairs(cfg):
    out = []
    for half in range(2):
        qts = range(half * cfg.QTH, (half + 1) * cfg.QTH)
        maxcov = max(cov_of(cfg, i) for i in qts)
        for kb in range(maxcov):
            for i in qts:
                if open_of(cfg, i) <= kb < cov_of(cfg, i):
                    out.append((half, kb, i))
    return out


def build_masks(cfg, cls):
    pairs = mask_pairs(cfg)
    m = np.zeros((len(pairs), 128, cfg.KB), np.float32)
    kr = np.arange(cfg.KB)
    qr = np.arange(128)
    for n, (_h, kb, i) in enumerate(pairs):
        qg = qpos_of(cfg, cls, i) + qr[:, None]
        kg = kb * cfg.KB + kr[None, :]
        m[n] = np.where(kg <= qg, 0.0, NEG).astype(np.float32)
    return m



ENGS = ("sync", "tensor", "vector", "scalar", "gpsimd")
_ESEM = {"sync": "SY", "gpsimd": "G", "scalar": "S", "vector": "V", "tensor": "T"}


def _flat(reads):
    out = []
    for x in reads:
        if x is None:
            continue
        if isinstance(x, (list, tuple)) and x and isinstance(x[0], (list, tuple)):
            out.extend(_flat(x))
        elif isinstance(x, (list, tuple)) and len(x) == 2 and isinstance(x[0], str):
            out.append(x)
        elif isinstance(x, (list, tuple)):
            out.extend(_flat(x))
    return out


NSYNC_LANES = 6
NGP_LANES = 4
SYNC_LANES = [f"DS{i}" for i in range(NSYNC_LANES)]
GP_LANES = [f"DG{i}" for i in range(NGP_LANES)]


class Rec:

    def __init__(self):
        self.items = {e: [] for e in ENGS}
        self.cnt = {}
        self.pending = {e: [] for e in ENGS}
        self.lane_rr = {"sync": 0, "gpsimd": 0}

    def op(self, eng, fn, reads=(), sem=None, inc=1):
        reads = _flat(list(reads))
        if self.pending[eng]:
            reads = self.pending[eng] + reads
            self.pending[eng] = []
        tok = None
        if sem is not None:
            self.cnt[sem] = self.cnt.get(sem, 0) + inc
            tok = (sem, self.cnt[sem])
        self.items[eng].append({"reads": reads, "fn": fn, "sem": sem,
                                "inc": inc if sem else 0})
        return tok

    def dma_batch(self, eng, pairs, reads=(), sem=None):
        if sem is None:
            lanes = SYNC_LANES if eng == "sync" else GP_LANES
            i = self.lane_rr[eng]
            sem = lanes[i % len(lanes)]
            self.lane_rr[eng] = i + 1
        gate = (sem, self.cnt.get(sem, 0))
        tok = None
        for n, (out, in_) in enumerate(pairs):
            tok = self.op(eng, (lambda o, i_: lambda e: e.dma_start(out=o, in_=i_))(out, in_),
                          reads=(list(reads) + [gate]) if n == 0 else [],
                          sem=sem, inc=16)
        return tok

    def dma(self, eng, out, in_, reads=(), sem=None):
        return self.dma_batch(eng, [(out, in_)], reads=reads, sem=sem)

    def phase_end(self):
        toks = []
        for e in ENGS:
            if not self.items[e]:
                continue
            it = self.items[e][-1]
            if it["sem"] is None:
                it["sem"] = _ESEM[e]
                it["inc"] = 1
                self.cnt[it["sem"]] = self.cnt.get(it["sem"], 0) + it["inc"]
            toks.append((it["sem"], self.cnt[it["sem"]]))
        for lane in SYNC_LANES + GP_LANES:
            if self.cnt.get(lane, 0):
                toks.append((lane, self.cnt[lane]))
        toks = list({s: (s, v) for s, v in toks}.values())
        for e in ENGS:
            self.pending[e] = list(toks)
        return toks

    def emit(self, blk, sems):
        for ename in ENGS:
            stream = self.items[ename]
            if not stream:
                continue

            def runner(e, stream=stream):
                waited = {}
                for it in stream:
                    for s, v in it["reads"]:
                        if waited.get(s, 0) < v:
                            e.wait_ge(sems[s], v)
                            waited[s] = v
                    ins = it["fn"](e)
                    if it["sem"] is not None:
                        ins.then_inc(sems[it["sem"]], it["inc"])
            getattr(blk, ename)(runner)



def build(cfg, dbg=False, lim=99):
    PH_LVL = int(os.environ.get("PH_LVL", "99"))
    nc = bass.Bass("TRN2", target_bir_lowering=False)
    C, T, PC, H = cfg.C, cfg.T, cfg.PC, cfg.H
    OWN, HALF, KB, CHK, SLAB = cfg.OWN, cfg.HALF, cfg.KB, cfg.CHK, cfg.SLAB
    HAUG, CH2 = cfg.HAUG, cfg.CH2
    QTH = cfg.QTH
    assert CHK == KB
    assert cfg.INTER % SLAB == 0 and cfg.RINTER % SLAB == 0
    TW = min(512, HALF)
    NTK = HALF // TW
    YW = min(1024, C)
    NYC = C // YW
    QD = min(4, PC)
    NQD = PC // QD
    VDW = min(512, C)
    NVD = C // VDW
    CCW = min(512, CH2)
    NCC = CH2 // CCW
    NST = max(C // 512, 1)
    STW = min(512, C)

    pairs = mask_pairs(cfg)
    NMASK = len(pairs)

    class _Lazy:
        def __init__(self, name, shape, dtype):
            self.args = (name, shape, dtype)
            self._ap = None

        def __getitem__(self, key):
            if self._ap is None:
                self._ap = nc.dram_tensor(self.args[0], self.args[1], self.args[2],
                                          kind="ExternalInput").ap()
            return self._ap[key]

    x_seq = _Lazy("x_seq", [T, C], F32)
    x_own = _Lazy("x_own", [OWN, C], F32)
    qkvT = _Lazy("qkvT", [C, 3 * C], F32)
    qkvb_cols = _Lazy("qkvb_cols", [128, 3 * PC], F32)
    qkvb_flat = _Lazy("qkvb_flat", [3 * C], F32)
    outwT = _Lazy("outwT", [C, C], F32)
    outb = _Lazy("outb", [C], F32)
    gT = _Lazy("gT", [C, cfg.INTER], F32)
    uT = _Lazy("uT", [C, cfg.INTER], F32)
    dT = _Lazy("dT", [cfg.INTER, C], F32)
    rgT = _Lazy("rgT", [C, cfg.RINTER], F32)
    ruT = _Lazy("ruT", [C, cfg.RINTER], F32)
    rdT = _Lazy("rdT", [cfg.RINTER, C], F32)
    routerT = _Lazy("routerT", [C, 8], F32)
    masks = _Lazy("masks", [max(NMASK, 1), 128, KB], F32)
    identin = _Lazy("identin", [128, 128], BF16)
    out_own = nc.dram_tensor("out_own", [OWN, C], F32, kind="ExternalOutput").ap()

    skind = "ExternalOutput" if dbg else "Internal"
    ln1s = nc.dram_tensor("ln1s", [C, T], BF16, kind=skind).ap()
    Ks = nc.dram_tensor("Ks", [C, T], BF16, kind=skind).ap()
    Vs = nc.dram_tensor("Vs", [T, HAUG], BF16, kind=skind).ap()
    Qs = nc.dram_tensor("Qs", [C, OWN], BF16, kind=skind).ap()
    dbg_y = nc.dram_tensor("dbg_y", [OWN, C], F32, kind="ExternalOutput").ap() if dbg else None
    dbg_ot = nc.dram_tensor("dbg_ot", [2, C, HALF], BF16, kind="ExternalOutput").ap() if dbg else None
    dbg_l2 = nc.dram_tensor("dbg_l2", [2, C, HALF], BF16, kind="ExternalOutput").ap() if dbg else None

    es = ExitStack()
    with es:
        LN2F0 = max(2 * PC * HALF, PC * HALF + PC * CH2, 4 * PC * SLAB)
        ARB_W = max(PC * C + 3 * PC * CHK,
                    PC * C + 2 * PC * CHK + 2 * HAUG + C,
                    PC * HALF + 2 * PC * KB + 2 * (KB // 128) * HAUG + 8 * KB + 512,
                    LN2F0 + PC * HALF + C,
                    LN2F0 + PC * HALF + (SLAB // 128) * C)
        ARF_W = max(QTH * HAUG + 2 * KB, QTH * C + 2 * C, QTH * C + HALF)
        ARB = es.enter_context(nc.sbuf_tensor("ARB", [128, ARB_W], BF16))
        ARF = es.enter_context(nc.sbuf_tensor("ARF", [128, ARF_W], F32))
        TMPBF = es.enter_context(nc.sbuf_tensor("TMPBF", [128, 2 * C + CHK], BF16))
        SMB = es.enter_context(nc.sbuf_tensor("SMB", [128, 256 + PC * 8 + H + CHK], BF16))
        SMF = es.enter_context(nc.sbuf_tensor("SMF", [128, 256 + QTH * H * 2 + 16 + QTH + cfg.NQT * H], F32))

        ar, af, tb, smb, smf = ARB.ap(), ARF.ap(), TMPBF.ap(), SMB.ap(), SMF.ap()

        ident = smb[:, 0:128]
        ones_bf = smb[:, 128:256]
        router_sb = smb[:, 256:256 + PC * 8]
        kmx2row = smb[0:1, 256 + PC * 8:256 + PC * 8 + H]
        rowtmp = smb[0:1, 256 + PC * 8 + H:256 + PC * 8 + H + CHK]

        qkvb_sb = smf[:, 0:3 * PC]
        qb_scaled = smf[:, 3 * PC:4 * PC]
        bnst = lambda p: smf[:, 64 + (p % 2) * 24: 64 + (p % 2) * 24 + 24]
        aggr = lambda p: smf[:, 112 + (p % 2) * 2: 112 + (p % 2) * 2 + 2]
        sdv = lambda p: smf[:, 116 + (p % 2): 117 + (p % 2)]
        rstd = lambda p: smf[:, 118 + (p % 2): 119 + (p % 2)]
        nmr = lambda p: smf[:, 120 + (p % 2): 121 + (p % 2)]
        nmcol = lambda p: smf[:, 128 + (p % 4): 129 + (p % 4)]
        dcol = lambda p: smf[:, 132 + (p % 4): 133 + (p % 4)]
        acol = lambda p: smf[:, 136 + (p % 4): 137 + (p % 4)]
        lrec = lambda p: smf[:, 140 + (p % 4):141 + (p % 4)]
        rlog = lambda p: smf[:, 144 + (p % 2) * 8: 144 + (p % 2) * 8 + 8]
        rtmp = smf[:, 160:168]
        rst = lambda j: smf[:, 168 + j:169 + j]
        MB0 = 192
        mcol = lambda qt, hd, par: smf[:, MB0 + ((qt % QTH) * H + hd) * 2 + (par % 2):
                                       MB0 + ((qt % QTH) * H + hd) * 2 + (par % 2) + 1]
        CC0 = MB0 + QTH * H * 2
        ccol = lambda qt: smf[:, CC0 + (qt % QTH): CC0 + (qt % QTH) + 1]
        NQ0 = CC0 + QTH
        nq2col = lambda qt, hd: smf[:, NQ0 + qt * H + hd: NQ0 + qt * H + hd + 1]

        XB0 = QTH * C
        xslot = lambda p: af[:, XB0 + (p % 2) * C: XB0 + (p % 2) * C + C]
        lnbf = lambda p: tb[:, (p % 2) * C:(p % 2) * C + C]
        sqA = tb[:, 2 * C:2 * C + CHK]

        semnames = ["T", "V", "S", "G", "SY"] + SYNC_LANES + GP_LANES
        sems = {n: es.enter_context(nc.semaphore(n)) for n in semnames}
        r = Rec()

        def ln_chain(xtile, par, outbf, reads0):
            t = None
            for j in range(NST):
                t = r.op("vector", lambda e, j=j: e.bn_stats(
                    out=bnst(par)[:, 6 * j:6 * j + 6], in_=xtile[:, j * STW:(j + 1) * STW]),
                    reads=(reads0 if j == 0 else []), sem="V" if j == NST - 1 else None)
            t_ag = r.op("vector", lambda e: e.bn_aggr(
                out=aggr(par), in_=bnst(par)[:, 0:6 * NST]), reads=[t], sem="V")
            t_vp = r.op("vector", lambda e: e.tensor_scalar_add(
                out=sdv(par), in0=aggr(par)[:, 1:2], scalar1=float(cfg.eps)),
                reads=[t_ag], sem="V")
            t_sd = r.op("scalar", lambda e: e.activation(
                out=sdv(par), in_=sdv(par), func=AF.Sqrt),
                reads=[t_vp], sem="S")
            t_rs = r.op("vector", lambda e: e.reciprocal(
                out=rstd(par), in_=sdv(par)), reads=[t_sd], sem="V")
            t_nm = r.op("vector", lambda e: e.tensor_scalar(
                out=nmr(par), in0=aggr(par)[:, 0:1], scalar1=rstd(par), scalar2=-1.0,
                op0=ALU.mult, op1=ALU.mult), reads=[t_rs], sem="V")
            t_norm = r.op("scalar", lambda e: e.activation(
                out=outbf, in_=xtile, func=AF.Identity, bias=nmr(par), scale=rstd(par)),
                reads=[t_nm], sem="S")
            return t_norm

        psA = ExitStack()
        pk = [psA.enter_context(nc.psum_tensor(f"pkA{i}", [128, CHK], F32)) for i in range(QD)]
        ptr = [psA.enter_context(nc.psum_tensor(f"ptA{i}", [128, 128], BF16)) for i in range(3)]
        pnA = psA.enter_context(nc.psum_tensor("pnA", [1, CHK], F32)).ap()
        pcol = [ptr[2].ap()[:, 0:1]]

        wk = ar[:, 0:PC * C]
        ln1f = lambda p: ar[:, PC * C + (p % 2) * PC * CHK: PC * C + (p % 2 + 1) * PC * CHK]
        kch = ar[:, PC * C + 2 * PC * CHK: PC * C + 3 * PC * CHK]

        t_ident = t_qb = r.dma_batch("sync", [(ident, identin[:, :]),
                                              (qkvb_sb, qkvb_cols[:, :])])
        t_ones = r.op("vector", lambda e: e.memset(ones_bf, 1.0), sem="V")
        t_qbs = r.op("vector", lambda e: e.tensor_scalar_mul(
            out=qb_scaled, in0=qkvb_sb[:, 0:PC], scalar1=float(cfg.qscale)),
            reads=[t_qb], sem="V")

        def phase_AC(src_x, nch, wsec, bias_cols, scale, dst, lnspill, knorm=False,
                     qnorm=False, end=True):
            hp = max(PC // 2, 1)
            t_w0 = r.dma_batch("gpsimd", [
                (wk[:, p * C:(p + 1) * C],
                 qkvT[p * 128:(p + 1) * 128, wsec * C:(wsec + 1) * C])
                for p in range(hp)])
            t_w1 = r.dma_batch("gpsimd", [
                (wk[:, p * C:(p + 1) * C],
                 qkvT[p * 128:(p + 1) * 128, wsec * C:(wsec + 1) * C])
                for p in range(hp, PC)]) if PC > 1 else t_w0
            guard = {}
            for ch in range(nch):
                lf = ln1f(ch)
                ev = None
                for tl in range(CHK // 128):
                    t0 = ch * CHK + tl * 128
                    xp = (ch * (CHK // 128) + tl) % 2
                    xtok = r.dma("sync", xslot(xp), src_x[t0:t0 + 128, :],
                                 reads=[guard.get(("xfree", xp))])
                    t_norm = ln_chain(xslot(xp), xp, lnbf(xp),
                                      [xtok, guard.get(("lnbf_free", xp))])
                    guard[("xfree", xp)] = t_norm
                    if PH_LVL < 2:
                        guard[("lnbf_free", xp)] = t_norm
                        continue
                    for p in range(PC):
                        pp = p % 3
                        t_tr = r.op("tensor", lambda e, p=p, pp=pp, xp=xp: e.transpose(
                            out=ptr[pp].ap()[:, :], in_=lnbf(xp)[:, p * 128:(p + 1) * 128],
                            identity=ident),
                            reads=[t_norm, t_ident, guard.get(("trev", pp)),
                                   (guard.get("pcolfree") if pp == 2 else None)], sem="T")
                        ev = r.op("vector", lambda e, p=p, pp=pp, tl=tl, lf=lf: e.tensor_copy(
                            out=lf[:, p * CHK + tl * 128: p * CHK + tl * 128 + 128],
                            in_=ptr[pp].ap()[:, :]),
                            reads=[t_tr, guard.get(("lfw", ch % 2))], sem="V")
                        guard[("trev", pp)] = ev
                    guard[("lnbf_free", xp)] = ev
                lf_ready = ev
                mm_last = None
                kevac = None
                if PH_LVL < 3:
                    continue
                for quad in range(NQD):
                    for ci in range(PC):
                        for j in range(QD):
                            od = quad * QD + j
                            mm_last = r.op("tensor", lambda e, ci=ci, od=od, j=j, lf=lf: e.matmul(
                                out=pk[j][:, :],
                                lhsT=wk[:, ci * C + od * 128: ci * C + od * 128 + 128],
                                rhs=lf[:, ci * CHK: ci * CHK + CHK],
                                start=(ci == 0), stop=(ci == PC - 1)),
                                reads=([lf_ready, t_w0, guard.get("kevac_all"), t_qbs]
                                       if (ci == 0 and j == 0) else
                                       ([t_w1] if (ci == hp and j == 0) else [])),
                                sem="T" if (ci == PC - 1 and j == QD - 1) else None)
                    for j in range(QD):
                        od = quad * QD + j
                        kevac = r.op("scalar", lambda e, od=od, j=j: e.activation(
                            out=kch[:, od * CHK:(od + 1) * CHK], in_=pk[j][:, :],
                            func=AF.Identity, bias=bias_cols[:, od:od + 1],
                            scale=float(scale)),
                            reads=[mm_last, guard.get("spill")], sem="S")
                        if knorm:
                            t_sq = r.op("gpsimd", lambda e, od=od: e.tensor_mul(
                                out=sqA, in0=kch[:, od * CHK:(od + 1) * CHK],
                                in1=kch[:, od * CHK:(od + 1) * CHK]),
                                reads=[kevac, guard.get("sqfree")], sem="G")
                            t_pn = r.op("tensor", lambda e: e.matmul(
                                out=pnA[:, :], lhsT=ones_bf[:, 0:1], rhs=sqA,
                                start=True, stop=True),
                                reads=[t_sq, t_ones, guard.get("pnfree")], sem="T")
                            guard["sqfree"] = t_pn
                            t_mx = r.op("vector", lambda e: e.reduce_max(
                                out=rowtmp[0:1, 0:1], in_=pnA[0:1, :], axis=AX.X),
                                reads=[t_pn, guard.get("rowfree")], sem="V")
                            guard["pnfree"] = t_mx
                            if guard.get(("km", od)) is None:
                                t_km = r.op("vector", lambda e, od=od: e.tensor_copy(
                                    out=kmx2row[0:1, od:od + 1], in_=rowtmp[0:1, 0:1]),
                                    reads=[t_mx], sem="V")
                            else:
                                t_km = r.op("vector", lambda e, od=od: e.tensor_tensor(
                                    out=kmx2row[0:1, od:od + 1], in0=kmx2row[0:1, od:od + 1],
                                    in1=rowtmp[0:1, 0:1], op=ALU.max),
                                    reads=[t_mx], sem="V")
                            guard[("km", od)] = t_km
                            guard["rowfree"] = t_km
                            r.kmtok = t_km
                    guard["kevac_all"] = kevac
                if PH_LVL >= 4:
                    sp = r.dma_batch("sync", [
                        (dst[od * 128:(od + 1) * 128, ch * CHK:(ch + 1) * CHK],
                         kch[:, od * CHK:(od + 1) * CHK]) for od in range(PC)],
                        reads=[kevac])
                    guard["spill"] = sp
                if qnorm:
                    for od in range(PC):
                        t_sq = r.op("gpsimd", lambda e, od=od: e.tensor_mul(
                            out=sqA, in0=kch[:, od * CHK:(od + 1) * CHK],
                            in1=kch[:, od * CHK:(od + 1) * CHK]),
                            reads=[kevac, guard.get("sqfree")], sem="G")
                        t_pn = r.op("tensor", lambda e: e.matmul(
                            out=pnA[:, :], lhsT=ones_bf[:, 0:1], rhs=sqA,
                            start=True, stop=True),
                            reads=[t_sq, t_ones, guard.get("pnfree")], sem="T")
                        guard["sqfree"] = t_pn
                        t_rw = r.op("vector", lambda e: e.tensor_copy(
                            out=rowtmp[0:1, 0:CHK], in_=pnA[0:1, :]),
                            reads=[t_pn, guard.get("rowfree")], sem="V")
                        guard["pnfree"] = t_rw
                        t_tc = None
                        for tl in range(CHK // 128):
                            qt_g = (ch * CHK) // 128 + tl
                            t_tc = r.op("tensor", lambda e, tl=tl: e.transpose(
                                out=pcol[0][:, :],
                                in_=rowtmp[0:1, tl * 128:(tl + 1) * 128],
                                identity=ident[0:1, 0:1]),
                                reads=[t_rw, guard.get("pcolfree"),
                                       guard.get(("trev", 2))], sem="T")
                            t_ev = r.op("scalar", lambda e, qt_g=qt_g, od=od, tl=tl: e.copy(
                                out=nq2col(qt_g, od), in_=pcol[0][:, :]),
                                reads=[t_tc], sem="S")
                            guard["pcolfree"] = t_ev
                            guard[("nq2", qt_g, od)] = t_ev
                        guard["rowfree"] = t_tc
                    r.nq2 = {k: v for k, v in guard.items() if isinstance(k, tuple) and k[0] == "nq2"}                         if not hasattr(r, "nq2") else {**r.nq2, **{k: v for k, v in guard.items() if isinstance(k, tuple) and k[0] == "nq2"}}
                if lnspill and PH_LVL >= 4:
                    sp2 = r.dma_batch("sync", [
                        (ln1s[p * 128:(p + 1) * 128, ch * CHK:(ch + 1) * CHK],
                         lf[:, p * CHK:(p + 1) * CHK]) for p in range(PC)],
                        reads=[lf_ready])
                    guard[("lfw", ch % 2)] = [mm_last, sp2]
                else:
                    guard[("lfw", ch % 2)] = [mm_last, lf_ready]
            if end:
                r.phase_end()
            return {"mm": mm_last, "kevac": kevac, "spill": guard.get("spill"),
                    "sq": guard.get("sqfree")}

        atoks = {}
        if lim >= 1:
            atoks = phase_AC(x_seq, T // CHK, 1, qkvb_sb[:, PC:2 * PC], 1.0, Ks,
                             lnspill=True, knorm=True, end=False)

        wv = wk
        VST0 = PC * C + 2 * PC * CHK
        vst = lambda p: ar[:, VST0 + (p % 2) * HAUG: VST0 + (p % 2 + 1) * HAUG]
        bv_row = ar[0:1, VST0 + 2 * HAUG: VST0 + 2 * HAUG + C]

        if lim >= 2:
            hpv = max(PC // 2, 1)
            t_wv0 = r.dma_batch("gpsimd", [
                (wv[:, p * C:(p + 1) * C], qkvT[p * 128:(p + 1) * 128, 2 * C:3 * C])
                for p in range(hpv)], reads=[atoks.get("mm"), atoks.get("sq")])
            t_wv = r.dma_batch("gpsimd", [
                (wv[:, p * C:(p + 1) * C], qkvT[p * 128:(p + 1) * 128, 2 * C:3 * C])
                for p in range(hpv, PC)]) if PC > 1 else t_wv0
            bhead = [atoks.get("mm"), atoks.get("kevac"), atoks.get("spill"),
                     atoks.get("sq")]
            t_bv = r.dma("gpsimd", bv_row, qkvb_flat[None, 2 * C:3 * C], reads=bhead)
            t_vi0 = r.op("vector", lambda e: e.memset(
                vst(0).rearrange("p (h x) -> p h x", x=129)[:, :, 128:129], 1.0),
                reads=bhead, sem="V")
            t_vi1 = r.op("vector", lambda e: e.memset(
                vst(1).rearrange("p (h x) -> p h x", x=129)[:, :, 128:129], 1.0), sem="V")

            gB = {}
            gB[("lfree", 0)] = [atoks.get("mm"), atoks.get("spill")]
            gB[("lfree", 1)] = [atoks.get("mm"), atoks.get("spill")]
            for _vd in range(NVD):
                gB[("pvfree", _vd)] = atoks.get("kevac")
            gB[("vsfree", 0)] = [atoks.get("spill"), atoks.get("sq")]
            gB[("vsfree", 1)] = [atoks.get("spill"), atoks.get("sq")]
            for ch in range(T // CHK):
                lf = ln1f(ch)
                t_lf = r.dma_batch("sync", [
                    (lf[:, p * CHK:(p + 1) * CHK],
                     ln1s[p * 128:(p + 1) * 128, ch * CHK:(ch + 1) * CHK])
                    for p in range(PC)], reads=[gB.get(("lfree", ch % 2))])
                mm_last = None
                for tl in range(CHK // 128):
                    nh = VDW // 128
                    for vd in range(NVD):
                        mm_last = r.op("tensor", lambda e, vd=vd: e.matmul(
                            out=pk[vd][:, 0:VDW], lhsT=ones_bf[0:1, 0:128],
                            rhs=bv_row[0:1, vd * VDW:(vd + 1) * VDW], start=True, stop=False),
                            reads=[t_lf, t_bv, t_ones, t_vi0, t_vi1,
                                   gB.get(("pvfree", vd))], sem=None)
                    for ci in range(PC):
                        for vd in range(NVD):
                            mm_last = r.op("tensor", lambda e, ci=ci, vd=vd, tl=tl, lf=lf: e.matmul(
                                out=pk[vd][:, 0:VDW],
                                lhsT=lf[:, ci * CHK + tl * 128: ci * CHK + tl * 128 + 128],
                                rhs=wv[:, ci * C + vd * VDW: ci * C + vd * VDW + VDW],
                                start=False, stop=(ci == PC - 1)),
                                reads=([t_wv0] if (ci == 0 and vd == 0) else
                                       ([t_wv] if (ci == hpv and vd == 0) else [])),
                                sem="T" if (ci == PC - 1 and vd == NVD - 1) else None)
                    tvev = None
                    for vd in range(NVD):
                        tvev = r.op("vector", lambda e, vd=vd, tl=tl, nh=nh: e.tensor_copy(
                            out=vst(tl).rearrange("p (h x) -> p h x", x=129)[:, vd * nh:(vd + 1) * nh, 0:128],
                            in_=pk[vd][:, 0:VDW].rearrange("p (h x) -> p h x", x=128)),
                            reads=[mm_last, gB.get(("vsfree", tl % 2))], sem="V")
                        gB[("pvfree", vd)] = tvev
                    t0 = ch * CHK + tl * 128
                    gB[("vsfree", tl % 2)] = r.dma("sync", Vs[t0:t0 + 128, :], vst(tl),
                                                   reads=[tvev])
                gB[("lfree", ch % 2)] = mm_last
            r.phase_end()

        if lim >= 3:
            phase_AC(x_own, OWN // CHK, 0, qb_scaled, cfg.qscale, Qs, lnspill=False,
                     qnorm=True)

        psA.close()

        qbuf = ar[:, 0:PC * HALF]
        KB0 = PC * HALF
        kbuf = lambda p: ar[:, KB0 + (p % 2) * PC * KB: KB0 + (p % 2 + 1) * PC * KB]
        VB0 = KB0 + 2 * PC * KB
        vbuf = lambda p: ar[:, VB0 + (p % 2) * (KB // 128) * HAUG:
                            VB0 + (p % 2 + 1) * (KB // 128) * HAUG]
        PB0 = VB0 + 2 * (KB // 128) * HAUG
        pbuf = lambda p: ar[:, PB0 + (p % 4) * KB: PB0 + (p % 4 + 1) * KB]
        PTB0 = PB0 + 4 * KB
        ptb = lambda p: ar[:, PTB0 + (p % 4) * KB: PTB0 + (p % 4 + 1) * KB]
        OFB0 = PTB0 + 4 * KB
        ofbf = lambda p: ar[:, OFB0 + (p % 4) * 128: OFB0 + (p % 4 + 1) * 128]

        obuf = lambda qt: af[:, (qt % QTH) * HAUG: (qt % QTH + 1) * HAUG]
        MKOFF = QTH * HAUG
        mslot = lambda p: af[:, MKOFF + (p % 2) * KB: MKOFF + (p % 2 + 1) * KB]

        WO0 = PC * HALF
        wo = ar[:, WO0: WO0 + PC * CH2]
        ln2f = ar[:, LN2F0: LN2F0 + PC * HALF]
        OBR0 = LN2F0 + PC * HALF
        ob_row = ar[0:1, OBR0: OBR0 + C]
        ybuf = lambda qt: af[:, (qt % QTH) * C: (qt % QTH + 1) * C]

        gsb = lambda p: ar[:, (p % 2) * PC * SLAB: (p % 2 + 1) * PC * SLAB]
        usb = lambda p: ar[:, 2 * PC * SLAB + (p % 2) * PC * SLAB:
                           2 * PC * SLAB + (p % 2 + 1) * PC * SLAB]
        DS0 = LN2F0 + PC * HALF
        dsb = ar[:, DS0: DS0 + (SLAB // 128) * C]
        hsb = tb[:, 0:(SLAB // 128) * HALF]
        silu_t = af[:, XB0: XB0 + HALF]

        t_rt = r.dma_batch("gpsimd", [
            (router_sb[:, p * 8:(p + 1) * 8], routerT[p * 128:(p + 1) * 128, 0:8])
            for p in range(PC)])

        mask_idx = 0
        out_toks = []
        t_ob = None
        for half in range(2):
            if lim < 4 + 3 * half:
                break
            qts = list(range(half * QTH, (half + 1) * QTH))
            maxcov = max(cov_of(cfg, i) for i in qts)

            psD = ExitStack()
            ps_s = [psD.enter_context(nc.psum_tensor(f"psS{half}{i}", [128, KB], F32)) for i in range(3)]
            ptp = [psD.enter_context(nc.psum_tensor(f"ptD{half}{i}", [128, KB], BF16)).ap()[:, :] for i in range(3)]
            pav = [psD.enter_context(nc.psum_tensor(f"pav{half}{i}", [128, 129], F32)).ap()[:, :] for i in range(2)]

            t_qbuf = r.dma_batch("sync", [
                (qbuf[:, p * HALF:(p + 1) * HALF],
                 Qs[p * 128:(p + 1) * 128, half * HALF:(half + 1) * HALF])
                for p in range(PC)])

            bset = {}
            for qt in qts:
                for hd in range(H):
                    bb = (qt % QTH) * H + hd
                    bpp = bb % 2
                    t_bc = r.op("tensor", lambda e, hd=hd, bpp=bpp: e.matmul(
                        out=pav[bpp][:, 0:1], lhsT=ones_bf[0:1, 0:128],
                        rhs=kmx2row[0:1, hd:hd + 1], start=True, stop=True),
                        reads=[bset.get(("pf", bpp)), r.kmtok, t_ones], sem="T")
                    t_m = r.op("vector", lambda e, qt=qt, hd=hd, bpp=bpp: e.tensor_mul(
                        out=mcol(qt, hd, 1), in0=nq2col(qt, hd), in1=pav[bpp][:, 0:1]),
                        reads=[t_bc, r.nq2[("nq2", qt, hd)], bset.get(("mf", qt, hd))], sem="V")
                    bset[("pf", bpp)] = t_m
                    t_sq = r.op("scalar", lambda e, qt=qt, hd=hd: e.activation(
                        out=mcol(qt, hd, 1), in_=mcol(qt, hd, 1), func=AF.Sqrt),
                        reads=[t_m], sem="S")
                    t_bn = r.op("vector", lambda e, qt=qt, hd=hd: e.tensor_scalar_mul(
                        out=mcol(qt, hd, 0), in0=mcol(qt, hd, 1), scalar1=-1.0),
                        reads=[t_sq], sem="V")
                    bset[(qt, hd)] = t_bn

            kv_free, o_tok, s_free, p_free, pt_free, av_free, m_free = {}, {}, {}, {}, {}, {}, {}
            tup = 0
            last_s_mm = None
            kv_done = {}
            pend = []
            pend2 = []
            bkc = [0]
            avc = [0]

            def flush_mid():
                st = pend.pop(0)
                bp = bkc[0] % 3
                ap_ = bkc[0] % 4
                bkc[0] += 1
                pp = st["pp"]
                t_tr = None
                for j in range(KB // 128):
                    t_tr = r.op("tensor", lambda e, j=j, pp=pp, bp=bp: e.transpose(
                        out=ptp[bp][:, j * 128:(j + 1) * 128],
                        in_=pbuf(pp)[:, j * 128:(j + 1) * 128], identity=ident),
                        reads=([st["t_p"], pt_free.get(bp)] if j == 0 else []),
                        sem="T" if j == KB // 128 - 1 else None)
                t_ptc = r.op("vector", lambda e, ap_=ap_, bp=bp: e.tensor_copy(
                    out=ptb(ap_), in_=ptp[bp][:, :]),
                    reads=[t_tr, av_free.get((ap_, "ptb"))], sem="V")
                p_free[pp] = t_ptc
                pt_free[bp] = t_ptc
                st["ap_"] = ap_
                st["t_ptc"] = t_ptc
                pend2.append(st)

            def flush_back():
                st = pend2.pop(0)
                ap_ = st["ap_"]
                vp_ = avc[0] % 2
                avc[0] += 1
                qt, hd, kp = st["qt"], st["hd"], st["kp"]
                t_av = None
                for j in range(KB // 128):
                    t_av = r.op("tensor", lambda e, j=j, ap_=ap_, vp_=vp_, hd=hd, kp=kp: e.matmul(
                        out=pav[vp_][:, :],
                        lhsT=ptb(ap_)[:, j * 128:(j + 1) * 128],
                        rhs=vbuf(kp)[:, j * HAUG + hd * 129: j * HAUG + hd * 129 + 129],
                        start=(j == 0), stop=(j == KB // 128 - 1)),
                        reads=([st["t_ptc"], av_free.get((vp_, "pav"))] if j == 0 else []),
                        sem="T" if j == KB // 128 - 1 else None)
                av_free[(ap_, "ptb")] = t_av
                osl = obuf(qt).rearrange("p (h x) -> p h x", x=129)[:, hd:hd + 1, :].squeeze(1)
                if st["first"]:
                    t_o = r.op("vector", lambda e, osl=osl, vp_=vp_: e.tensor_copy(
                        out=osl, in_=pav[vp_][:, :]),
                        reads=[t_av, o_tok.get((qt, hd))], sem="V")
                else:
                    t_o = r.op("vector", lambda e, osl=osl, vp_=vp_: e.tensor_add(
                        out=osl, in0=osl, in1=pav[vp_][:, :]),
                        reads=[t_av, o_tok.get((qt, hd))], sem="V")
                o_tok[(qt, hd)] = t_o
                o_tok[(qt, hd, "mf")] = t_o
                av_free[(vp_, "pav")] = t_o
                if st["last_of_kb"]:
                    kv_done[st["kb"]] = t_o
                return t_o

            LAG = 2
            LAG2 = 2
            for kb in range(maxcov):
                kp = kb % 2
                kvpairs = [(kbuf(kp)[:, p * KB:(p + 1) * KB],
                            Ks[p * 128:(p + 1) * 128, kb * KB:(kb + 1) * KB])
                           for p in range(PC)]
                kvpairs += [(vbuf(kp)[:, j * HAUG:(j + 1) * HAUG],
                             Vs[kb * KB + j * 128: kb * KB + (j + 1) * 128, :])
                            for j in range(KB // 128)]
                t_kv = r.dma_batch("sync", kvpairs, reads=[kv_done.get(kb - 2)])
                active = [q for q in qts if kb < cov_of(cfg, q)]
                for qn, qt in enumerate(active):
                    qrel = qt % QTH
                    need_mask = open_of(cfg, qt) <= kb
                    t_mask = None
                    mp = mask_idx % 2
                    if need_mask:
                        t_mask = r.dma("sync", mslot(mp), masks[mask_idx, :, :],
                                       reads=[m_free.get(mp)])
                        mask_idx += 1
                    first = (kb == 0)
                    for hd in range(H):
                        pp = tup % 4
                        ps = tup % 3
                        t_smm = r.op("tensor", lambda e, hd=hd, qrel=qrel, kp=kp, ps=ps: e.matmul(
                            out=ps_s[ps][:, :],
                            lhsT=qbuf[:, hd * HALF + qrel * 128: hd * HALF + qrel * 128 + 128],
                            rhs=kbuf(kp)[:, hd * KB:(hd + 1) * KB],
                            start=True, stop=True),
                            reads=[t_qbuf, t_kv, s_free.get(ps)], sem="T")
                        last_s_mm = t_smm
                        t_sm = t_smm
                        if need_mask:
                            t_sm = r.op("vector", lambda e, ps=ps, mp=mp: e.tensor_add(
                                out=ps_s[ps][:, :], in0=ps_s[ps][:, :], in1=mslot(mp)),
                                reads=[t_smm, t_mask], sem="V")
                            m_free[mp] = t_sm
                        t_p = r.op("scalar", lambda e, pp=pp, ps=ps, qt=qt, hd=hd: e.activation(
                            out=pbuf(pp), in_=ps_s[ps][:, :], func=AF.Exp,
                            bias=mcol(qt, hd, 0)),
                            reads=[t_sm, bset[(qt, hd)], p_free.get(pp)], sem="S")
                        s_free[ps] = t_p
                        pend.append({"pp": pp, "qt": qt, "hd": hd, "kp": kp,
                                     "first": first, "t_p": t_p,
                                     "kb": kb,
                                     "last_of_kb": (qn == len(active) - 1 and hd == H - 1)})
                        if len(pend) > LAG:
                            flush_mid()
                        if len(pend2) > LAG2:
                            flush_back()
                        tup += 1
            while pend:
                flush_mid()
            while pend2:
                flush_back()

            fin_ev = {}
            t_fin_last = None
            fpend = []
            fbk = [0]

            def flush_fin():
                stf = fpend.pop(0)
                fp2 = fbk[0] % 3
                fbk[0] += 1
                t_tr = r.op("tensor", lambda e, fp=stf["fp"], fp2=fp2: e.transpose(
                    out=ptp[fp2][:, 0:128], in_=ofbf(fp), identity=ident),
                    reads=[stf["t_of"], fin_ev.get(("ev", fp2)), last_s_mm], sem="T")
                t_ev = r.op("scalar", lambda e, fp2=fp2, hd=stf["hd"], qrel=stf["qrel"]: e.copy(
                    out=qbuf[:, hd * HALF + qrel * 128: hd * HALF + qrel * 128 + 128],
                    in_=ptp[fp2][:, 0:128]),
                    reads=[t_tr], sem="S")
                fin_ev[("ev", fp2)] = t_ev
                fin_ev[("trdone", stf["fp"])] = t_ev
                return t_ev

            fi = 0
            for qt in qts:
                qrel = qt % QTH
                for hd in range(H):
                    osl = obuf(qt).rearrange("p (h x) -> p h x", x=129)[:, hd:hd + 1, :].squeeze(1)
                    fp = fi % 4
                    t_rl = r.op("vector", lambda e, osl=osl, fp=fp: e.reciprocal(
                        out=lrec(fp), in_=osl[:, 128:129]),
                        reads=[o_tok[(qt, hd)], fin_ev.get(("lfree", fp))], sem="V")
                    t_of = r.op("vector", lambda e, osl=osl, fp=fp: e.tensor_scalar_mul(
                        out=ofbf(fp), in0=osl[:, 0:128], scalar1=lrec(fp)),
                        reads=[t_rl, fin_ev.get(("trdone", fp))], sem="V")
                    fin_ev[("lfree", fp)] = t_of
                    fpend.append({"fp": fp, "t_of": t_of, "hd": hd, "qrel": qrel})
                    if len(fpend) > LAG:
                        t_fin_last = flush_fin()
                    fi += 1
            while fpend:
                t_fin_last = flush_fin()
            if dbg:
                r.dma_batch("sync", [
                    (dbg_ot[half, p * 128:(p + 1) * 128, :], qbuf[:, p * HALF:(p + 1) * HALF])
                    for p in range(PC)], reads=[t_fin_last])
            r.phase_end()

            if lim < 5 + 3 * half:
                psD.close()
                break
            y_tok, x_used, xfree, ln2_toks = {}, {}, {}, {}
            mm = None
            for coh in range(2):
                wt = r.dma_batch("gpsimd", [
                    (wo[:, p * CH2:(p + 1) * CH2],
                     outwT[p * 128:(p + 1) * 128, coh * CH2:(coh + 1) * CH2])
                    for p in range(PC)], reads=[y_tok.get("wofree")])
                if coh == 0:
                    t_ob = r.dma("gpsimd", ob_row, outb[None, :])
                unit = 0
                for qt in qts:
                    qrel = qt % QTH
                    xp = qt % 2
                    x_used[(qt, coh)] = r.dma("sync", xslot(xp),
                                              x_own[qt * 128:(qt + 1) * 128, :],
                                              reads=[xfree.get(xp)])
                    for cc in range(NCC):
                        up = unit % 2
                        mm = r.op("tensor", lambda e, coh=coh, cc=cc, up=up: e.matmul(
                            out=ps_s[up][:, 0:CCW], lhsT=ones_bf[0:1, 0:128],
                            rhs=ob_row[0:1, coh * CH2 + cc * CCW: coh * CH2 + cc * CCW + CCW],
                            start=True, stop=False),
                            reads=[wt, t_ob, y_tok.get(("pofree", up)), t_fin_last], sem=None)
                        for ci in range(PC):
                            mm = r.op("tensor", lambda e, ci=ci, cc=cc, coh=coh, qrel=qrel, up=up: e.matmul(
                                out=ps_s[up][:, 0:CCW],
                                lhsT=qbuf[:, ci * HALF + qrel * 128: ci * HALF + qrel * 128 + 128],
                                rhs=wo[:, ci * CH2 + cc * CCW: ci * CH2 + cc * CCW + CCW],
                                start=False, stop=(ci == PC - 1)),
                                sem="T" if ci == PC - 1 else None)
                        t_ev = r.op("vector", lambda e, qt=qt, coh=coh, cc=cc, xp=xp, up=up: e.scalar_tensor_tensor(
                            out=ybuf(qt)[:, coh * CH2 + cc * CCW: coh * CH2 + cc * CCW + CCW],
                            in0=xslot(xp)[:, coh * CH2 + cc * CCW: coh * CH2 + cc * CCW + CCW],
                            scalar=1.0, in1=ps_s[up][:, 0:CCW], op0=ALU.mult, op1=ALU.add),
                            reads=[mm, x_used[(qt, coh)], y_tok.get(("yfree", qt))], sem="V")
                        y_tok[("pofree", up)] = t_ev
                        xfree[xp] = t_ev
                        unit += 1
                    y_tok[(qt, coh)] = t_ev
                y_tok["wofree"] = mm
            t_l2last = None
            for qt in qts:
                qrel = qt % QTH
                xp = qt % 2
                t_norm = ln_chain(ybuf(qt), xp, lnbf(xp),
                                  [y_tok[(qt, 0)], y_tok[(qt, 1)], ln2_toks.get(("lnbf_free", xp))])
                ev = None
                for p in range(PC):
                    pp = p % 2
                    t_tr = r.op("tensor", lambda e, p=p, pp=pp, xp=xp: e.transpose(
                        out=ptp[pp][:, 0:128], in_=lnbf(xp)[:, p * 128:(p + 1) * 128],
                        identity=ident),
                        reads=[t_norm, ln2_toks.get(("ev", pp))], sem="T")
                    ev = r.op("scalar", lambda e, p=p, pp=pp, qrel=qrel: e.copy(
                        out=ln2f[:, p * HALF + qrel * 128: p * HALF + qrel * 128 + 128],
                        in_=ptp[pp][:, 0:128]),
                        reads=[t_tr], sem="S")
                    ln2_toks[("ev", pp)] = ev
                ln2_toks[("lnbf_free", xp)] = ev
                ln2_toks[qt] = ev
                t_l2last = ev
            if dbg:
                for qt in qts:
                    r.dma("sync", dbg_y[qt * 128:(qt + 1) * 128, :], ybuf(qt),
                          reads=[y_tok[(qt, 0)], y_tok[(qt, 1)]])
                r.dma_batch("sync", [
                    (dbg_l2[half, p * 128:(p + 1) * 128, :], ln2f[:, p * HALF:(p + 1) * HALF])
                    for p in range(PC)], reads=[t_l2last])
            r.phase_end()
            psD.close()

            if lim < 6 + 3 * half:
                break
            psF = ExitStack()
            pgF = psF.enter_context(nc.psum_tensor(f"pgF{half}", [128, HALF], F32))
            puF = psF.enter_context(nc.psum_tensor(f"puF{half}", [128, HALF], F32))
            pyF = [psF.enter_context(nc.psum_tensor(f"pyF{half}{i}", [128, YW], F32)) for i in range(2)]

            rt_ev = {}
            for qt in qts:
                qrel = qt % QTH
                up = qt % 2
                mm = None
                for ci in range(PC):
                    mm = r.op("tensor", lambda e, ci=ci, qrel=qrel: e.matmul(
                        out=pgF[:, 0:8],
                        lhsT=ln2f[:, ci * HALF + qrel * 128: ci * HALF + qrel * 128 + 128],
                        rhs=router_sb[:, ci * 8:(ci + 1) * 8],
                        start=(ci == 0), stop=(ci == PC - 1)),
                        reads=([t_rt, t_l2last, rt_ev.get("prfree")] if ci == 0 else []),
                        sem="T" if ci == PC - 1 else None)
                lg = rlog(up)
                t_lg = r.op("vector", lambda e, lg=lg: e.tensor_copy(out=lg, in_=pgF[:, 0:8]),
                            reads=[mm, rt_ev.get(("lgfree", up))], sem="V")
                rt_ev["prfree"] = t_lg
                t_nmx = r.op("vector", lambda e, lg=lg: e.reduce_max(
                    out=rst(0), in_=lg, axis=AX.X, negate=True), reads=[t_lg, rt_ev.get("st")], sem="V")
                t_d1 = r.op("vector", lambda e, lg=lg: e.tensor_scalar_add(
                    out=lg, in0=lg, scalar1=rst(0)), reads=[t_nmx], sem="V")
                t_eq = r.op("vector", lambda e, lg=lg: e.tensor_single_scalar(
                    out=rtmp, in_=lg, scalar=0.0, op=ALU.is_ge), reads=[t_d1], sem="V")
                t_cnt = r.op("vector", lambda e: e.reduce_sum(
                    out=rst(1), in_=rtmp, axis=AX.X), reads=[t_eq], sem="V")
                t_msk = r.op("vector", lambda e, lg=lg: e.scalar_tensor_tensor(
                    out=rtmp, in0=rtmp, scalar=NEG, in1=lg, op0=ALU.mult, op1=ALU.add),
                    reads=[t_eq, t_cnt], sem="V")
                t_n2 = r.op("vector", lambda e: e.reduce_max(
                    out=rst(2), in_=rtmp, axis=AX.X, negate=True), reads=[t_msk], sem="V")
                t_tie = r.op("vector", lambda e: e.tensor_single_scalar(
                    out=rst(3), in_=rst(1), scalar=2.0, op=ALU.is_ge), reads=[t_cnt], sem="V")
                t_de = r.op("vector", lambda e: e.tensor_scalar(
                    out=rst(4), in0=rst(3), scalar1=-1.0, scalar2=1.0,
                    op0=ALU.mult, op1=ALU.add), reads=[t_tie], sem="V")
                t_de2 = r.op("vector", lambda e: e.tensor_mul(
                    out=rst(5), in0=rst(2), in1=rst(4)), reads=[t_n2, t_de], sem="V")
                t_u2 = r.op("scalar", lambda e: e.activation(
                    out=rst(6), in_=rst(5), func=AF.Exp, scale=-1.0), reads=[t_de2], sem="S")
                t_ss = r.op("vector", lambda e: e.tensor_scalar_add(
                    out=rst(7), in0=rst(6), scalar1=1.0), reads=[t_u2], sem="V")
                t_rc = r.op("vector", lambda e: e.reciprocal(
                    out=rst(8), in_=rst(7)), reads=[t_ss], sem="V")
                t_c = r.op("vector", lambda e, qt=qt: e.tensor_mul(
                    out=ccol(qt), in0=rst(7), in1=rst(8)), reads=[t_rc], sem="V")
                rt_ev[qt] = t_c
                rt_ev[("lgfree", up)] = t_d1
                rt_ev["st"] = t_c

            ff = {}
            ff["pgfree"] = rt_ev["prfree"]
            slabs = [("r", js) for js in range(cfg.RINTER // SLAB)] + \
                    [("s", js) for js in range(cfg.INTER // SLAB)]
            for sn, (stype, js) in enumerate(slabs):
                gsrc, usrc, dsrc = (rgT, ruT, rdT) if stype == "r" else (gT, uT, dT)
                sp = sn % 2
                tg = r.dma_batch("gpsimd", [
                    (gsb(sp)[:, p * SLAB:(p + 1) * SLAB],
                     gsrc[p * 128:(p + 1) * 128, js * SLAB:(js + 1) * SLAB])
                    for p in range(PC)], reads=[ff.get(("gfree", sp))])
                tu = r.dma_batch("gpsimd", [
                    (usb(sp)[:, p * SLAB:(p + 1) * SLAB],
                     usrc[p * 128:(p + 1) * 128, js * SLAB:(js + 1) * SLAB])
                    for p in range(PC)], reads=[ff.get(("ufree", sp))])
                td = r.dma_batch("gpsimd", [
                    (dsb[:, j * C:(j + 1) * C],
                     dsrc[js * SLAB + j * 128: js * SLAB + (j + 1) * 128, :])
                    for j in range(SLAB // 128)], reads=[ff.get("dfree")])
                dn_last = None
                for j in range(SLAB // 128):
                    mmg = None
                    for tk in range(NTK):
                        for ci in range(PC):
                            mmg = r.op("tensor", lambda e, ci=ci, j=j, tk=tk, sp=sp: e.matmul(
                                out=pgF[:, tk * TW:(tk + 1) * TW],
                                lhsT=gsb(sp)[:, ci * SLAB + j * 128: ci * SLAB + j * 128 + 128],
                                rhs=ln2f[:, ci * HALF + tk * TW: ci * HALF + tk * TW + TW],
                                start=(ci == 0), stop=(ci == PC - 1)),
                                reads=([tg, ff.get("pgfree")] if ci == 0 and tk == 0 else []),
                                sem="T" if (ci == PC - 1 and tk == NTK - 1) else None)
                    mmu = None
                    for tk in range(NTK):
                        for ci in range(PC):
                            mmu = r.op("tensor", lambda e, ci=ci, j=j, tk=tk, sp=sp: e.matmul(
                                out=puF[:, tk * TW:(tk + 1) * TW],
                                lhsT=usb(sp)[:, ci * SLAB + j * 128: ci * SLAB + j * 128 + 128],
                                rhs=ln2f[:, ci * HALF + tk * TW: ci * HALF + tk * TW + TW],
                                start=(ci == 0), stop=(ci == PC - 1)),
                                reads=([tu, ff.get("pufree")] if ci == 0 and tk == 0 else []),
                                sem="T" if (ci == PC - 1 and tk == NTK - 1) else None)
                    t_si = r.op("scalar", lambda e: e.activation(
                        out=silu_t, in_=pgF[:, :], func=AF.Silu),
                        reads=[mmg, ff.get("silufree")], sem="S")
                    ff["pgfree"] = t_si
                    t_h = r.op("vector", lambda e, j=j: e.tensor_tensor(
                        out=hsb[:, j * HALF:(j + 1) * HALF], in0=silu_t, in1=puF[:, :],
                        op=ALU.mult),
                        reads=[t_si, mmu, ff.get(("hfree", j))], sem="V")
                    ff["pufree"] = t_h
                    ff["silufree"] = t_h
                    ff[("h", j)] = t_h
                unit = 0
                for qt in qts:
                    qrel = qt % QTH
                    for yc in range(NYC):
                        up = unit % 2
                        mm = None
                        ncc2 = max(YW // 512, 1)
                        ccw2 = min(512, YW)
                        for j in range(SLAB // 128):
                            for cc in range(ncc2):
                                mm = r.op("tensor", lambda e, j=j, cc=cc, yc=yc, qrel=qrel, up=up, ccw2=ccw2: e.matmul(
                                    out=pyF[up][:, cc * ccw2:(cc + 1) * ccw2],
                                    lhsT=hsb[:, j * HALF + qrel * 128: j * HALF + qrel * 128 + 128],
                                    rhs=dsb[:, j * C + yc * YW + cc * ccw2: j * C + yc * YW + cc * ccw2 + ccw2],
                                    start=(j == 0), stop=(j == SLAB // 128 - 1)),
                                    reads=([td, ff.get(("pyfree", up)), ff[("h", j)]]
                                           if cc == 0 else [ff[("h", j)]]),
                                    sem="T" if (j == SLAB // 128 - 1 and cc == ncc2 - 1) else None)
                        dn_last = mm
                        if stype == "r":
                            t_ev = r.op("vector", lambda e, qt=qt, yc=yc, up=up: e.scalar_tensor_tensor(
                                out=ybuf(qt)[:, yc * YW:(yc + 1) * YW],
                                in0=pyF[up][:, :], scalar=ccol(qt),
                                in1=ybuf(qt)[:, yc * YW:(yc + 1) * YW],
                                op0=ALU.mult, op1=ALU.add),
                                reads=[mm, rt_ev[qt], ff.get(("ylast", qt, yc)),
                                       ln2_toks[qt]], sem="V")
                        else:
                            t_ev = r.op("vector", lambda e, qt=qt, yc=yc, up=up: e.tensor_add(
                                out=ybuf(qt)[:, yc * YW:(yc + 1) * YW],
                                in0=ybuf(qt)[:, yc * YW:(yc + 1) * YW],
                                in1=pyF[up][:, :]),
                                reads=[mm, ff.get(("ylast", qt, yc)), ln2_toks[qt]], sem="V")
                        ff[("pyfree", up)] = t_ev
                        ff[("ylast", qt, yc)] = t_ev
                        unit += 1
                for j in range(SLAB // 128):
                    ff[("hfree", j)] = dn_last
                ff[("gfree", sp)] = dn_last
                ff[("ufree", sp)] = dn_last
                ff["dfree"] = dn_last
            for qt in qts:
                reads = [ff[("ylast", qt, yc)] for yc in range(NYC)]
                tko = r.dma("sync", out_own[qt * 128:(qt + 1) * 128, :], ybuf(qt),
                            reads=reads)
                out_toks.append(tko)
            r.phase_end()
            psF.close()

        if out_toks:
            r.op("sync", lambda e: e.nop(), reads=out_toks)
        else:
            r.op("sync", lambda e: e.nop(), reads=r.phase_end())

        blk = es.enter_context(nc.Block())
        r.emit(blk, sems)

    return nc



def _host_inputs(cfg, x, qkv_w, qkv_b, out_w, out_b, router_w,
                 sh_gate, sh_up, sh_down, r_gate, r_up, r_down):
    C, T, OWN, A = cfg.C, cfg.T, cfg.OWN, cfg.A
    import ml_dtypes
    shared = {
        "qkvT": np.ascontiguousarray(np.asarray(qkv_w, np.float32).T),
        "qkvb_cols": np.ascontiguousarray(np.asarray(qkv_b, np.float32).reshape(3 * C // 128, 128).T),
        "qkvb_flat": np.ascontiguousarray(np.asarray(qkv_b, np.float32)),
        "outwT": np.ascontiguousarray(np.asarray(out_w, np.float32).T),
        "outb": np.ascontiguousarray(np.asarray(out_b, np.float32)),
        "gT": np.ascontiguousarray(np.asarray(sh_gate, np.float32).T),
        "uT": np.ascontiguousarray(np.asarray(sh_up, np.float32).T),
        "dT": np.ascontiguousarray(np.asarray(sh_down, np.float32).T),
        "rgT": np.ascontiguousarray(np.asarray(r_gate, np.float32).T),
        "ruT": np.ascontiguousarray(np.asarray(r_up, np.float32).T),
        "rdT": np.ascontiguousarray(np.asarray(r_down, np.float32).T),
        "routerT": np.ascontiguousarray(np.asarray(router_w, np.float32).T),
        "identin": np.eye(128, dtype=ml_dtypes.bfloat16),
    }
    masks_by_cls = [build_masks(cfg, 0), build_masks(cfg, 1)]
    B = x.shape[0]
    in_maps = []
    for c in range(2 * B):
        b, cls = c // 2, c % 2
        xs = np.ascontiguousarray(x[b])
        xo = np.ascontiguousarray(x[b].reshape(T // 128, 128, C)[cls::2].reshape(OWN, C))
        in_maps.append({"x_seq": xs, "x_own": xo, "masks": masks_by_cls[cls], **shared})
    return in_maps


def _scatter(cfg, outs, B):
    T, OWN, C = cfg.T, cfg.OWN, cfg.C
    full = np.empty((B, T, C), np.float32)
    for c, o in enumerate(outs):
        b, cls = c // 2, c % 2
        full[b].reshape(T // 128, 128, C)[cls::2] = o.reshape(OWN // 128, 128, C)
    return full


_NC_CACHE = {}


def kernel(L0_attn_x, L0_attn_qkv_w, L0_attn_qkv_b, L0_attn_out_w, L0_attn_out_b,
           L0_ffn_router_weight, L0_ffn_sh_gate, L0_ffn_sh_up, L0_ffn_sh_down,
           L0_ffn_r_gate, L0_ffn_r_up, L0_ffn_r_down):
    cfg = FULL
    x = np.asarray(L0_attn_x, np.float32)
    in_maps = _host_inputs(cfg, x, L0_attn_qkv_w, L0_attn_qkv_b, L0_attn_out_w,
                           L0_attn_out_b, L0_ffn_router_weight, L0_ffn_sh_gate,
                           L0_ffn_sh_up, L0_ffn_sh_down, L0_ffn_r_gate,
                           L0_ffn_r_up, L0_ffn_r_down)
    if "nc" not in _NC_CACHE:
        _NC_CACHE["nc"] = build(cfg)
    nc = _NC_CACHE["nc"]
    res = run_bass_kernel_spmd(nc, in_maps, core_ids=list(range(8)))
    outs = [r["out_own"] for r in res.results]
    return _scatter(cfg, outs, x.shape[0])


# version 24
# speedup vs baseline: 1.2080x; 1.0690x over previous
import os
import numpy as np
from contextlib import ExitStack

import concourse.bass as bass
import concourse.mybir as mybir
from concourse.bass_utils import run_bass_kernel_spmd

F32 = mybir.dt.float32
BF16 = mybir.dt.bfloat16
AF = mybir.ActivationFunctionType
ALU = mybir.AluOpType
AX = mybir.AxisListType

NEG = -1.0e30


class Cfg:
    def __init__(self, C, T, INTER, RINTER):
        self.C, self.T, self.INTER, self.RINTER = C, T, INTER, RINTER
        self.PC = C // 128
        self.H = self.PC
        self.HAUG = self.H * 129
        self.OWN = T // 2
        self.A = self.OWN // 2
        self.HALF = self.OWN // 2
        self.NQT = self.OWN // 128
        self.QTH = self.NQT // 2
        self.KB = 512
        self.NKB = T // self.KB
        self.CHK = 512
        self.SLAB = 512
        self.CH2 = C // 2
        self.eps = 1e-5
        self.qscale = 1.0 / np.sqrt(128.0)


FULL = Cfg(C=2048, T=4096, INTER=8192, RINTER=1024)
MINI = Cfg(C=256, T=1024, INTER=1024, RINTER=512)


def qpos_of(cfg, cls, i):
    return 256 * i + 128 * cls


def cov_of(cfg, i):
    mx = max(qpos_of(cfg, 0, i), qpos_of(cfg, 1, i))
    return (mx + 128 + cfg.KB - 1) // cfg.KB


def open_of(cfg, i):
    mn = min(qpos_of(cfg, 0, i), qpos_of(cfg, 1, i))
    return mn // cfg.KB


def mask_pairs(cfg):
    out = []
    for half in range(2):
        qts = range(half * cfg.QTH, (half + 1) * cfg.QTH)
        maxcov = max(cov_of(cfg, i) for i in qts)
        for kb in range(maxcov):
            for i in qts:
                if open_of(cfg, i) <= kb < cov_of(cfg, i):
                    out.append((half, kb, i))
    return out


def build_masks(cfg, cls):
    pairs = mask_pairs(cfg)
    m = np.zeros((len(pairs), 128, cfg.KB), np.float32)
    kr = np.arange(cfg.KB)
    qr = np.arange(128)
    for n, (_h, kb, i) in enumerate(pairs):
        qg = qpos_of(cfg, cls, i) + qr[:, None]
        kg = kb * cfg.KB + kr[None, :]
        m[n] = np.where(kg <= qg, 0.0, NEG).astype(np.float32)
    return m



ENGS = ("sync", "tensor", "vector", "scalar", "gpsimd")
_ESEM = {"sync": "SY", "gpsimd": "G", "scalar": "S", "vector": "V", "tensor": "T"}


def _flat(reads):
    out = []
    for x in reads:
        if x is None:
            continue
        if isinstance(x, (list, tuple)) and x and isinstance(x[0], (list, tuple)):
            out.extend(_flat(x))
        elif isinstance(x, (list, tuple)) and len(x) == 2 and isinstance(x[0], str):
            out.append(x)
        elif isinstance(x, (list, tuple)):
            out.extend(_flat(x))
    return out


NSYNC_LANES = 6
NGP_LANES = 4
SYNC_LANES = [f"DS{i}" for i in range(NSYNC_LANES)]
GP_LANES = [f"DG{i}" for i in range(NGP_LANES)]


class Rec:

    def __init__(self):
        self.items = {e: [] for e in ENGS}
        self.cnt = {}
        self.pending = {e: [] for e in ENGS}
        self.lane_rr = {"sync": 0, "gpsimd": 0}

    def op(self, eng, fn, reads=(), sem=None, inc=1):
        reads = _flat(list(reads))
        if self.pending[eng]:
            reads = self.pending[eng] + reads
            self.pending[eng] = []
        tok = None
        if sem is not None:
            self.cnt[sem] = self.cnt.get(sem, 0) + inc
            tok = (sem, self.cnt[sem])
        self.items[eng].append({"reads": reads, "fn": fn, "sem": sem,
                                "inc": inc if sem else 0})
        return tok

    def dma_batch(self, eng, pairs, reads=(), sem=None):
        if sem is None:
            lanes = SYNC_LANES if eng == "sync" else GP_LANES
            i = self.lane_rr[eng]
            sem = lanes[i % len(lanes)]
            self.lane_rr[eng] = i + 1
        gate = (sem, self.cnt.get(sem, 0))
        tok = None
        for n, (out, in_) in enumerate(pairs):
            tok = self.op(eng, (lambda o, i_: lambda e: e.dma_start(out=o, in_=i_))(out, in_),
                          reads=(list(reads) + [gate]) if n == 0 else [],
                          sem=sem, inc=16)
        return tok

    def dma(self, eng, out, in_, reads=(), sem=None):
        return self.dma_batch(eng, [(out, in_)], reads=reads, sem=sem)

    def phase_end(self):
        toks = []
        for e in ENGS:
            if not self.items[e]:
                continue
            it = self.items[e][-1]
            if it["sem"] is None:
                it["sem"] = _ESEM[e]
                it["inc"] = 1
                self.cnt[it["sem"]] = self.cnt.get(it["sem"], 0) + it["inc"]
            toks.append((it["sem"], self.cnt[it["sem"]]))
        for lane in SYNC_LANES + GP_LANES:
            if self.cnt.get(lane, 0):
                toks.append((lane, self.cnt[lane]))
        toks = list({s: (s, v) for s, v in toks}.values())
        for e in ENGS:
            self.pending[e] = list(toks)
        return toks

    def emit(self, blk, sems):
        for ename in ENGS:
            stream = self.items[ename]
            if not stream:
                continue

            def runner(e, stream=stream):
                waited = {}
                for it in stream:
                    for s, v in it["reads"]:
                        if waited.get(s, 0) < v:
                            e.wait_ge(sems[s], v)
                            waited[s] = v
                    ins = it["fn"](e)
                    if it["sem"] is not None:
                        ins.then_inc(sems[it["sem"]], it["inc"])
            getattr(blk, ename)(runner)



def build(cfg, dbg=False, lim=99):
    PH_LVL = int(os.environ.get("PH_LVL", "99"))
    nc = bass.Bass("TRN2", target_bir_lowering=False)
    C, T, PC, H = cfg.C, cfg.T, cfg.PC, cfg.H
    OWN, HALF, KB, CHK, SLAB = cfg.OWN, cfg.HALF, cfg.KB, cfg.CHK, cfg.SLAB
    HAUG, CH2 = cfg.HAUG, cfg.CH2
    QTH = cfg.QTH
    assert CHK == KB
    assert cfg.INTER % SLAB == 0 and cfg.RINTER % SLAB == 0
    TW = min(512, HALF)
    NTK = HALF // TW
    YW = min(1024, C)
    NYC = C // YW
    QD = min(4, PC)
    NQD = PC // QD
    VDW = min(512, C)
    NVD = C // VDW
    CCW = min(512, CH2)
    NCC = CH2 // CCW
    NST = max(C // 512, 1)
    STW = min(512, C)

    pairs = mask_pairs(cfg)
    NMASK = len(pairs)

    class _Lazy:
        def __init__(self, name, shape, dtype):
            self.args = (name, shape, dtype)
            self._ap = None

        def __getitem__(self, key):
            if self._ap is None:
                self._ap = nc.dram_tensor(self.args[0], self.args[1], self.args[2],
                                          kind="ExternalInput").ap()
            return self._ap[key]

    x_seq = _Lazy("x_seq", [T, C], F32)
    x_own = _Lazy("x_own", [OWN, C], F32)
    qkvT = _Lazy("qkvT", [C, 3 * C], F32)
    qkvb_cols = _Lazy("qkvb_cols", [128, 3 * PC], F32)
    qkvb_flat = _Lazy("qkvb_flat", [3 * C], F32)
    outwT = _Lazy("outwT", [C, C], F32)
    outb = _Lazy("outb", [C], F32)
    gT = _Lazy("gT", [C, cfg.INTER], F32)
    uT = _Lazy("uT", [C, cfg.INTER], F32)
    dT = _Lazy("dT", [cfg.INTER, C], F32)
    rgT = _Lazy("rgT", [C, cfg.RINTER], F32)
    ruT = _Lazy("ruT", [C, cfg.RINTER], F32)
    rdT = _Lazy("rdT", [cfg.RINTER, C], F32)
    routerT = _Lazy("routerT", [C, 8], F32)
    masks = _Lazy("masks", [max(NMASK, 1), 128, KB], F32)
    identin = _Lazy("identin", [128, 128], BF16)
    out_own = nc.dram_tensor("out_own", [OWN, C], F32, kind="ExternalOutput").ap()

    skind = "ExternalOutput" if dbg else "Internal"
    ln1s = nc.dram_tensor("ln1s", [C, T], BF16, kind=skind).ap()
    Ks = nc.dram_tensor("Ks", [C, T], BF16, kind=skind).ap()
    Vs = nc.dram_tensor("Vs", [T, HAUG], BF16, kind=skind).ap()
    Qs = nc.dram_tensor("Qs", [C, OWN], BF16, kind=skind).ap()
    dbg_y = nc.dram_tensor("dbg_y", [OWN, C], F32, kind="ExternalOutput").ap() if dbg else None
    dbg_ot = nc.dram_tensor("dbg_ot", [2, C, HALF], BF16, kind="ExternalOutput").ap() if dbg else None
    dbg_l2 = nc.dram_tensor("dbg_l2", [2, C, HALF], BF16, kind="ExternalOutput").ap() if dbg else None

    es = ExitStack()
    with es:
        LN2F0 = max(2 * PC * HALF, PC * HALF + PC * CH2, 4 * PC * SLAB)
        ARB_W = max(PC * C + 3 * PC * CHK,
                    PC * C + 2 * PC * CHK + 2 * HAUG + C,
                    PC * HALF + 2 * PC * KB + 2 * (KB // 128) * HAUG + 8 * KB + 512,
                    LN2F0 + PC * HALF + C,
                    LN2F0 + PC * HALF + (SLAB // 128) * C)
        ARF_W = max(QTH * HAUG + 2 * KB, QTH * C + 2 * C, QTH * C + HALF)
        ARB = es.enter_context(nc.sbuf_tensor("ARB", [128, ARB_W], BF16))
        ARF = es.enter_context(nc.sbuf_tensor("ARF", [128, ARF_W], F32))
        TMPBF = es.enter_context(nc.sbuf_tensor("TMPBF", [128, 2 * C + CHK], BF16))
        SMB = es.enter_context(nc.sbuf_tensor("SMB", [128, 256 + PC * 8 + H + CHK], BF16))
        SMF = es.enter_context(nc.sbuf_tensor("SMF", [128, 256 + QTH * H * 2 + 16 + QTH + cfg.NQT * H], F32))

        ar, af, tb, smb, smf = ARB.ap(), ARF.ap(), TMPBF.ap(), SMB.ap(), SMF.ap()

        ident = smb[:, 0:128]
        ones_bf = smb[:, 128:256]
        router_sb = smb[:, 256:256 + PC * 8]
        kmx2row = smb[0:1, 256 + PC * 8:256 + PC * 8 + H]
        rowtmp = smb[0:1, 256 + PC * 8 + H:256 + PC * 8 + H + CHK]

        qkvb_sb = smf[:, 0:3 * PC]
        qb_scaled = smf[:, 3 * PC:4 * PC]
        bnst = lambda p: smf[:, 64 + (p % 2) * 24: 64 + (p % 2) * 24 + 24]
        aggr = lambda p: smf[:, 112 + (p % 2) * 2: 112 + (p % 2) * 2 + 2]
        sdv = lambda p: smf[:, 116 + (p % 2): 117 + (p % 2)]
        rstd = lambda p: smf[:, 118 + (p % 2): 119 + (p % 2)]
        nmr = lambda p: smf[:, 120 + (p % 2): 121 + (p % 2)]
        nmcol = lambda p: smf[:, 128 + (p % 4): 129 + (p % 4)]
        dcol = lambda p: smf[:, 132 + (p % 4): 133 + (p % 4)]
        acol = lambda p: smf[:, 136 + (p % 4): 137 + (p % 4)]
        lrec = lambda p: smf[:, 140 + (p % 4):141 + (p % 4)]
        rlog = lambda p: smf[:, 144 + (p % 2) * 8: 144 + (p % 2) * 8 + 8]
        rtmp = smf[:, 160:168]
        rst = lambda j: smf[:, 168 + j:169 + j]
        MB0 = 192
        mcol = lambda qt, hd, par: smf[:, MB0 + ((qt % QTH) * H + hd) * 2 + (par % 2):
                                       MB0 + ((qt % QTH) * H + hd) * 2 + (par % 2) + 1]
        CC0 = MB0 + QTH * H * 2
        ccol = lambda qt: smf[:, CC0 + (qt % QTH): CC0 + (qt % QTH) + 1]
        NQ0 = CC0 + QTH
        nq2col = lambda qt, hd: smf[:, NQ0 + qt * H + hd: NQ0 + qt * H + hd + 1]

        XB0 = QTH * C
        xslot = lambda p: af[:, XB0 + (p % 2) * C: XB0 + (p % 2) * C + C]
        lnbf = lambda p: tb[:, (p % 2) * C:(p % 2) * C + C]
        sqA = tb[:, 2 * C:2 * C + CHK]

        semnames = ["T", "V", "S", "G", "SY"] + SYNC_LANES + GP_LANES
        sems = {n: es.enter_context(nc.semaphore(n)) for n in semnames}
        r = Rec()

        def ln_chain(xtile, par, outbf, reads0):
            t = None
            for j in range(NST):
                t = r.op("vector", lambda e, j=j: e.bn_stats(
                    out=bnst(par)[:, 6 * j:6 * j + 6], in_=xtile[:, j * STW:(j + 1) * STW]),
                    reads=(reads0 if j == 0 else []), sem="V" if j == NST - 1 else None)
            t_ag = r.op("vector", lambda e: e.bn_aggr(
                out=aggr(par), in_=bnst(par)[:, 0:6 * NST]), reads=[t], sem="V")
            t_vp = r.op("vector", lambda e: e.tensor_scalar_add(
                out=sdv(par), in0=aggr(par)[:, 1:2], scalar1=float(cfg.eps)),
                reads=[t_ag], sem="V")
            t_sd = r.op("scalar", lambda e: e.activation(
                out=sdv(par), in_=sdv(par), func=AF.Sqrt),
                reads=[t_vp], sem="S")
            t_rs = r.op("vector", lambda e: e.reciprocal(
                out=rstd(par), in_=sdv(par)), reads=[t_sd], sem="V")
            t_nm = r.op("vector", lambda e: e.tensor_scalar(
                out=nmr(par), in0=aggr(par)[:, 0:1], scalar1=rstd(par), scalar2=-1.0,
                op0=ALU.mult, op1=ALU.mult), reads=[t_rs], sem="V")
            t_norm = r.op("scalar", lambda e: e.activation(
                out=outbf, in_=xtile, func=AF.Identity, bias=nmr(par), scale=rstd(par)),
                reads=[t_nm], sem="S")
            return t_norm

        psA = ExitStack()
        pk = [psA.enter_context(nc.psum_tensor(f"pkA{i}", [128, CHK], F32)) for i in range(QD)]
        ptr = [psA.enter_context(nc.psum_tensor(f"ptA{i}", [128, 128], BF16)) for i in range(3)]
        pnA = psA.enter_context(nc.psum_tensor("pnA", [1, CHK], F32)).ap()
        pcol = [ptr[2].ap()[:, 0:1]]

        wk = ar[:, 0:PC * C]
        ln1f = lambda p: ar[:, PC * C + (p % 2) * PC * CHK: PC * C + (p % 2 + 1) * PC * CHK]
        kch = ar[:, PC * C + 2 * PC * CHK: PC * C + 3 * PC * CHK]

        t_ident = t_qb = r.dma_batch("sync", [(ident, identin[:, :]),
                                              (qkvb_sb, qkvb_cols[:, :])])
        t_ones = r.op("vector", lambda e: e.memset(ones_bf, 1.0), sem="V")
        t_qbs = r.op("vector", lambda e: e.tensor_scalar_mul(
            out=qb_scaled, in0=qkvb_sb[:, 0:PC], scalar1=float(cfg.qscale)),
            reads=[t_qb], sem="V")

        def phase_AC(src_x, nch, wsec, bias_cols, scale, dst, lnspill, knorm=False,
                     qnorm=False, end=True):
            hp = max(PC // 2, 1)
            t_w0 = r.dma_batch("gpsimd", [
                (wk[:, p * C:(p + 1) * C],
                 qkvT[p * 128:(p + 1) * 128, wsec * C:(wsec + 1) * C])
                for p in range(hp)])
            t_w1 = r.dma_batch("gpsimd", [
                (wk[:, p * C:(p + 1) * C],
                 qkvT[p * 128:(p + 1) * 128, wsec * C:(wsec + 1) * C])
                for p in range(hp, PC)]) if PC > 1 else t_w0
            guard = {}
            for ch in range(nch):
                lf = ln1f(ch)
                ev = None
                for tl in range(CHK // 128):
                    t0 = ch * CHK + tl * 128
                    xp = (ch * (CHK // 128) + tl) % 2
                    xtok = r.dma("sync", xslot(xp), src_x[t0:t0 + 128, :],
                                 reads=[guard.get(("xfree", xp))])
                    t_norm = ln_chain(xslot(xp), xp, lnbf(xp),
                                      [xtok, guard.get(("lnbf_free", xp))])
                    guard[("xfree", xp)] = t_norm
                    if PH_LVL < 2:
                        guard[("lnbf_free", xp)] = t_norm
                        continue
                    for p in range(PC):
                        pp = p % 3
                        t_tr = r.op("tensor", lambda e, p=p, pp=pp, xp=xp: e.transpose(
                            out=ptr[pp].ap()[:, :], in_=lnbf(xp)[:, p * 128:(p + 1) * 128],
                            identity=ident),
                            reads=[t_norm, t_ident, guard.get(("trev", pp)),
                                   (guard.get("pcolfree") if pp == 2 else None)], sem="T")
                        ev = r.op("vector", lambda e, p=p, pp=pp, tl=tl, lf=lf: e.tensor_copy(
                            out=lf[:, p * CHK + tl * 128: p * CHK + tl * 128 + 128],
                            in_=ptr[pp].ap()[:, :]),
                            reads=[t_tr, guard.get(("lfw", ch % 2))], sem="V")
                        guard[("trev", pp)] = ev
                    guard[("lnbf_free", xp)] = ev
                lf_ready = ev
                mm_last = None
                kevac = None
                if PH_LVL < 3:
                    continue
                for quad in range(NQD):
                    for ci in range(PC):
                        for j in range(QD):
                            od = quad * QD + j
                            mm_last = r.op("tensor", lambda e, ci=ci, od=od, j=j, lf=lf: e.matmul(
                                out=pk[j][:, :],
                                lhsT=wk[:, ci * C + od * 128: ci * C + od * 128 + 128],
                                rhs=lf[:, ci * CHK: ci * CHK + CHK],
                                start=(ci == 0), stop=(ci == PC - 1)),
                                reads=([lf_ready, t_w0, guard.get("kevac_all"), t_qbs]
                                       if (ci == 0 and j == 0) else
                                       ([t_w1] if (ci == hp and j == 0) else [])),
                                sem="T" if (ci == PC - 1 and j == QD - 1) else None)
                    for j in range(QD):
                        od = quad * QD + j
                        kevac = r.op("scalar", lambda e, od=od, j=j: e.activation(
                            out=kch[:, od * CHK:(od + 1) * CHK], in_=pk[j][:, :],
                            func=AF.Identity, bias=bias_cols[:, od:od + 1],
                            scale=float(scale)),
                            reads=[mm_last, guard.get("spill")], sem="S")
                        if knorm:
                            t_sq = r.op("scalar", lambda e, od=od: e.activation(
                                out=sqA, in_=kch[:, od * CHK:(od + 1) * CHK],
                                func=AF.Square),
                                reads=[kevac, guard.get("sqfree")], sem="S")
                            t_pn = r.op("tensor", lambda e: e.matmul(
                                out=pnA[:, :], lhsT=ones_bf[:, 0:1], rhs=sqA,
                                start=True, stop=True),
                                reads=[t_sq, t_ones, guard.get("pnfree")], sem="T")
                            guard["sqfree"] = t_pn
                            t_mx = r.op("vector", lambda e: e.reduce_max(
                                out=rowtmp[0:1, 0:1], in_=pnA[0:1, :], axis=AX.X),
                                reads=[t_pn, guard.get("rowfree")], sem="V")
                            guard["pnfree"] = t_mx
                            if guard.get(("km", od)) is None:
                                t_km = r.op("vector", lambda e, od=od: e.tensor_copy(
                                    out=kmx2row[0:1, od:od + 1], in_=rowtmp[0:1, 0:1]),
                                    reads=[t_mx], sem="V")
                            else:
                                t_km = r.op("vector", lambda e, od=od: e.tensor_tensor(
                                    out=kmx2row[0:1, od:od + 1], in0=kmx2row[0:1, od:od + 1],
                                    in1=rowtmp[0:1, 0:1], op=ALU.max),
                                    reads=[t_mx], sem="V")
                            guard[("km", od)] = t_km
                            guard["rowfree"] = t_km
                            r.kmtok = t_km
                    guard["kevac_all"] = kevac
                if PH_LVL >= 4:
                    sp = r.dma_batch("sync", [
                        (dst[od * 128:(od + 1) * 128, ch * CHK:(ch + 1) * CHK],
                         kch[:, od * CHK:(od + 1) * CHK]) for od in range(PC)],
                        reads=[kevac])
                    guard["spill"] = sp
                if qnorm:
                    for od in range(PC):
                        t_sq = r.op("scalar", lambda e, od=od: e.activation(
                            out=sqA, in_=kch[:, od * CHK:(od + 1) * CHK],
                            func=AF.Square),
                            reads=[kevac, guard.get("sqfree")], sem="S")
                        t_pn = r.op("tensor", lambda e: e.matmul(
                            out=pnA[:, :], lhsT=ones_bf[:, 0:1], rhs=sqA,
                            start=True, stop=True),
                            reads=[t_sq, t_ones, guard.get("pnfree")], sem="T")
                        guard["sqfree"] = t_pn
                        t_rw = r.op("vector", lambda e: e.tensor_copy(
                            out=rowtmp[0:1, 0:CHK], in_=pnA[0:1, :]),
                            reads=[t_pn, guard.get("rowfree")], sem="V")
                        guard["pnfree"] = t_rw
                        t_tc = None
                        for tl in range(CHK // 128):
                            qt_g = (ch * CHK) // 128 + tl
                            t_tc = r.op("tensor", lambda e, tl=tl: e.transpose(
                                out=pcol[0][:, :],
                                in_=rowtmp[0:1, tl * 128:(tl + 1) * 128],
                                identity=ident[0:1, 0:1]),
                                reads=[t_rw, guard.get("pcolfree"),
                                       guard.get(("trev", 2))], sem="T")
                            t_ev = r.op("scalar", lambda e, qt_g=qt_g, od=od, tl=tl: e.copy(
                                out=nq2col(qt_g, od), in_=pcol[0][:, :]),
                                reads=[t_tc], sem="S")
                            guard["pcolfree"] = t_ev
                            guard[("nq2", qt_g, od)] = t_ev
                        guard["rowfree"] = t_tc
                    r.nq2 = {k: v for k, v in guard.items() if isinstance(k, tuple) and k[0] == "nq2"}                         if not hasattr(r, "nq2") else {**r.nq2, **{k: v for k, v in guard.items() if isinstance(k, tuple) and k[0] == "nq2"}}
                if lnspill and PH_LVL >= 4:
                    sp2 = r.dma_batch("sync", [
                        (ln1s[p * 128:(p + 1) * 128, ch * CHK:(ch + 1) * CHK],
                         lf[:, p * CHK:(p + 1) * CHK]) for p in range(PC)],
                        reads=[lf_ready])
                    guard[("lfw", ch % 2)] = [mm_last, sp2]
                else:
                    guard[("lfw", ch % 2)] = [mm_last, lf_ready]
            if end:
                r.phase_end()
            return {"mm": mm_last, "kevac": kevac, "spill": guard.get("spill"),
                    "sq": guard.get("sqfree")}

        atoks = {}
        if lim >= 1:
            atoks = phase_AC(x_seq, T // CHK, 1, qkvb_sb[:, PC:2 * PC], 1.0, Ks,
                             lnspill=True, knorm=True, end=False)

        wv = wk
        VST0 = PC * C + 2 * PC * CHK
        vst = lambda p: ar[:, VST0 + (p % 2) * HAUG: VST0 + (p % 2 + 1) * HAUG]
        bv_row = ar[0:1, VST0 + 2 * HAUG: VST0 + 2 * HAUG + C]

        if lim >= 2:
            hpv = max(PC // 2, 1)
            t_wv0 = r.dma_batch("gpsimd", [
                (wv[:, p * C:(p + 1) * C], qkvT[p * 128:(p + 1) * 128, 2 * C:3 * C])
                for p in range(hpv)], reads=[atoks.get("mm"), atoks.get("sq")])
            t_wv = r.dma_batch("gpsimd", [
                (wv[:, p * C:(p + 1) * C], qkvT[p * 128:(p + 1) * 128, 2 * C:3 * C])
                for p in range(hpv, PC)]) if PC > 1 else t_wv0
            bhead = [atoks.get("mm"), atoks.get("kevac"), atoks.get("spill"),
                     atoks.get("sq")]
            t_bv = r.dma("gpsimd", bv_row, qkvb_flat[None, 2 * C:3 * C], reads=bhead)
            t_vi0 = r.op("vector", lambda e: e.memset(
                vst(0).rearrange("p (h x) -> p h x", x=129)[:, :, 128:129], 1.0),
                reads=bhead, sem="V")
            t_vi1 = r.op("vector", lambda e: e.memset(
                vst(1).rearrange("p (h x) -> p h x", x=129)[:, :, 128:129], 1.0), sem="V")

            gB = {}
            gB[("lfree", 0)] = [atoks.get("mm"), atoks.get("spill")]
            gB[("lfree", 1)] = [atoks.get("mm"), atoks.get("spill")]
            for _vd in range(NVD):
                gB[("pvfree", _vd)] = atoks.get("kevac")
            gB[("vsfree", 0)] = [atoks.get("spill"), atoks.get("sq")]
            gB[("vsfree", 1)] = [atoks.get("spill"), atoks.get("sq")]
            for ch in range(T // CHK):
                lf = ln1f(ch)
                t_lf = r.dma_batch("sync", [
                    (lf[:, p * CHK:(p + 1) * CHK],
                     ln1s[p * 128:(p + 1) * 128, ch * CHK:(ch + 1) * CHK])
                    for p in range(PC)], reads=[gB.get(("lfree", ch % 2))])
                mm_last = None
                for tl in range(CHK // 128):
                    nh = VDW // 128
                    for vd in range(NVD):
                        mm_last = r.op("tensor", lambda e, vd=vd: e.matmul(
                            out=pk[vd][:, 0:VDW], lhsT=ones_bf[0:1, 0:128],
                            rhs=bv_row[0:1, vd * VDW:(vd + 1) * VDW], start=True, stop=False),
                            reads=[t_lf, t_bv, t_ones, t_vi0, t_vi1,
                                   gB.get(("pvfree", vd))], sem=None)
                    for ci in range(PC):
                        for vd in range(NVD):
                            mm_last = r.op("tensor", lambda e, ci=ci, vd=vd, tl=tl, lf=lf: e.matmul(
                                out=pk[vd][:, 0:VDW],
                                lhsT=lf[:, ci * CHK + tl * 128: ci * CHK + tl * 128 + 128],
                                rhs=wv[:, ci * C + vd * VDW: ci * C + vd * VDW + VDW],
                                start=False, stop=(ci == PC - 1)),
                                reads=([t_wv0] if (ci == 0 and vd == 0) else
                                       ([t_wv] if (ci == hpv and vd == 0) else [])),
                                sem="T" if (ci == PC - 1 and vd == NVD - 1) else None)
                    tvev = None
                    for vd in range(NVD):
                        tvev = r.op("vector", lambda e, vd=vd, tl=tl, nh=nh: e.tensor_copy(
                            out=vst(tl).rearrange("p (h x) -> p h x", x=129)[:, vd * nh:(vd + 1) * nh, 0:128],
                            in_=pk[vd][:, 0:VDW].rearrange("p (h x) -> p h x", x=128)),
                            reads=[mm_last, gB.get(("vsfree", tl % 2))], sem="V")
                        gB[("pvfree", vd)] = tvev
                    t0 = ch * CHK + tl * 128
                    gB[("vsfree", tl % 2)] = r.dma("sync", Vs[t0:t0 + 128, :], vst(tl),
                                                   reads=[tvev])
                gB[("lfree", ch % 2)] = mm_last
            r.phase_end()

        if lim >= 3:
            phase_AC(x_own, OWN // CHK, 0, qb_scaled, cfg.qscale, Qs, lnspill=False,
                     qnorm=True)

        psA.close()

        qbuf = ar[:, 0:PC * HALF]
        KB0 = PC * HALF
        kbuf = lambda p: ar[:, KB0 + (p % 2) * PC * KB: KB0 + (p % 2 + 1) * PC * KB]
        VB0 = KB0 + 2 * PC * KB
        vbuf = lambda p: ar[:, VB0 + (p % 2) * (KB // 128) * HAUG:
                            VB0 + (p % 2 + 1) * (KB // 128) * HAUG]
        PB0 = VB0 + 2 * (KB // 128) * HAUG
        pbuf = lambda p: ar[:, PB0 + (p % 4) * KB: PB0 + (p % 4 + 1) * KB]
        PTB0 = PB0 + 4 * KB
        ptb = lambda p: ar[:, PTB0 + (p % 4) * KB: PTB0 + (p % 4 + 1) * KB]
        OFB0 = PTB0 + 4 * KB
        ofbf = lambda p: ar[:, OFB0 + (p % 4) * 128: OFB0 + (p % 4 + 1) * 128]

        obuf = lambda qt: af[:, (qt % QTH) * HAUG: (qt % QTH + 1) * HAUG]
        MKOFF = QTH * HAUG
        mslot = lambda p: af[:, MKOFF + (p % 2) * KB: MKOFF + (p % 2 + 1) * KB]

        WO0 = PC * HALF
        wo = ar[:, WO0: WO0 + PC * CH2]
        ln2f = ar[:, LN2F0: LN2F0 + PC * HALF]
        OBR0 = LN2F0 + PC * HALF
        ob_row = ar[0:1, OBR0: OBR0 + C]
        ybuf = lambda qt: af[:, (qt % QTH) * C: (qt % QTH + 1) * C]

        gsb = lambda p: ar[:, (p % 2) * PC * SLAB: (p % 2 + 1) * PC * SLAB]
        usb = lambda p: ar[:, 2 * PC * SLAB + (p % 2) * PC * SLAB:
                           2 * PC * SLAB + (p % 2 + 1) * PC * SLAB]
        DS0 = LN2F0 + PC * HALF
        dsb = ar[:, DS0: DS0 + (SLAB // 128) * C]
        hsb = tb[:, 0:(SLAB // 128) * HALF]
        silu_t = af[:, XB0: XB0 + HALF]

        t_rt = r.dma_batch("gpsimd", [
            (router_sb[:, p * 8:(p + 1) * 8], routerT[p * 128:(p + 1) * 128, 0:8])
            for p in range(PC)])

        mask_idx = 0
        out_toks = []
        t_ob = None
        for half in range(2):
            if lim < 4 + 3 * half:
                break
            qts = list(range(half * QTH, (half + 1) * QTH))
            maxcov = max(cov_of(cfg, i) for i in qts)

            psD = ExitStack()
            ps_s = [psD.enter_context(nc.psum_tensor(f"psS{half}{i}", [128, KB], F32)) for i in range(3)]
            ptp = [psD.enter_context(nc.psum_tensor(f"ptD{half}{i}", [128, KB], BF16)).ap()[:, :] for i in range(3)]
            pav = [psD.enter_context(nc.psum_tensor(f"pav{half}{i}", [128, 129], F32)).ap()[:, :] for i in range(2)]

            t_qbuf = r.dma_batch("sync", [
                (qbuf[:, p * HALF:(p + 1) * HALF],
                 Qs[p * 128:(p + 1) * 128, half * HALF:(half + 1) * HALF])
                for p in range(PC)])

            bset = {}
            t_bcast = r.op("tensor", lambda e: e.matmul(
                out=pav[0][:, 0:H], lhsT=ones_bf[0:1, 0:128],
                rhs=kmx2row[0:1, 0:H], start=True, stop=True),
                reads=[r.kmtok, t_ones], sem="T")
            tms = {}
            for qt in qts:
                for hd in range(H):
                    tms[(qt, hd)] = r.op("vector", lambda e, qt=qt, hd=hd: e.tensor_mul(
                        out=mcol(qt, hd, 1), in0=nq2col(qt, hd), in1=pav[0][:, hd:hd + 1]),
                        reads=[t_bcast, r.nq2[("nq2", qt, hd)]], sem="V")
            tsqs = {}
            for qt in qts:
                for hd in range(H):
                    tsqs[(qt, hd)] = r.op("scalar", lambda e, qt=qt, hd=hd: e.activation(
                        out=mcol(qt, hd, 1), in_=mcol(qt, hd, 1), func=AF.Sqrt),
                        reads=[tms[(qt, hd)]], sem="S")
            for qt in qts:
                for hd in range(H):
                    t_bn = r.op("vector", lambda e, qt=qt, hd=hd: e.tensor_scalar_mul(
                        out=mcol(qt, hd, 0), in0=mcol(qt, hd, 1), scalar1=-1.0),
                        reads=[tsqs[(qt, hd)]], sem="V")
                    bset[(qt, hd)] = t_bn

            kv_free, o_tok, s_free, p_free, pt_free, av_free, m_free = {}, {}, {}, {}, {}, {}, {}
            tup = 0
            last_s_mm = None
            kv_done = {}
            pend = []
            pend2 = []
            bkc = [0]
            avc = [0]

            def flush_mid():
                st = pend.pop(0)
                bp = bkc[0] % 3
                ap_ = bkc[0] % 4
                bkc[0] += 1
                pp = st["pp"]
                t_tr = None
                for j in range(KB // 128):
                    t_tr = r.op("tensor", lambda e, j=j, pp=pp, bp=bp: e.transpose(
                        out=ptp[bp][:, j * 128:(j + 1) * 128],
                        in_=pbuf(pp)[:, j * 128:(j + 1) * 128], identity=ident),
                        reads=([st["t_p"], pt_free.get(bp)] if j == 0 else []),
                        sem="T" if j == KB // 128 - 1 else None)
                t_ptc = r.op("vector", lambda e, ap_=ap_, bp=bp: e.tensor_copy(
                    out=ptb(ap_), in_=ptp[bp][:, :]),
                    reads=[t_tr, av_free.get((ap_, "ptb"))], sem="V")
                p_free[pp] = t_ptc
                pt_free[bp] = t_ptc
                st["ap_"] = ap_
                st["t_ptc"] = t_ptc
                pend2.append(st)

            def flush_back():
                st = pend2.pop(0)
                ap_ = st["ap_"]
                vp_ = avc[0] % 2
                avc[0] += 1
                qt, hd, kp = st["qt"], st["hd"], st["kp"]
                t_av = None
                for j in range(KB // 128):
                    t_av = r.op("tensor", lambda e, j=j, ap_=ap_, vp_=vp_, hd=hd, kp=kp: e.matmul(
                        out=pav[vp_][:, :],
                        lhsT=ptb(ap_)[:, j * 128:(j + 1) * 128],
                        rhs=vbuf(kp)[:, j * HAUG + hd * 129: j * HAUG + hd * 129 + 129],
                        start=(j == 0), stop=(j == KB // 128 - 1)),
                        reads=([st["t_ptc"], av_free.get((vp_, "pav"))] if j == 0 else []),
                        sem="T" if j == KB // 128 - 1 else None)
                av_free[(ap_, "ptb")] = t_av
                osl = obuf(qt).rearrange("p (h x) -> p h x", x=129)[:, hd:hd + 1, :].squeeze(1)
                if st["first"]:
                    t_o = r.op("vector", lambda e, osl=osl, vp_=vp_: e.tensor_copy(
                        out=osl, in_=pav[vp_][:, :]),
                        reads=[t_av, o_tok.get((qt, hd))], sem="V")
                else:
                    t_o = r.op("vector", lambda e, osl=osl, vp_=vp_: e.tensor_add(
                        out=osl, in0=osl, in1=pav[vp_][:, :]),
                        reads=[t_av, o_tok.get((qt, hd))], sem="V")
                o_tok[(qt, hd)] = t_o
                o_tok[(qt, hd, "mf")] = t_o
                av_free[(vp_, "pav")] = t_o
                if st["last_of_kb"]:
                    kv_done[st["kb"]] = t_o
                return t_o

            LAG = 2
            LAG2 = 2
            for kb in range(maxcov):
                kp = kb % 2
                kvpairs = [(kbuf(kp)[:, p * KB:(p + 1) * KB],
                            Ks[p * 128:(p + 1) * 128, kb * KB:(kb + 1) * KB])
                           for p in range(PC)]
                kvpairs += [(vbuf(kp)[:, j * HAUG:(j + 1) * HAUG],
                             Vs[kb * KB + j * 128: kb * KB + (j + 1) * 128, :])
                            for j in range(KB // 128)]
                t_kv = r.dma_batch("sync", kvpairs, reads=[kv_done.get(kb - 2)])
                active = [q for q in qts if kb < cov_of(cfg, q)]
                for qn, qt in enumerate(active):
                    qrel = qt % QTH
                    need_mask = open_of(cfg, qt) <= kb
                    t_mask = None
                    mp = mask_idx % 2
                    if need_mask:
                        t_mask = r.dma("sync", mslot(mp), masks[mask_idx, :, :],
                                       reads=[m_free.get(mp)])
                        mask_idx += 1
                    first = (kb == 0)
                    for hd in range(H):
                        pp = tup % 4
                        ps = tup % 3
                        t_smm = r.op("tensor", lambda e, hd=hd, qrel=qrel, kp=kp, ps=ps: e.matmul(
                            out=ps_s[ps][:, :],
                            lhsT=qbuf[:, hd * HALF + qrel * 128: hd * HALF + qrel * 128 + 128],
                            rhs=kbuf(kp)[:, hd * KB:(hd + 1) * KB],
                            start=True, stop=True),
                            reads=[t_qbuf, t_kv, s_free.get(ps)], sem="T")
                        last_s_mm = t_smm
                        t_sm = t_smm
                        if need_mask:
                            t_sm = r.op("vector", lambda e, ps=ps, mp=mp: e.tensor_add(
                                out=ps_s[ps][:, :], in0=ps_s[ps][:, :], in1=mslot(mp)),
                                reads=[t_smm, t_mask], sem="V")
                            m_free[mp] = t_sm
                        t_p = r.op("scalar", lambda e, pp=pp, ps=ps, qt=qt, hd=hd: e.activation(
                            out=pbuf(pp), in_=ps_s[ps][:, :], func=AF.Exp,
                            bias=mcol(qt, hd, 0)),
                            reads=[t_sm, bset[(qt, hd)], p_free.get(pp)], sem="S")
                        s_free[ps] = t_p
                        pend.append({"pp": pp, "qt": qt, "hd": hd, "kp": kp,
                                     "first": first, "t_p": t_p,
                                     "kb": kb,
                                     "last_of_kb": (qn == len(active) - 1 and hd == H - 1)})
                        if len(pend) > LAG:
                            flush_mid()
                        if len(pend2) > LAG2:
                            flush_back()
                        tup += 1
            while pend:
                flush_mid()
            while pend2:
                flush_back()

            fin_ev = {}
            t_fin_last = None
            fpend = []
            fbk = [0]

            def flush_fin():
                stf = fpend.pop(0)
                fp2 = fbk[0] % 3
                fbk[0] += 1
                t_tr = r.op("tensor", lambda e, fp=stf["fp"], fp2=fp2: e.transpose(
                    out=ptp[fp2][:, 0:128], in_=ofbf(fp), identity=ident),
                    reads=[stf["t_of"], fin_ev.get(("ev", fp2)), last_s_mm], sem="T")
                t_ev = r.op("scalar", lambda e, fp2=fp2, hd=stf["hd"], qrel=stf["qrel"]: e.copy(
                    out=qbuf[:, hd * HALF + qrel * 128: hd * HALF + qrel * 128 + 128],
                    in_=ptp[fp2][:, 0:128]),
                    reads=[t_tr], sem="S")
                fin_ev[("ev", fp2)] = t_ev
                fin_ev[("trdone", stf["fp"])] = t_ev
                return t_ev

            fi = 0
            for qt in qts:
                qrel = qt % QTH
                for hd in range(H):
                    osl = obuf(qt).rearrange("p (h x) -> p h x", x=129)[:, hd:hd + 1, :].squeeze(1)
                    fp = fi % 4
                    t_rl = r.op("vector", lambda e, osl=osl, fp=fp: e.reciprocal(
                        out=lrec(fp), in_=osl[:, 128:129]),
                        reads=[o_tok[(qt, hd)], fin_ev.get(("lfree", fp))], sem="V")
                    t_of = r.op("vector", lambda e, osl=osl, fp=fp: e.tensor_scalar_mul(
                        out=ofbf(fp), in0=osl[:, 0:128], scalar1=lrec(fp)),
                        reads=[t_rl, fin_ev.get(("trdone", fp))], sem="V")
                    fin_ev[("lfree", fp)] = t_of
                    fpend.append({"fp": fp, "t_of": t_of, "hd": hd, "qrel": qrel})
                    if len(fpend) > LAG:
                        t_fin_last = flush_fin()
                    fi += 1
            while fpend:
                t_fin_last = flush_fin()
            if dbg:
                r.dma_batch("sync", [
                    (dbg_ot[half, p * 128:(p + 1) * 128, :], qbuf[:, p * HALF:(p + 1) * HALF])
                    for p in range(PC)], reads=[t_fin_last])
            r.phase_end()

            if lim < 5 + 3 * half:
                psD.close()
                break
            y_tok, x_used, xfree, ln2_toks = {}, {}, {}, {}
            mm = None
            for coh in range(2):
                wt = r.dma_batch("gpsimd", [
                    (wo[:, p * CH2:(p + 1) * CH2],
                     outwT[p * 128:(p + 1) * 128, coh * CH2:(coh + 1) * CH2])
                    for p in range(PC)], reads=[y_tok.get("wofree")])
                if coh == 0:
                    t_ob = r.dma("gpsimd", ob_row, outb[None, :])
                unit = 0
                for qt in qts:
                    qrel = qt % QTH
                    xp = qt % 2
                    x_used[(qt, coh)] = r.dma("sync", xslot(xp),
                                              x_own[qt * 128:(qt + 1) * 128, :],
                                              reads=[xfree.get(xp)])
                    for cc in range(NCC):
                        up = unit % 2
                        mm = r.op("tensor", lambda e, coh=coh, cc=cc, up=up: e.matmul(
                            out=ps_s[up][:, 0:CCW], lhsT=ones_bf[0:1, 0:128],
                            rhs=ob_row[0:1, coh * CH2 + cc * CCW: coh * CH2 + cc * CCW + CCW],
                            start=True, stop=False),
                            reads=[wt, t_ob, y_tok.get(("pofree", up)), t_fin_last], sem=None)
                        for ci in range(PC):
                            mm = r.op("tensor", lambda e, ci=ci, cc=cc, coh=coh, qrel=qrel, up=up: e.matmul(
                                out=ps_s[up][:, 0:CCW],
                                lhsT=qbuf[:, ci * HALF + qrel * 128: ci * HALF + qrel * 128 + 128],
                                rhs=wo[:, ci * CH2 + cc * CCW: ci * CH2 + cc * CCW + CCW],
                                start=False, stop=(ci == PC - 1)),
                                sem="T" if ci == PC - 1 else None)
                        t_ev = r.op("vector", lambda e, qt=qt, coh=coh, cc=cc, xp=xp, up=up: e.scalar_tensor_tensor(
                            out=ybuf(qt)[:, coh * CH2 + cc * CCW: coh * CH2 + cc * CCW + CCW],
                            in0=xslot(xp)[:, coh * CH2 + cc * CCW: coh * CH2 + cc * CCW + CCW],
                            scalar=1.0, in1=ps_s[up][:, 0:CCW], op0=ALU.mult, op1=ALU.add),
                            reads=[mm, x_used[(qt, coh)], y_tok.get(("yfree", qt))], sem="V")
                        y_tok[("pofree", up)] = t_ev
                        xfree[xp] = t_ev
                        unit += 1
                    y_tok[(qt, coh)] = t_ev
                y_tok["wofree"] = mm
            t_l2last = None
            for qt in qts:
                qrel = qt % QTH
                xp = qt % 2
                t_norm = ln_chain(ybuf(qt), xp, lnbf(xp),
                                  [y_tok[(qt, 0)], y_tok[(qt, 1)], ln2_toks.get(("lnbf_free", xp))])
                ev = None
                for p in range(PC):
                    pp = p % 2
                    t_tr = r.op("tensor", lambda e, p=p, pp=pp, xp=xp: e.transpose(
                        out=ptp[pp][:, 0:128], in_=lnbf(xp)[:, p * 128:(p + 1) * 128],
                        identity=ident),
                        reads=[t_norm, ln2_toks.get(("ev", pp))], sem="T")
                    ev = r.op("scalar", lambda e, p=p, pp=pp, qrel=qrel: e.copy(
                        out=ln2f[:, p * HALF + qrel * 128: p * HALF + qrel * 128 + 128],
                        in_=ptp[pp][:, 0:128]),
                        reads=[t_tr], sem="S")
                    ln2_toks[("ev", pp)] = ev
                ln2_toks[("lnbf_free", xp)] = ev
                ln2_toks[qt] = ev
                t_l2last = ev
            if dbg:
                for qt in qts:
                    r.dma("sync", dbg_y[qt * 128:(qt + 1) * 128, :], ybuf(qt),
                          reads=[y_tok[(qt, 0)], y_tok[(qt, 1)]])
                r.dma_batch("sync", [
                    (dbg_l2[half, p * 128:(p + 1) * 128, :], ln2f[:, p * HALF:(p + 1) * HALF])
                    for p in range(PC)], reads=[t_l2last])
            r.phase_end()
            psD.close()

            if lim < 6 + 3 * half:
                break
            psF = ExitStack()
            pgF = psF.enter_context(nc.psum_tensor(f"pgF{half}", [128, HALF], F32))
            puF = psF.enter_context(nc.psum_tensor(f"puF{half}", [128, HALF], F32))
            pyF = [psF.enter_context(nc.psum_tensor(f"pyF{half}{i}", [128, YW], F32)) for i in range(2)]

            rt_ev = {}
            for qt in qts:
                qrel = qt % QTH
                up = qt % 2
                mm = None
                for ci in range(PC):
                    mm = r.op("tensor", lambda e, ci=ci, qrel=qrel: e.matmul(
                        out=pgF[:, 0:8],
                        lhsT=ln2f[:, ci * HALF + qrel * 128: ci * HALF + qrel * 128 + 128],
                        rhs=router_sb[:, ci * 8:(ci + 1) * 8],
                        start=(ci == 0), stop=(ci == PC - 1)),
                        reads=([t_rt, t_l2last, rt_ev.get("prfree")] if ci == 0 else []),
                        sem="T" if ci == PC - 1 else None)
                lg = rlog(up)
                t_lg = r.op("vector", lambda e, lg=lg: e.tensor_copy(out=lg, in_=pgF[:, 0:8]),
                            reads=[mm, rt_ev.get(("lgfree", up))], sem="V")
                rt_ev["prfree"] = t_lg
                t_nmx = r.op("vector", lambda e, lg=lg: e.reduce_max(
                    out=rst(0), in_=lg, axis=AX.X, negate=True), reads=[t_lg, rt_ev.get("st")], sem="V")
                t_d1 = r.op("vector", lambda e, lg=lg: e.tensor_scalar_add(
                    out=lg, in0=lg, scalar1=rst(0)), reads=[t_nmx], sem="V")
                t_eq = r.op("vector", lambda e, lg=lg: e.tensor_single_scalar(
                    out=rtmp, in_=lg, scalar=0.0, op=ALU.is_ge), reads=[t_d1], sem="V")
                t_cnt = r.op("vector", lambda e: e.reduce_sum(
                    out=rst(1), in_=rtmp, axis=AX.X), reads=[t_eq], sem="V")
                t_msk = r.op("vector", lambda e, lg=lg: e.scalar_tensor_tensor(
                    out=rtmp, in0=rtmp, scalar=NEG, in1=lg, op0=ALU.mult, op1=ALU.add),
                    reads=[t_eq, t_cnt], sem="V")
                t_n2 = r.op("vector", lambda e: e.reduce_max(
                    out=rst(2), in_=rtmp, axis=AX.X, negate=True), reads=[t_msk], sem="V")
                t_tie = r.op("vector", lambda e: e.tensor_single_scalar(
                    out=rst(3), in_=rst(1), scalar=2.0, op=ALU.is_ge), reads=[t_cnt], sem="V")
                t_de = r.op("vector", lambda e: e.tensor_scalar(
                    out=rst(4), in0=rst(3), scalar1=-1.0, scalar2=1.0,
                    op0=ALU.mult, op1=ALU.add), reads=[t_tie], sem="V")
                t_de2 = r.op("vector", lambda e: e.tensor_mul(
                    out=rst(5), in0=rst(2), in1=rst(4)), reads=[t_n2, t_de], sem="V")
                t_u2 = r.op("scalar", lambda e: e.activation(
                    out=rst(6), in_=rst(5), func=AF.Exp, scale=-1.0), reads=[t_de2], sem="S")
                t_ss = r.op("vector", lambda e: e.tensor_scalar_add(
                    out=rst(7), in0=rst(6), scalar1=1.0), reads=[t_u2], sem="V")
                t_rc = r.op("vector", lambda e: e.reciprocal(
                    out=rst(8), in_=rst(7)), reads=[t_ss], sem="V")
                t_c = r.op("vector", lambda e, qt=qt: e.tensor_mul(
                    out=ccol(qt), in0=rst(7), in1=rst(8)), reads=[t_rc], sem="V")
                rt_ev[qt] = t_c
                rt_ev[("lgfree", up)] = t_d1
                rt_ev["st"] = t_c

            ff = {}
            ff["pgfree"] = rt_ev["prfree"]
            slabs = [("r", js) for js in range(cfg.RINTER // SLAB)] + \
                    [("s", js) for js in range(cfg.INTER // SLAB)]
            for sn, (stype, js) in enumerate(slabs):
                gsrc, usrc, dsrc = (rgT, ruT, rdT) if stype == "r" else (gT, uT, dT)
                sp = sn % 2
                tg = r.dma_batch("gpsimd", [
                    (gsb(sp)[:, p * SLAB:(p + 1) * SLAB],
                     gsrc[p * 128:(p + 1) * 128, js * SLAB:(js + 1) * SLAB])
                    for p in range(PC)], reads=[ff.get(("gfree", sp))])
                tu = r.dma_batch("gpsimd", [
                    (usb(sp)[:, p * SLAB:(p + 1) * SLAB],
                     usrc[p * 128:(p + 1) * 128, js * SLAB:(js + 1) * SLAB])
                    for p in range(PC)], reads=[ff.get(("ufree", sp))])
                td = r.dma_batch("gpsimd", [
                    (dsb[:, j * C:(j + 1) * C],
                     dsrc[js * SLAB + j * 128: js * SLAB + (j + 1) * 128, :])
                    for j in range(SLAB // 128)], reads=[ff.get("dfree")])
                dn_last = None
                for j in range(SLAB // 128):
                    mmg = None
                    for tk in range(NTK):
                        for ci in range(PC):
                            mmg = r.op("tensor", lambda e, ci=ci, j=j, tk=tk, sp=sp: e.matmul(
                                out=pgF[:, tk * TW:(tk + 1) * TW],
                                lhsT=gsb(sp)[:, ci * SLAB + j * 128: ci * SLAB + j * 128 + 128],
                                rhs=ln2f[:, ci * HALF + tk * TW: ci * HALF + tk * TW + TW],
                                start=(ci == 0), stop=(ci == PC - 1)),
                                reads=([tg, ff.get("pgfree")] if ci == 0 and tk == 0 else []),
                                sem="T" if (ci == PC - 1 and tk == NTK - 1) else None)
                    mmu = None
                    for tk in range(NTK):
                        for ci in range(PC):
                            mmu = r.op("tensor", lambda e, ci=ci, j=j, tk=tk, sp=sp: e.matmul(
                                out=puF[:, tk * TW:(tk + 1) * TW],
                                lhsT=usb(sp)[:, ci * SLAB + j * 128: ci * SLAB + j * 128 + 128],
                                rhs=ln2f[:, ci * HALF + tk * TW: ci * HALF + tk * TW + TW],
                                start=(ci == 0), stop=(ci == PC - 1)),
                                reads=([tu, ff.get("pufree")] if ci == 0 and tk == 0 else []),
                                sem="T" if (ci == PC - 1 and tk == NTK - 1) else None)
                    t_si = r.op("scalar", lambda e: e.activation(
                        out=silu_t, in_=pgF[:, :], func=AF.Silu),
                        reads=[mmg, ff.get("silufree")], sem="S")
                    ff["pgfree"] = t_si
                    t_h = r.op("vector", lambda e, j=j: e.tensor_tensor(
                        out=hsb[:, j * HALF:(j + 1) * HALF], in0=silu_t, in1=puF[:, :],
                        op=ALU.mult),
                        reads=[t_si, mmu, ff.get(("hfree", j))], sem="V")
                    ff["pufree"] = t_h
                    ff["silufree"] = t_h
                    ff[("h", j)] = t_h
                unit = 0
                for qt in qts:
                    qrel = qt % QTH
                    for yc in range(NYC):
                        up = unit % 2
                        mm = None
                        ncc2 = max(YW // 512, 1)
                        ccw2 = min(512, YW)
                        for j in range(SLAB // 128):
                            for cc in range(ncc2):
                                mm = r.op("tensor", lambda e, j=j, cc=cc, yc=yc, qrel=qrel, up=up, ccw2=ccw2: e.matmul(
                                    out=pyF[up][:, cc * ccw2:(cc + 1) * ccw2],
                                    lhsT=hsb[:, j * HALF + qrel * 128: j * HALF + qrel * 128 + 128],
                                    rhs=dsb[:, j * C + yc * YW + cc * ccw2: j * C + yc * YW + cc * ccw2 + ccw2],
                                    start=(j == 0), stop=(j == SLAB // 128 - 1)),
                                    reads=([td, ff.get(("pyfree", up)), ff[("h", j)]]
                                           if cc == 0 else [ff[("h", j)]]),
                                    sem="T" if (j == SLAB // 128 - 1 and cc == ncc2 - 1) else None)
                        dn_last = mm
                        if stype == "r":
                            t_ev = r.op("vector", lambda e, qt=qt, yc=yc, up=up: e.scalar_tensor_tensor(
                                out=ybuf(qt)[:, yc * YW:(yc + 1) * YW],
                                in0=pyF[up][:, :], scalar=ccol(qt),
                                in1=ybuf(qt)[:, yc * YW:(yc + 1) * YW],
                                op0=ALU.mult, op1=ALU.add),
                                reads=[mm, rt_ev[qt], ff.get(("ylast", qt, yc)),
                                       ln2_toks[qt]], sem="V")
                        else:
                            t_ev = r.op("vector", lambda e, qt=qt, yc=yc, up=up: e.tensor_add(
                                out=ybuf(qt)[:, yc * YW:(yc + 1) * YW],
                                in0=ybuf(qt)[:, yc * YW:(yc + 1) * YW],
                                in1=pyF[up][:, :]),
                                reads=[mm, ff.get(("ylast", qt, yc)), ln2_toks[qt]], sem="V")
                        ff[("pyfree", up)] = t_ev
                        ff[("ylast", qt, yc)] = t_ev
                        unit += 1
                for j in range(SLAB // 128):
                    ff[("hfree", j)] = dn_last
                ff[("gfree", sp)] = dn_last
                ff[("ufree", sp)] = dn_last
                ff["dfree"] = dn_last
            for qt in qts:
                reads = [ff[("ylast", qt, yc)] for yc in range(NYC)]
                tko = r.dma("sync", out_own[qt * 128:(qt + 1) * 128, :], ybuf(qt),
                            reads=reads)
                out_toks.append(tko)
            r.phase_end()
            psF.close()

        if out_toks:
            r.op("sync", lambda e: e.nop(), reads=out_toks)
        else:
            r.op("sync", lambda e: e.nop(), reads=r.phase_end())

        blk = es.enter_context(nc.Block())
        r.emit(blk, sems)

    return nc



def _host_inputs(cfg, x, qkv_w, qkv_b, out_w, out_b, router_w,
                 sh_gate, sh_up, sh_down, r_gate, r_up, r_down):
    C, T, OWN, A = cfg.C, cfg.T, cfg.OWN, cfg.A
    import ml_dtypes
    shared = {
        "qkvT": np.ascontiguousarray(np.asarray(qkv_w, np.float32).T),
        "qkvb_cols": np.ascontiguousarray(np.asarray(qkv_b, np.float32).reshape(3 * C // 128, 128).T),
        "qkvb_flat": np.ascontiguousarray(np.asarray(qkv_b, np.float32)),
        "outwT": np.ascontiguousarray(np.asarray(out_w, np.float32).T),
        "outb": np.ascontiguousarray(np.asarray(out_b, np.float32)),
        "gT": np.ascontiguousarray(np.asarray(sh_gate, np.float32).T),
        "uT": np.ascontiguousarray(np.asarray(sh_up, np.float32).T),
        "dT": np.ascontiguousarray(np.asarray(sh_down, np.float32).T),
        "rgT": np.ascontiguousarray(np.asarray(r_gate, np.float32).T),
        "ruT": np.ascontiguousarray(np.asarray(r_up, np.float32).T),
        "rdT": np.ascontiguousarray(np.asarray(r_down, np.float32).T),
        "routerT": np.ascontiguousarray(np.asarray(router_w, np.float32).T),
        "identin": np.eye(128, dtype=ml_dtypes.bfloat16),
    }
    masks_by_cls = [build_masks(cfg, 0), build_masks(cfg, 1)]
    B = x.shape[0]
    in_maps = []
    for c in range(2 * B):
        b, cls = c // 2, c % 2
        xs = np.ascontiguousarray(x[b])
        xo = np.ascontiguousarray(x[b].reshape(T // 128, 128, C)[cls::2].reshape(OWN, C))
        in_maps.append({"x_seq": xs, "x_own": xo, "masks": masks_by_cls[cls], **shared})
    return in_maps


def _scatter(cfg, outs, B):
    T, OWN, C = cfg.T, cfg.OWN, cfg.C
    full = np.empty((B, T, C), np.float32)
    for c, o in enumerate(outs):
        b, cls = c // 2, c % 2
        full[b].reshape(T // 128, 128, C)[cls::2] = o.reshape(OWN // 128, 128, C)
    return full


_NC_CACHE = {}


def kernel(L0_attn_x, L0_attn_qkv_w, L0_attn_qkv_b, L0_attn_out_w, L0_attn_out_b,
           L0_ffn_router_weight, L0_ffn_sh_gate, L0_ffn_sh_up, L0_ffn_sh_down,
           L0_ffn_r_gate, L0_ffn_r_up, L0_ffn_r_down):
    cfg = FULL
    x = np.asarray(L0_attn_x, np.float32)
    in_maps = _host_inputs(cfg, x, L0_attn_qkv_w, L0_attn_qkv_b, L0_attn_out_w,
                           L0_attn_out_b, L0_ffn_router_weight, L0_ffn_sh_gate,
                           L0_ffn_sh_up, L0_ffn_sh_down, L0_ffn_r_gate,
                           L0_ffn_r_up, L0_ffn_r_down)
    if "nc" not in _NC_CACHE:
        _NC_CACHE["nc"] = build(cfg)
    nc = _NC_CACHE["nc"]
    res = run_bass_kernel_spmd(nc, in_maps, core_ids=list(range(8)))
    outs = [r["out_own"] for r in res.results]
    return _scatter(cfg, outs, x.shape[0])


# version 25 (final)
# speedup vs baseline: 1.2210x; 1.0107x over previous
import os
import numpy as np
from contextlib import ExitStack

import concourse.bass as bass
import concourse.mybir as mybir
from concourse.bass_utils import run_bass_kernel_spmd

F32 = mybir.dt.float32
BF16 = mybir.dt.bfloat16
AF = mybir.ActivationFunctionType
ALU = mybir.AluOpType
AX = mybir.AxisListType

NEG = -1.0e30


class Cfg:
    def __init__(self, C, T, INTER, RINTER):
        self.C, self.T, self.INTER, self.RINTER = C, T, INTER, RINTER
        self.PC = C // 128
        self.H = self.PC
        self.HAUG = self.H * 129
        self.OWN = T // 2
        self.A = self.OWN // 2
        self.HALF = self.OWN // 2
        self.NQT = self.OWN // 128
        self.QTH = self.NQT // 2
        self.KB = 512
        self.NKB = T // self.KB
        self.CHK = 512
        self.SLAB = 512
        self.CH2 = C // 2
        self.eps = 1e-5
        self.qscale = 1.0 / np.sqrt(128.0)


FULL = Cfg(C=2048, T=4096, INTER=8192, RINTER=1024)
MINI = Cfg(C=256, T=1024, INTER=1024, RINTER=512)


def qpos_of(cfg, cls, i):
    return 256 * i + 128 * cls


def cov_of(cfg, i):
    mx = max(qpos_of(cfg, 0, i), qpos_of(cfg, 1, i))
    return (mx + 128 + cfg.KB - 1) // cfg.KB


def open_of(cfg, i):
    mn = min(qpos_of(cfg, 0, i), qpos_of(cfg, 1, i))
    return mn // cfg.KB


def mask_pairs(cfg):
    out = []
    for half in range(2):
        qts = range(half * cfg.QTH, (half + 1) * cfg.QTH)
        maxcov = max(cov_of(cfg, i) for i in qts)
        for kb in range(maxcov):
            for i in qts:
                if open_of(cfg, i) <= kb < cov_of(cfg, i):
                    out.append((half, kb, i))
    return out


def build_masks(cfg, cls):
    pairs = mask_pairs(cfg)
    m = np.zeros((len(pairs), 128, cfg.KB), np.float32)
    kr = np.arange(cfg.KB)
    qr = np.arange(128)
    for n, (_h, kb, i) in enumerate(pairs):
        qg = qpos_of(cfg, cls, i) + qr[:, None]
        kg = kb * cfg.KB + kr[None, :]
        m[n] = np.where(kg <= qg, 0.0, NEG).astype(np.float32)
    return m



ENGS = ("sync", "tensor", "vector", "scalar", "gpsimd")
_ESEM = {"sync": "SY", "gpsimd": "G", "scalar": "S", "vector": "V", "tensor": "T"}


def _flat(reads):
    out = []
    for x in reads:
        if x is None:
            continue
        if isinstance(x, (list, tuple)) and x and isinstance(x[0], (list, tuple)):
            out.extend(_flat(x))
        elif isinstance(x, (list, tuple)) and len(x) == 2 and isinstance(x[0], str):
            out.append(x)
        elif isinstance(x, (list, tuple)):
            out.extend(_flat(x))
    return out


NSYNC_LANES = 6
NGP_LANES = 4
SYNC_LANES = [f"DS{i}" for i in range(NSYNC_LANES)]
GP_LANES = [f"DG{i}" for i in range(NGP_LANES)]


class Rec:

    def __init__(self):
        self.items = {e: [] for e in ENGS}
        self.cnt = {}
        self.pending = {e: [] for e in ENGS}
        self.lane_rr = {"sync": 0, "gpsimd": 0}

    def op(self, eng, fn, reads=(), sem=None, inc=1):
        reads = _flat(list(reads))
        if self.pending[eng]:
            reads = self.pending[eng] + reads
            self.pending[eng] = []
        tok = None
        if sem is not None:
            self.cnt[sem] = self.cnt.get(sem, 0) + inc
            tok = (sem, self.cnt[sem])
        self.items[eng].append({"reads": reads, "fn": fn, "sem": sem,
                                "inc": inc if sem else 0})
        return tok

    def dma_batch(self, eng, pairs, reads=(), sem=None):
        if sem is None:
            lanes = SYNC_LANES if eng == "sync" else GP_LANES
            i = self.lane_rr[eng]
            sem = lanes[i % len(lanes)]
            self.lane_rr[eng] = i + 1
        gate = (sem, self.cnt.get(sem, 0))
        tok = None
        for n, (out, in_) in enumerate(pairs):
            tok = self.op(eng, (lambda o, i_: lambda e: e.dma_start(out=o, in_=i_))(out, in_),
                          reads=(list(reads) + [gate]) if n == 0 else [],
                          sem=sem, inc=16)
        return tok

    def dma(self, eng, out, in_, reads=(), sem=None):
        return self.dma_batch(eng, [(out, in_)], reads=reads, sem=sem)

    def phase_end(self):
        toks = []
        for e in ENGS:
            if not self.items[e]:
                continue
            it = self.items[e][-1]
            if it["sem"] is None:
                it["sem"] = _ESEM[e]
                it["inc"] = 1
                self.cnt[it["sem"]] = self.cnt.get(it["sem"], 0) + it["inc"]
            toks.append((it["sem"], self.cnt[it["sem"]]))
        for lane in SYNC_LANES + GP_LANES:
            if self.cnt.get(lane, 0):
                toks.append((lane, self.cnt[lane]))
        toks = list({s: (s, v) for s, v in toks}.values())
        for e in ENGS:
            self.pending[e] = list(toks)
        return toks

    def emit(self, blk, sems):
        for ename in ENGS:
            stream = self.items[ename]
            if not stream:
                continue

            def runner(e, stream=stream):
                waited = {}
                for it in stream:
                    for s, v in it["reads"]:
                        if waited.get(s, 0) < v:
                            e.wait_ge(sems[s], v)
                            waited[s] = v
                    ins = it["fn"](e)
                    if it["sem"] is not None:
                        ins.then_inc(sems[it["sem"]], it["inc"])
            getattr(blk, ename)(runner)



def build(cfg, dbg=False, lim=99):
    PH_LVL = int(os.environ.get("PH_LVL", "99"))
    nc = bass.Bass("TRN2", target_bir_lowering=False)
    C, T, PC, H = cfg.C, cfg.T, cfg.PC, cfg.H
    OWN, HALF, KB, CHK, SLAB = cfg.OWN, cfg.HALF, cfg.KB, cfg.CHK, cfg.SLAB
    HAUG, CH2 = cfg.HAUG, cfg.CH2
    QTH = cfg.QTH
    assert CHK == KB
    assert cfg.INTER % SLAB == 0 and cfg.RINTER % SLAB == 0
    TW = min(512, HALF)
    NTK = HALF // TW
    YW = min(1024, C)
    NYC = C // YW
    QD = min(4, PC)
    NQD = PC // QD
    VDW = min(512, C)
    NVD = C // VDW
    CCW = min(512, CH2)
    NCC = CH2 // CCW
    NST = max(C // 512, 1)
    STW = min(512, C)

    pairs = mask_pairs(cfg)
    NMASK = len(pairs)

    class _Lazy:
        def __init__(self, name, shape, dtype):
            self.args = (name, shape, dtype)
            self._ap = None

        def __getitem__(self, key):
            if self._ap is None:
                self._ap = nc.dram_tensor(self.args[0], self.args[1], self.args[2],
                                          kind="ExternalInput").ap()
            return self._ap[key]

    x_seq = _Lazy("x_seq", [T, C], F32)
    x_own = _Lazy("x_own", [OWN, C], F32)
    qkvT = _Lazy("qkvT", [C, 3 * C], F32)
    qkvb_cols = _Lazy("qkvb_cols", [128, 3 * PC], F32)
    qkvb_flat = _Lazy("qkvb_flat", [3 * C], F32)
    outwT = _Lazy("outwT", [C, C], F32)
    outb = _Lazy("outb", [C], F32)
    gT = _Lazy("gT", [C, cfg.INTER], F32)
    uT = _Lazy("uT", [C, cfg.INTER], F32)
    dT = _Lazy("dT", [cfg.INTER, C], F32)
    rgT = _Lazy("rgT", [C, cfg.RINTER], F32)
    ruT = _Lazy("ruT", [C, cfg.RINTER], F32)
    rdT = _Lazy("rdT", [cfg.RINTER, C], F32)
    routerT = _Lazy("routerT", [C, 8], F32)
    masks = _Lazy("masks", [max(NMASK, 1), 128, KB], F32)
    identin = _Lazy("identin", [128, 128], BF16)
    out_own = nc.dram_tensor("out_own", [OWN, C], F32, kind="ExternalOutput").ap()

    skind = "ExternalOutput" if dbg else "Internal"
    ln1s = nc.dram_tensor("ln1s", [C, T], BF16, kind=skind).ap()
    Ks = nc.dram_tensor("Ks", [C, T], BF16, kind=skind).ap()
    Vs = nc.dram_tensor("Vs", [T, HAUG], BF16, kind=skind).ap()
    Qs = nc.dram_tensor("Qs", [C, OWN], BF16, kind=skind).ap()
    dbg_y = nc.dram_tensor("dbg_y", [OWN, C], F32, kind="ExternalOutput").ap() if dbg else None
    dbg_ot = nc.dram_tensor("dbg_ot", [2, C, HALF], BF16, kind="ExternalOutput").ap() if dbg else None
    dbg_l2 = nc.dram_tensor("dbg_l2", [2, C, HALF], BF16, kind="ExternalOutput").ap() if dbg else None

    es = ExitStack()
    with es:
        LN2F0 = max(2 * PC * HALF, PC * HALF + PC * CH2, 4 * PC * SLAB)
        ARB_W = max(PC * C + 3 * PC * CHK,
                    PC * C + 2 * PC * CHK + 2 * HAUG + C,
                    PC * HALF + 2 * PC * KB + 2 * (KB // 128) * HAUG + 8 * KB + 512,
                    LN2F0 + PC * HALF + C,
                    LN2F0 + PC * HALF + (SLAB // 128) * C)
        ARF_W = max(QTH * HAUG + 2 * KB, QTH * C + 2 * C, QTH * C + HALF)
        ARB = es.enter_context(nc.sbuf_tensor("ARB", [128, ARB_W], BF16))
        ARF = es.enter_context(nc.sbuf_tensor("ARF", [128, ARF_W], F32))
        TMPBF = es.enter_context(nc.sbuf_tensor("TMPBF", [128, 2 * C + CHK], BF16))
        SMB = es.enter_context(nc.sbuf_tensor("SMB", [128, 256 + PC * 8 + H + CHK], BF16))
        SMF = es.enter_context(nc.sbuf_tensor("SMF", [128, 256 + QTH * H * 2 + 16 + QTH + cfg.NQT * H], F32))

        ar, af, tb, smb, smf = ARB.ap(), ARF.ap(), TMPBF.ap(), SMB.ap(), SMF.ap()

        ident = smb[:, 0:128]
        ones_bf = smb[:, 128:256]
        router_sb = smb[:, 256:256 + PC * 8]
        kmx2row = smb[0:1, 256 + PC * 8:256 + PC * 8 + H]
        rowtmp = smb[0:1, 256 + PC * 8 + H:256 + PC * 8 + H + CHK]

        qkvb_sb = smf[:, 0:3 * PC]
        qb_scaled = smf[:, 3 * PC:4 * PC]
        bnst = lambda p: smf[:, 64 + (p % 2) * 24: 64 + (p % 2) * 24 + 24]
        aggr = lambda p: smf[:, 112 + (p % 2) * 2: 112 + (p % 2) * 2 + 2]
        sdv = lambda p: smf[:, 116 + (p % 2): 117 + (p % 2)]
        rstd = lambda p: smf[:, 118 + (p % 2): 119 + (p % 2)]
        nmr = lambda p: smf[:, 120 + (p % 2): 121 + (p % 2)]
        nmcol = lambda p: smf[:, 128 + (p % 4): 129 + (p % 4)]
        dcol = lambda p: smf[:, 132 + (p % 4): 133 + (p % 4)]
        acol = lambda p: smf[:, 136 + (p % 4): 137 + (p % 4)]
        lrec = lambda p: smf[:, 140 + (p % 4):141 + (p % 4)]
        rlog = lambda p: smf[:, 144 + (p % 2) * 8: 144 + (p % 2) * 8 + 8]
        rtmp = smf[:, 160:168]
        rst = lambda j: smf[:, 168 + j:169 + j]
        MB0 = 192
        mcol = lambda qt, hd, par: smf[:, MB0 + ((qt % QTH) * H + hd) * 2 + (par % 2):
                                       MB0 + ((qt % QTH) * H + hd) * 2 + (par % 2) + 1]
        CC0 = MB0 + QTH * H * 2
        ccol = lambda qt: smf[:, CC0 + (qt % QTH): CC0 + (qt % QTH) + 1]
        NQ0 = CC0 + QTH
        nq2col = lambda qt, hd: smf[:, NQ0 + qt * H + hd: NQ0 + qt * H + hd + 1]

        XB0 = QTH * C
        xslot = lambda p: af[:, XB0 + (p % 2) * C: XB0 + (p % 2) * C + C]
        lnbf = lambda p: tb[:, (p % 2) * C:(p % 2) * C + C]
        sqA = tb[:, 2 * C:2 * C + CHK]

        semnames = ["T", "V", "S", "G", "SY"] + SYNC_LANES + GP_LANES
        sems = {n: es.enter_context(nc.semaphore(n)) for n in semnames}
        r = Rec()

        def ln_chain(xtile, par, outbf, reads0):
            t = None
            for j in range(NST):
                t = r.op("vector", lambda e, j=j: e.bn_stats(
                    out=bnst(par)[:, 6 * j:6 * j + 6], in_=xtile[:, j * STW:(j + 1) * STW]),
                    reads=(reads0 if j == 0 else []), sem="V" if j == NST - 1 else None)
            t_ag = r.op("vector", lambda e: e.bn_aggr(
                out=aggr(par), in_=bnst(par)[:, 0:6 * NST]), reads=[t], sem="V")
            t_vp = r.op("vector", lambda e: e.tensor_scalar_add(
                out=sdv(par), in0=aggr(par)[:, 1:2], scalar1=float(cfg.eps)),
                reads=[t_ag], sem="V")
            t_sd = r.op("scalar", lambda e: e.activation(
                out=sdv(par), in_=sdv(par), func=AF.Sqrt),
                reads=[t_vp], sem="S")
            t_rs = r.op("vector", lambda e: e.reciprocal(
                out=rstd(par), in_=sdv(par)), reads=[t_sd], sem="V")
            t_nm = r.op("vector", lambda e: e.tensor_scalar(
                out=nmr(par), in0=aggr(par)[:, 0:1], scalar1=rstd(par), scalar2=-1.0,
                op0=ALU.mult, op1=ALU.mult), reads=[t_rs], sem="V")
            t_norm = r.op("scalar", lambda e: e.activation(
                out=outbf, in_=xtile, func=AF.Identity, bias=nmr(par), scale=rstd(par)),
                reads=[t_nm], sem="S")
            return t_norm

        psA = ExitStack()
        pk = [psA.enter_context(nc.psum_tensor(f"pkA{i}", [128, CHK], F32)) for i in range(QD)]
        ptr = [psA.enter_context(nc.psum_tensor(f"ptA{i}", [128, 128], BF16)) for i in range(3)]
        pnA = psA.enter_context(nc.psum_tensor("pnA", [1, CHK], F32)).ap()
        pcol = [ptr[2].ap()[:, 0:1]]

        wk = ar[:, 0:PC * C]
        ln1f = lambda p: ar[:, PC * C + (p % 2) * PC * CHK: PC * C + (p % 2 + 1) * PC * CHK]
        kch = ar[:, PC * C + 2 * PC * CHK: PC * C + 3 * PC * CHK]

        t_ident = t_qb = r.dma_batch("sync", [(ident, identin[:, :]),
                                              (qkvb_sb, qkvb_cols[:, :])])
        t_ones = r.op("vector", lambda e: e.memset(ones_bf, 1.0), sem="V")
        t_qbs = r.op("vector", lambda e: e.tensor_scalar_mul(
            out=qb_scaled, in0=qkvb_sb[:, 0:PC], scalar1=float(cfg.qscale)),
            reads=[t_qb], sem="V")

        def phase_AC(src_x, nch, wsec, bias_cols, scale, dst, lnspill, knorm=False,
                     qnorm=False, end=True):
            hp = max(PC // 2, 1)
            t_w0 = r.dma_batch("gpsimd", [
                (wk[:, p * C:(p + 1) * C],
                 qkvT[p * 128:(p + 1) * 128, wsec * C:(wsec + 1) * C])
                for p in range(hp)])
            t_w1 = r.dma_batch("gpsimd", [
                (wk[:, p * C:(p + 1) * C],
                 qkvT[p * 128:(p + 1) * 128, wsec * C:(wsec + 1) * C])
                for p in range(hp, PC)]) if PC > 1 else t_w0
            guard = {}
            for ch in range(nch):
                lf = ln1f(ch)
                ev = None
                for tl in range(CHK // 128):
                    t0 = ch * CHK + tl * 128
                    xp = (ch * (CHK // 128) + tl) % 2
                    xtok = r.dma("sync", xslot(xp), src_x[t0:t0 + 128, :],
                                 reads=[guard.get(("xfree", xp))])
                    t_norm = ln_chain(xslot(xp), xp, lnbf(xp),
                                      [xtok, guard.get(("lnbf_free", xp))])
                    guard[("xfree", xp)] = t_norm
                    if PH_LVL < 2:
                        guard[("lnbf_free", xp)] = t_norm
                        continue
                    for p in range(PC):
                        pp = p % 3
                        t_tr = r.op("tensor", lambda e, p=p, pp=pp, xp=xp: e.transpose(
                            out=ptr[pp].ap()[:, :], in_=lnbf(xp)[:, p * 128:(p + 1) * 128],
                            identity=ident),
                            reads=[t_norm, t_ident, guard.get(("trev", pp)),
                                   (guard.get("pcolfree") if pp == 2 else None)], sem="T")
                        ev = r.op("vector", lambda e, p=p, pp=pp, tl=tl, lf=lf: e.tensor_copy(
                            out=lf[:, p * CHK + tl * 128: p * CHK + tl * 128 + 128],
                            in_=ptr[pp].ap()[:, :]),
                            reads=[t_tr, guard.get(("lfw", ch % 2))], sem="V")
                        guard[("trev", pp)] = ev
                    guard[("lnbf_free", xp)] = ev
                lf_ready = ev
                mm_last = None
                kevac = None
                if PH_LVL < 3:
                    continue
                for quad in range(NQD):
                    for ci in range(PC):
                        for j in range(QD):
                            od = quad * QD + j
                            mm_last = r.op("tensor", lambda e, ci=ci, od=od, j=j, lf=lf: e.matmul(
                                out=pk[j][:, :],
                                lhsT=wk[:, ci * C + od * 128: ci * C + od * 128 + 128],
                                rhs=lf[:, ci * CHK: ci * CHK + CHK],
                                start=(ci == 0), stop=(ci == PC - 1)),
                                reads=([lf_ready, t_w0, guard.get("kevac_all"), t_qbs]
                                       if (ci == 0 and j == 0) else
                                       ([t_w1] if (ci == hp and j == 0) else [])),
                                sem="T" if (ci == PC - 1 and j == QD - 1) else None)
                    for j in range(QD):
                        od = quad * QD + j
                        kevac = r.op("scalar", lambda e, od=od, j=j: e.activation(
                            out=kch[:, od * CHK:(od + 1) * CHK], in_=pk[j][:, :],
                            func=AF.Identity, bias=bias_cols[:, od:od + 1],
                            scale=float(scale)),
                            reads=[mm_last, guard.get("spill")], sem="S")
                        if knorm:
                            t_sq = r.op("gpsimd", lambda e, od=od: e.tensor_mul(
                                out=sqA, in0=kch[:, od * CHK:(od + 1) * CHK],
                                in1=kch[:, od * CHK:(od + 1) * CHK]),
                                reads=[kevac, guard.get("sqfree")], sem="G")
                            t_pn = r.op("tensor", lambda e: e.matmul(
                                out=pnA[:, :], lhsT=ones_bf[:, 0:1], rhs=sqA,
                                start=True, stop=True),
                                reads=[t_sq, t_ones, guard.get("pnfree")], sem="T")
                            guard["sqfree"] = t_pn
                            t_mx = r.op("vector", lambda e: e.reduce_max(
                                out=rowtmp[0:1, 0:1], in_=pnA[0:1, :], axis=AX.X),
                                reads=[t_pn, guard.get("rowfree")], sem="V")
                            guard["pnfree"] = t_mx
                            if guard.get(("km", od)) is None:
                                t_km = r.op("vector", lambda e, od=od: e.tensor_copy(
                                    out=kmx2row[0:1, od:od + 1], in_=rowtmp[0:1, 0:1]),
                                    reads=[t_mx], sem="V")
                            else:
                                t_km = r.op("vector", lambda e, od=od: e.tensor_tensor(
                                    out=kmx2row[0:1, od:od + 1], in0=kmx2row[0:1, od:od + 1],
                                    in1=rowtmp[0:1, 0:1], op=ALU.max),
                                    reads=[t_mx], sem="V")
                            guard[("km", od)] = t_km
                            guard["rowfree"] = t_km
                            r.kmtok = t_km
                    guard["kevac_all"] = kevac
                if PH_LVL >= 4:
                    sp = r.dma_batch("sync", [
                        (dst[od * 128:(od + 1) * 128, ch * CHK:(ch + 1) * CHK],
                         kch[:, od * CHK:(od + 1) * CHK]) for od in range(PC)],
                        reads=[kevac])
                    guard["spill"] = sp
                if qnorm:
                    for od in range(PC):
                        t_sq = r.op("gpsimd", lambda e, od=od: e.tensor_mul(
                            out=sqA, in0=kch[:, od * CHK:(od + 1) * CHK],
                            in1=kch[:, od * CHK:(od + 1) * CHK]),
                            reads=[kevac, guard.get("sqfree")], sem="G")
                        t_pn = r.op("tensor", lambda e: e.matmul(
                            out=pnA[:, :], lhsT=ones_bf[:, 0:1], rhs=sqA,
                            start=True, stop=True),
                            reads=[t_sq, t_ones, guard.get("pnfree")], sem="T")
                        guard["sqfree"] = t_pn
                        t_rw = r.op("vector", lambda e: e.tensor_copy(
                            out=rowtmp[0:1, 0:CHK], in_=pnA[0:1, :]),
                            reads=[t_pn, guard.get("rowfree")], sem="V")
                        guard["pnfree"] = t_rw
                        t_tc = None
                        for tl in range(CHK // 128):
                            qt_g = (ch * CHK) // 128 + tl
                            t_tc = r.op("tensor", lambda e, tl=tl: e.transpose(
                                out=pcol[0][:, :],
                                in_=rowtmp[0:1, tl * 128:(tl + 1) * 128],
                                identity=ident[0:1, 0:1]),
                                reads=[t_rw, guard.get("pcolfree"),
                                       guard.get(("trev", 2))], sem="T")
                            t_ev = r.op("scalar", lambda e, qt_g=qt_g, od=od, tl=tl: e.copy(
                                out=nq2col(qt_g, od), in_=pcol[0][:, :]),
                                reads=[t_tc], sem="S")
                            guard["pcolfree"] = t_ev
                            guard[("nq2", qt_g, od)] = t_ev
                        guard["rowfree"] = t_tc
                    r.nq2 = {k: v for k, v in guard.items() if isinstance(k, tuple) and k[0] == "nq2"}                         if not hasattr(r, "nq2") else {**r.nq2, **{k: v for k, v in guard.items() if isinstance(k, tuple) and k[0] == "nq2"}}
                if lnspill and PH_LVL >= 4:
                    sp2 = r.dma_batch("sync", [
                        (ln1s[p * 128:(p + 1) * 128, ch * CHK:(ch + 1) * CHK],
                         lf[:, p * CHK:(p + 1) * CHK]) for p in range(PC)],
                        reads=[lf_ready])
                    guard[("lfw", ch % 2)] = [mm_last, sp2]
                else:
                    guard[("lfw", ch % 2)] = [mm_last, lf_ready]
            if end:
                r.phase_end()
            return {"mm": mm_last, "kevac": kevac, "spill": guard.get("spill"),
                    "sq": guard.get("sqfree")}

        atoks = {}
        if lim >= 1:
            atoks = phase_AC(x_seq, T // CHK, 1, qkvb_sb[:, PC:2 * PC], 1.0, Ks,
                             lnspill=True, knorm=True, end=False)

        wv = wk
        VST0 = PC * C + 2 * PC * CHK
        vst = lambda p: ar[:, VST0 + (p % 2) * HAUG: VST0 + (p % 2 + 1) * HAUG]
        bv_row = ar[0:1, VST0 + 2 * HAUG: VST0 + 2 * HAUG + C]

        if lim >= 2:
            hpv = max(PC // 2, 1)
            t_wv0 = r.dma_batch("gpsimd", [
                (wv[:, p * C:(p + 1) * C], qkvT[p * 128:(p + 1) * 128, 2 * C:3 * C])
                for p in range(hpv)], reads=[atoks.get("mm"), atoks.get("sq")])
            t_wv = r.dma_batch("gpsimd", [
                (wv[:, p * C:(p + 1) * C], qkvT[p * 128:(p + 1) * 128, 2 * C:3 * C])
                for p in range(hpv, PC)]) if PC > 1 else t_wv0
            bhead = [atoks.get("mm"), atoks.get("kevac"), atoks.get("spill"),
                     atoks.get("sq")]
            t_bv = r.dma("gpsimd", bv_row, qkvb_flat[None, 2 * C:3 * C], reads=bhead)
            t_vi0 = r.op("vector", lambda e: e.memset(
                vst(0).rearrange("p (h x) -> p h x", x=129)[:, :, 128:129], 1.0),
                reads=bhead, sem="V")
            t_vi1 = r.op("vector", lambda e: e.memset(
                vst(1).rearrange("p (h x) -> p h x", x=129)[:, :, 128:129], 1.0), sem="V")

            gB = {}
            gB[("lfree", 0)] = [atoks.get("mm"), atoks.get("spill")]
            gB[("lfree", 1)] = [atoks.get("mm"), atoks.get("spill")]
            for _vd in range(NVD):
                gB[("pvfree", _vd)] = atoks.get("kevac")
            gB[("vsfree", 0)] = [atoks.get("spill"), atoks.get("sq")]
            gB[("vsfree", 1)] = [atoks.get("spill"), atoks.get("sq")]
            for ch in range(T // CHK):
                lf = ln1f(ch)
                t_lf = r.dma_batch("sync", [
                    (lf[:, p * CHK:(p + 1) * CHK],
                     ln1s[p * 128:(p + 1) * 128, ch * CHK:(ch + 1) * CHK])
                    for p in range(PC)], reads=[gB.get(("lfree", ch % 2))])
                mm_last = None
                for tl in range(CHK // 128):
                    nh = VDW // 128
                    for vd in range(NVD):
                        mm_last = r.op("tensor", lambda e, vd=vd: e.matmul(
                            out=pk[vd][:, 0:VDW], lhsT=ones_bf[0:1, 0:128],
                            rhs=bv_row[0:1, vd * VDW:(vd + 1) * VDW], start=True, stop=False),
                            reads=[t_lf, t_bv, t_ones, t_vi0, t_vi1,
                                   gB.get(("pvfree", vd))], sem=None)
                    for ci in range(PC):
                        for vd in range(NVD):
                            mm_last = r.op("tensor", lambda e, ci=ci, vd=vd, tl=tl, lf=lf: e.matmul(
                                out=pk[vd][:, 0:VDW],
                                lhsT=lf[:, ci * CHK + tl * 128: ci * CHK + tl * 128 + 128],
                                rhs=wv[:, ci * C + vd * VDW: ci * C + vd * VDW + VDW],
                                start=False, stop=(ci == PC - 1)),
                                reads=([t_wv0] if (ci == 0 and vd == 0) else
                                       ([t_wv] if (ci == hpv and vd == 0) else [])),
                                sem="T" if (ci == PC - 1 and vd == NVD - 1) else None)
                    tvev = None
                    for vd in range(NVD):
                        tvev = r.op("vector", lambda e, vd=vd, tl=tl, nh=nh: e.tensor_copy(
                            out=vst(tl).rearrange("p (h x) -> p h x", x=129)[:, vd * nh:(vd + 1) * nh, 0:128],
                            in_=pk[vd][:, 0:VDW].rearrange("p (h x) -> p h x", x=128)),
                            reads=[mm_last, gB.get(("vsfree", tl % 2))], sem="V")
                        gB[("pvfree", vd)] = tvev
                    t0 = ch * CHK + tl * 128
                    gB[("vsfree", tl % 2)] = r.dma("sync", Vs[t0:t0 + 128, :], vst(tl),
                                                   reads=[tvev])
                gB[("lfree", ch % 2)] = mm_last
            r.phase_end()

        if lim >= 3:
            phase_AC(x_own, OWN // CHK, 0, qb_scaled, cfg.qscale, Qs, lnspill=False,
                     qnorm=True)

        psA.close()

        qbuf = ar[:, 0:PC * HALF]
        KB0 = PC * HALF
        kbuf = lambda p: ar[:, KB0 + (p % 2) * PC * KB: KB0 + (p % 2 + 1) * PC * KB]
        VB0 = KB0 + 2 * PC * KB
        vbuf = lambda p: ar[:, VB0 + (p % 2) * (KB // 128) * HAUG:
                            VB0 + (p % 2 + 1) * (KB // 128) * HAUG]
        PB0 = VB0 + 2 * (KB // 128) * HAUG
        pbuf = lambda p: ar[:, PB0 + (p % 4) * KB: PB0 + (p % 4 + 1) * KB]
        PTB0 = PB0 + 4 * KB
        ptb = lambda p: ar[:, PTB0 + (p % 4) * KB: PTB0 + (p % 4 + 1) * KB]
        OFB0 = PTB0 + 4 * KB
        ofbf = lambda p: ar[:, OFB0 + (p % 4) * 128: OFB0 + (p % 4 + 1) * 128]

        obuf = lambda qt: af[:, (qt % QTH) * HAUG: (qt % QTH + 1) * HAUG]
        MKOFF = QTH * HAUG
        mslot = lambda p: af[:, MKOFF + (p % 2) * KB: MKOFF + (p % 2 + 1) * KB]

        WO0 = PC * HALF
        wo = ar[:, WO0: WO0 + PC * CH2]
        ln2f = ar[:, LN2F0: LN2F0 + PC * HALF]
        OBR0 = LN2F0 + PC * HALF
        ob_row = ar[0:1, OBR0: OBR0 + C]
        ybuf = lambda qt: af[:, (qt % QTH) * C: (qt % QTH + 1) * C]

        gsb = lambda p: ar[:, (p % 2) * PC * SLAB: (p % 2 + 1) * PC * SLAB]
        usb = lambda p: ar[:, 2 * PC * SLAB + (p % 2) * PC * SLAB:
                           2 * PC * SLAB + (p % 2 + 1) * PC * SLAB]
        DS0 = LN2F0 + PC * HALF
        dsb = ar[:, DS0: DS0 + (SLAB // 128) * C]
        hsb = tb[:, 0:(SLAB // 128) * HALF]
        silu_t = af[:, XB0: XB0 + HALF]

        t_rt = r.dma_batch("gpsimd", [
            (router_sb[:, p * 8:(p + 1) * 8], routerT[p * 128:(p + 1) * 128, 0:8])
            for p in range(PC)])

        mask_idx = 0
        out_toks = []
        t_ob = None
        for half in range(2):
            if lim < 4 + 3 * half:
                break
            qts = list(range(half * QTH, (half + 1) * QTH))
            maxcov = max(cov_of(cfg, i) for i in qts)

            psD = ExitStack()
            ps_s = [psD.enter_context(nc.psum_tensor(f"psS{half}{i}", [128, KB], F32)) for i in range(3)]
            ptp = [psD.enter_context(nc.psum_tensor(f"ptD{half}{i}", [128, KB], BF16)).ap()[:, :] for i in range(3)]
            pav = [psD.enter_context(nc.psum_tensor(f"pav{half}{i}", [128, 129], F32)).ap()[:, :] for i in range(2)]

            t_qbuf = r.dma_batch("sync", [
                (qbuf[:, p * HALF:(p + 1) * HALF],
                 Qs[p * 128:(p + 1) * 128, half * HALF:(half + 1) * HALF])
                for p in range(PC)])

            bset = {}
            t_bcast = r.op("tensor", lambda e: e.matmul(
                out=pav[0][:, 0:H], lhsT=ones_bf[0:1, 0:128],
                rhs=kmx2row[0:1, 0:H], start=True, stop=True),
                reads=[r.kmtok, t_ones], sem="T")
            tms = {}
            for qt in qts:
                for hd in range(H):
                    tms[(qt, hd)] = r.op("vector", lambda e, qt=qt, hd=hd: e.tensor_mul(
                        out=mcol(qt, hd, 1), in0=nq2col(qt, hd), in1=pav[0][:, hd:hd + 1]),
                        reads=[t_bcast, r.nq2[("nq2", qt, hd)]], sem="V")
            tsqs = {}
            for qt in qts:
                for hd in range(H):
                    tsqs[(qt, hd)] = r.op("scalar", lambda e, qt=qt, hd=hd: e.activation(
                        out=mcol(qt, hd, 1), in_=mcol(qt, hd, 1), func=AF.Sqrt),
                        reads=[tms[(qt, hd)]], sem="S")
            for qt in qts:
                for hd in range(H):
                    t_bn = r.op("vector", lambda e, qt=qt, hd=hd: e.tensor_scalar_mul(
                        out=mcol(qt, hd, 0), in0=mcol(qt, hd, 1), scalar1=-1.0),
                        reads=[tsqs[(qt, hd)]], sem="V")
                    bset[(qt, hd)] = t_bn

            kv_free, o_tok, s_free, p_free, pt_free, av_free, m_free = {}, {}, {}, {}, {}, {}, {}
            tup = 0
            last_s_mm = None
            kv_done = {}
            pend = []
            pend2 = []
            bkc = [0]
            avc = [0]

            def flush_mid():
                st = pend.pop(0)
                bp = bkc[0] % 3
                ap_ = bkc[0] % 4
                bkc[0] += 1
                pp = st["pp"]
                t_tr = None
                for j in range(KB // 128):
                    t_tr = r.op("tensor", lambda e, j=j, pp=pp, bp=bp: e.transpose(
                        out=ptp[bp][:, j * 128:(j + 1) * 128],
                        in_=pbuf(pp)[:, j * 128:(j + 1) * 128], identity=ident),
                        reads=([st["t_p"], pt_free.get(bp)] if j == 0 else []),
                        sem="T" if j == KB // 128 - 1 else None)
                t_ptc = r.op("vector", lambda e, ap_=ap_, bp=bp: e.tensor_copy(
                    out=ptb(ap_), in_=ptp[bp][:, :]),
                    reads=[t_tr, av_free.get((ap_, "ptb"))], sem="V")
                p_free[pp] = t_ptc
                pt_free[bp] = t_ptc
                st["ap_"] = ap_
                st["t_ptc"] = t_ptc
                pend2.append(st)

            def flush_back():
                st = pend2.pop(0)
                ap_ = st["ap_"]
                vp_ = avc[0] % 2
                avc[0] += 1
                qt, hd, kp = st["qt"], st["hd"], st["kp"]
                t_av = None
                for j in range(KB // 128):
                    t_av = r.op("tensor", lambda e, j=j, ap_=ap_, vp_=vp_, hd=hd, kp=kp: e.matmul(
                        out=pav[vp_][:, :],
                        lhsT=ptb(ap_)[:, j * 128:(j + 1) * 128],
                        rhs=vbuf(kp)[:, j * HAUG + hd * 129: j * HAUG + hd * 129 + 129],
                        start=(j == 0), stop=(j == KB // 128 - 1)),
                        reads=([st["t_ptc"], av_free.get((vp_, "pav"))] if j == 0 else []),
                        sem="T" if j == KB // 128 - 1 else None)
                av_free[(ap_, "ptb")] = t_av
                osl = obuf(qt).rearrange("p (h x) -> p h x", x=129)[:, hd:hd + 1, :].squeeze(1)
                if st["first"]:
                    t_o = r.op("vector", lambda e, osl=osl, vp_=vp_: e.tensor_copy(
                        out=osl, in_=pav[vp_][:, :]),
                        reads=[t_av, o_tok.get((qt, hd))], sem="V")
                else:
                    t_o = r.op("vector", lambda e, osl=osl, vp_=vp_: e.tensor_add(
                        out=osl, in0=osl, in1=pav[vp_][:, :]),
                        reads=[t_av, o_tok.get((qt, hd))], sem="V")
                o_tok[(qt, hd)] = t_o
                o_tok[(qt, hd, "mf")] = t_o
                av_free[(vp_, "pav")] = t_o
                if st["last_of_kb"]:
                    kv_done[st["kb"]] = t_o
                return t_o

            LAG = 2
            LAG2 = 2
            for kb in range(maxcov):
                kp = kb % 2
                kvpairs = [(kbuf(kp)[:, p * KB:(p + 1) * KB],
                            Ks[p * 128:(p + 1) * 128, kb * KB:(kb + 1) * KB])
                           for p in range(PC)]
                kvpairs += [(vbuf(kp)[:, j * HAUG:(j + 1) * HAUG],
                             Vs[kb * KB + j * 128: kb * KB + (j + 1) * 128, :])
                            for j in range(KB // 128)]
                t_kv = r.dma_batch("sync", kvpairs, reads=[kv_done.get(kb - 2)])
                active = [q for q in qts if kb < cov_of(cfg, q)]
                for qn, qt in enumerate(active):
                    qrel = qt % QTH
                    need_mask = open_of(cfg, qt) <= kb
                    t_mask = None
                    mp = mask_idx % 2
                    if need_mask:
                        t_mask = r.dma("sync", mslot(mp), masks[mask_idx, :, :],
                                       reads=[m_free.get(mp)])
                        mask_idx += 1
                    first = (kb == 0)
                    for hd in range(H):
                        pp = tup % 4
                        ps = tup % 3
                        t_smm = r.op("tensor", lambda e, hd=hd, qrel=qrel, kp=kp, ps=ps: e.matmul(
                            out=ps_s[ps][:, :],
                            lhsT=qbuf[:, hd * HALF + qrel * 128: hd * HALF + qrel * 128 + 128],
                            rhs=kbuf(kp)[:, hd * KB:(hd + 1) * KB],
                            start=True, stop=True),
                            reads=[t_qbuf, t_kv, s_free.get(ps)], sem="T")
                        last_s_mm = t_smm
                        t_sm = t_smm
                        if need_mask:
                            t_sm = r.op("vector", lambda e, ps=ps, mp=mp: e.tensor_add(
                                out=ps_s[ps][:, :], in0=ps_s[ps][:, :], in1=mslot(mp)),
                                reads=[t_smm, t_mask], sem="V")
                            m_free[mp] = t_sm
                        t_p = r.op("scalar", lambda e, pp=pp, ps=ps, qt=qt, hd=hd: e.activation(
                            out=pbuf(pp), in_=ps_s[ps][:, :], func=AF.Exp,
                            bias=mcol(qt, hd, 0)),
                            reads=[t_sm, bset[(qt, hd)], p_free.get(pp)], sem="S")
                        s_free[ps] = t_p
                        pend.append({"pp": pp, "qt": qt, "hd": hd, "kp": kp,
                                     "first": first, "t_p": t_p,
                                     "kb": kb,
                                     "last_of_kb": (qn == len(active) - 1 and hd == H - 1)})
                        if len(pend) > LAG:
                            flush_mid()
                        if len(pend2) > LAG2:
                            flush_back()
                        tup += 1
            while pend:
                flush_mid()
            while pend2:
                flush_back()

            fin_ev = {}
            t_fin_last = None
            fpend = []
            fbk = [0]

            def flush_fin():
                stf = fpend.pop(0)
                fp2 = fbk[0] % 3
                fbk[0] += 1
                t_tr = r.op("tensor", lambda e, fp=stf["fp"], fp2=fp2: e.transpose(
                    out=ptp[fp2][:, 0:128], in_=ofbf(fp), identity=ident),
                    reads=[stf["t_of"], fin_ev.get(("ev", fp2)), last_s_mm], sem="T")
                t_ev = r.op("scalar", lambda e, fp2=fp2, hd=stf["hd"], qrel=stf["qrel"]: e.copy(
                    out=qbuf[:, hd * HALF + qrel * 128: hd * HALF + qrel * 128 + 128],
                    in_=ptp[fp2][:, 0:128]),
                    reads=[t_tr], sem="S")
                fin_ev[("ev", fp2)] = t_ev
                fin_ev[("trdone", stf["fp"])] = t_ev
                return t_ev

            fi = 0
            for qt in qts:
                qrel = qt % QTH
                for hd in range(H):
                    osl = obuf(qt).rearrange("p (h x) -> p h x", x=129)[:, hd:hd + 1, :].squeeze(1)
                    fp = fi % 4
                    t_rl = r.op("vector", lambda e, osl=osl, fp=fp: e.reciprocal(
                        out=lrec(fp), in_=osl[:, 128:129]),
                        reads=[o_tok[(qt, hd)], fin_ev.get(("lfree", fp))], sem="V")
                    t_of = r.op("vector", lambda e, osl=osl, fp=fp: e.tensor_scalar_mul(
                        out=ofbf(fp), in0=osl[:, 0:128], scalar1=lrec(fp)),
                        reads=[t_rl, fin_ev.get(("trdone", fp))], sem="V")
                    fin_ev[("lfree", fp)] = t_of
                    fpend.append({"fp": fp, "t_of": t_of, "hd": hd, "qrel": qrel})
                    if len(fpend) > LAG:
                        t_fin_last = flush_fin()
                    fi += 1
            while fpend:
                t_fin_last = flush_fin()
            if dbg:
                r.dma_batch("sync", [
                    (dbg_ot[half, p * 128:(p + 1) * 128, :], qbuf[:, p * HALF:(p + 1) * HALF])
                    for p in range(PC)], reads=[t_fin_last])
            r.phase_end()

            if lim < 5 + 3 * half:
                psD.close()
                break
            y_tok, x_used, xfree, ln2_toks = {}, {}, {}, {}
            mm = None
            for coh in range(2):
                wt = r.dma_batch("gpsimd", [
                    (wo[:, p * CH2:(p + 1) * CH2],
                     outwT[p * 128:(p + 1) * 128, coh * CH2:(coh + 1) * CH2])
                    for p in range(PC)], reads=[y_tok.get("wofree")])
                if coh == 0:
                    t_ob = r.dma("gpsimd", ob_row, outb[None, :])
                unit = 0
                for qt in qts:
                    qrel = qt % QTH
                    xp = qt % 2
                    x_used[(qt, coh)] = r.dma("sync", xslot(xp),
                                              x_own[qt * 128:(qt + 1) * 128, :],
                                              reads=[xfree.get(xp)])
                    for cc in range(NCC):
                        up = unit % 2
                        mm = r.op("tensor", lambda e, coh=coh, cc=cc, up=up: e.matmul(
                            out=ps_s[up][:, 0:CCW], lhsT=ones_bf[0:1, 0:128],
                            rhs=ob_row[0:1, coh * CH2 + cc * CCW: coh * CH2 + cc * CCW + CCW],
                            start=True, stop=False),
                            reads=[wt, t_ob, y_tok.get(("pofree", up)), t_fin_last], sem=None)
                        for ci in range(PC):
                            mm = r.op("tensor", lambda e, ci=ci, cc=cc, coh=coh, qrel=qrel, up=up: e.matmul(
                                out=ps_s[up][:, 0:CCW],
                                lhsT=qbuf[:, ci * HALF + qrel * 128: ci * HALF + qrel * 128 + 128],
                                rhs=wo[:, ci * CH2 + cc * CCW: ci * CH2 + cc * CCW + CCW],
                                start=False, stop=(ci == PC - 1)),
                                sem="T" if ci == PC - 1 else None)
                        t_ev = r.op("vector", lambda e, qt=qt, coh=coh, cc=cc, xp=xp, up=up: e.scalar_tensor_tensor(
                            out=ybuf(qt)[:, coh * CH2 + cc * CCW: coh * CH2 + cc * CCW + CCW],
                            in0=xslot(xp)[:, coh * CH2 + cc * CCW: coh * CH2 + cc * CCW + CCW],
                            scalar=1.0, in1=ps_s[up][:, 0:CCW], op0=ALU.mult, op1=ALU.add),
                            reads=[mm, x_used[(qt, coh)], y_tok.get(("yfree", qt))], sem="V")
                        y_tok[("pofree", up)] = t_ev
                        xfree[xp] = t_ev
                        unit += 1
                    y_tok[(qt, coh)] = t_ev
                y_tok["wofree"] = mm
            t_l2last = None
            for qt in qts:
                qrel = qt % QTH
                xp = qt % 2
                t_norm = ln_chain(ybuf(qt), xp, lnbf(xp),
                                  [y_tok[(qt, 0)], y_tok[(qt, 1)], ln2_toks.get(("lnbf_free", xp))])
                ev = None
                for p in range(PC):
                    pp = p % 2
                    t_tr = r.op("tensor", lambda e, p=p, pp=pp, xp=xp: e.transpose(
                        out=ptp[pp][:, 0:128], in_=lnbf(xp)[:, p * 128:(p + 1) * 128],
                        identity=ident),
                        reads=[t_norm, ln2_toks.get(("ev", pp))], sem="T")
                    ev = r.op("scalar", lambda e, p=p, pp=pp, qrel=qrel: e.copy(
                        out=ln2f[:, p * HALF + qrel * 128: p * HALF + qrel * 128 + 128],
                        in_=ptp[pp][:, 0:128]),
                        reads=[t_tr], sem="S")
                    ln2_toks[("ev", pp)] = ev
                ln2_toks[("lnbf_free", xp)] = ev
                ln2_toks[qt] = ev
                t_l2last = ev
            if dbg:
                for qt in qts:
                    r.dma("sync", dbg_y[qt * 128:(qt + 1) * 128, :], ybuf(qt),
                          reads=[y_tok[(qt, 0)], y_tok[(qt, 1)]])
                r.dma_batch("sync", [
                    (dbg_l2[half, p * 128:(p + 1) * 128, :], ln2f[:, p * HALF:(p + 1) * HALF])
                    for p in range(PC)], reads=[t_l2last])
            r.phase_end()
            psD.close()

            if lim < 6 + 3 * half:
                break
            psF = ExitStack()
            pgF = psF.enter_context(nc.psum_tensor(f"pgF{half}", [128, HALF], F32))
            puF = psF.enter_context(nc.psum_tensor(f"puF{half}", [128, HALF], F32))
            pyF = [psF.enter_context(nc.psum_tensor(f"pyF{half}{i}", [128, YW], F32)) for i in range(2)]

            rt_ev = {}
            for qt in qts:
                qrel = qt % QTH
                up = qt % 2
                mm = None
                for ci in range(PC):
                    mm = r.op("tensor", lambda e, ci=ci, qrel=qrel: e.matmul(
                        out=pgF[:, 0:8],
                        lhsT=ln2f[:, ci * HALF + qrel * 128: ci * HALF + qrel * 128 + 128],
                        rhs=router_sb[:, ci * 8:(ci + 1) * 8],
                        start=(ci == 0), stop=(ci == PC - 1)),
                        reads=([t_rt, t_l2last, rt_ev.get("prfree")] if ci == 0 else []),
                        sem="T" if ci == PC - 1 else None)
                lg = rlog(up)
                t_lg = r.op("vector", lambda e, lg=lg: e.tensor_copy(out=lg, in_=pgF[:, 0:8]),
                            reads=[mm, rt_ev.get(("lgfree", up))], sem="V")
                rt_ev["prfree"] = t_lg
                t_nmx = r.op("vector", lambda e, lg=lg: e.reduce_max(
                    out=rst(0), in_=lg, axis=AX.X, negate=True), reads=[t_lg, rt_ev.get("st")], sem="V")
                t_d1 = r.op("vector", lambda e, lg=lg: e.tensor_scalar_add(
                    out=lg, in0=lg, scalar1=rst(0)), reads=[t_nmx], sem="V")
                t_eq = r.op("vector", lambda e, lg=lg: e.tensor_single_scalar(
                    out=rtmp, in_=lg, scalar=0.0, op=ALU.is_ge), reads=[t_d1], sem="V")
                t_cnt = r.op("vector", lambda e: e.reduce_sum(
                    out=rst(1), in_=rtmp, axis=AX.X), reads=[t_eq], sem="V")
                t_msk = r.op("vector", lambda e, lg=lg: e.scalar_tensor_tensor(
                    out=rtmp, in0=rtmp, scalar=NEG, in1=lg, op0=ALU.mult, op1=ALU.add),
                    reads=[t_eq, t_cnt], sem="V")
                t_n2 = r.op("vector", lambda e: e.reduce_max(
                    out=rst(2), in_=rtmp, axis=AX.X, negate=True), reads=[t_msk], sem="V")
                t_tie = r.op("vector", lambda e: e.tensor_single_scalar(
                    out=rst(3), in_=rst(1), scalar=2.0, op=ALU.is_ge), reads=[t_cnt], sem="V")
                t_de = r.op("vector", lambda e: e.tensor_scalar(
                    out=rst(4), in0=rst(3), scalar1=-1.0, scalar2=1.0,
                    op0=ALU.mult, op1=ALU.add), reads=[t_tie], sem="V")
                t_de2 = r.op("vector", lambda e: e.tensor_mul(
                    out=rst(5), in0=rst(2), in1=rst(4)), reads=[t_n2, t_de], sem="V")
                t_u2 = r.op("scalar", lambda e: e.activation(
                    out=rst(6), in_=rst(5), func=AF.Exp, scale=-1.0), reads=[t_de2], sem="S")
                t_ss = r.op("vector", lambda e: e.tensor_scalar_add(
                    out=rst(7), in0=rst(6), scalar1=1.0), reads=[t_u2], sem="V")
                t_rc = r.op("vector", lambda e: e.reciprocal(
                    out=rst(8), in_=rst(7)), reads=[t_ss], sem="V")
                t_c = r.op("vector", lambda e, qt=qt: e.tensor_mul(
                    out=ccol(qt), in0=rst(7), in1=rst(8)), reads=[t_rc], sem="V")
                rt_ev[qt] = t_c
                rt_ev[("lgfree", up)] = t_d1
                rt_ev["st"] = t_c

            ff = {}
            ff["pgfree"] = rt_ev["prfree"]
            slabs = [("r", js) for js in range(cfg.RINTER // SLAB)] + \
                    [("s", js) for js in range(cfg.INTER // SLAB)]
            for sn, (stype, js) in enumerate(slabs):
                gsrc, usrc, dsrc = (rgT, ruT, rdT) if stype == "r" else (gT, uT, dT)
                sp = sn % 2
                tg = r.dma_batch("gpsimd", [
                    (gsb(sp)[:, p * SLAB:(p + 1) * SLAB],
                     gsrc[p * 128:(p + 1) * 128, js * SLAB:(js + 1) * SLAB])
                    for p in range(PC)], reads=[ff.get(("gfree", sp))])
                tu = r.dma_batch("gpsimd", [
                    (usb(sp)[:, p * SLAB:(p + 1) * SLAB],
                     usrc[p * 128:(p + 1) * 128, js * SLAB:(js + 1) * SLAB])
                    for p in range(PC)], reads=[ff.get(("ufree", sp))])
                td = r.dma_batch("gpsimd", [
                    (dsb[:, j * C:(j + 1) * C],
                     dsrc[js * SLAB + j * 128: js * SLAB + (j + 1) * 128, :])
                    for j in range(SLAB // 128)], reads=[ff.get("dfree")])
                dn_last = None
                for j in range(SLAB // 128):
                    mmg = None
                    for tk in range(NTK):
                        for ci in range(PC):
                            mmg = r.op("tensor", lambda e, ci=ci, j=j, tk=tk, sp=sp: e.matmul(
                                out=pgF[:, tk * TW:(tk + 1) * TW],
                                lhsT=gsb(sp)[:, ci * SLAB + j * 128: ci * SLAB + j * 128 + 128],
                                rhs=ln2f[:, ci * HALF + tk * TW: ci * HALF + tk * TW + TW],
                                start=(ci == 0), stop=(ci == PC - 1)),
                                reads=([tg, ff.get("pgfree")] if ci == 0 and tk == 0 else []),
                                sem="T" if (ci == PC - 1 and tk == NTK - 1) else None)
                    mmu = None
                    for tk in range(NTK):
                        for ci in range(PC):
                            mmu = r.op("tensor", lambda e, ci=ci, j=j, tk=tk, sp=sp: e.matmul(
                                out=puF[:, tk * TW:(tk + 1) * TW],
                                lhsT=usb(sp)[:, ci * SLAB + j * 128: ci * SLAB + j * 128 + 128],
                                rhs=ln2f[:, ci * HALF + tk * TW: ci * HALF + tk * TW + TW],
                                start=(ci == 0), stop=(ci == PC - 1)),
                                reads=([tu, ff.get("pufree")] if ci == 0 and tk == 0 else []),
                                sem="T" if (ci == PC - 1 and tk == NTK - 1) else None)
                    t_si = r.op("scalar", lambda e: e.activation(
                        out=silu_t, in_=pgF[:, :], func=AF.Silu),
                        reads=[mmg, ff.get("silufree")], sem="S")
                    ff["pgfree"] = t_si
                    t_h = r.op("vector", lambda e, j=j: e.tensor_tensor(
                        out=hsb[:, j * HALF:(j + 1) * HALF], in0=silu_t, in1=puF[:, :],
                        op=ALU.mult),
                        reads=[t_si, mmu, ff.get(("hfree", j))], sem="V")
                    ff["pufree"] = t_h
                    ff["silufree"] = t_h
                    ff[("h", j)] = t_h
                unit = 0
                for qt in qts:
                    qrel = qt % QTH
                    for yc in range(NYC):
                        up = unit % 2
                        mm = None
                        ncc2 = max(YW // 512, 1)
                        ccw2 = min(512, YW)
                        for j in range(SLAB // 128):
                            for cc in range(ncc2):
                                mm = r.op("tensor", lambda e, j=j, cc=cc, yc=yc, qrel=qrel, up=up, ccw2=ccw2: e.matmul(
                                    out=pyF[up][:, cc * ccw2:(cc + 1) * ccw2],
                                    lhsT=hsb[:, j * HALF + qrel * 128: j * HALF + qrel * 128 + 128],
                                    rhs=dsb[:, j * C + yc * YW + cc * ccw2: j * C + yc * YW + cc * ccw2 + ccw2],
                                    start=(j == 0), stop=(j == SLAB // 128 - 1)),
                                    reads=([td, ff.get(("pyfree", up)), ff[("h", j)]]
                                           if cc == 0 else [ff[("h", j)]]),
                                    sem="T" if (j == SLAB // 128 - 1 and cc == ncc2 - 1) else None)
                        dn_last = mm
                        if stype == "r":
                            t_ev = r.op("vector", lambda e, qt=qt, yc=yc, up=up: e.scalar_tensor_tensor(
                                out=ybuf(qt)[:, yc * YW:(yc + 1) * YW],
                                in0=pyF[up][:, :], scalar=ccol(qt),
                                in1=ybuf(qt)[:, yc * YW:(yc + 1) * YW],
                                op0=ALU.mult, op1=ALU.add),
                                reads=[mm, rt_ev[qt], ff.get(("ylast", qt, yc)),
                                       ln2_toks[qt]], sem="V")
                        else:
                            t_ev = r.op("vector", lambda e, qt=qt, yc=yc, up=up: e.tensor_add(
                                out=ybuf(qt)[:, yc * YW:(yc + 1) * YW],
                                in0=ybuf(qt)[:, yc * YW:(yc + 1) * YW],
                                in1=pyF[up][:, :]),
                                reads=[mm, ff.get(("ylast", qt, yc)), ln2_toks[qt]], sem="V")
                        ff[("pyfree", up)] = t_ev
                        ff[("ylast", qt, yc)] = t_ev
                        unit += 1
                for j in range(SLAB // 128):
                    ff[("hfree", j)] = dn_last
                ff[("gfree", sp)] = dn_last
                ff[("ufree", sp)] = dn_last
                ff["dfree"] = dn_last
            for qt in qts:
                reads = [ff[("ylast", qt, yc)] for yc in range(NYC)]
                tko = r.dma("sync", out_own[qt * 128:(qt + 1) * 128, :], ybuf(qt),
                            reads=reads)
                out_toks.append(tko)
            r.phase_end()
            psF.close()

        if out_toks:
            r.op("sync", lambda e: e.nop(), reads=out_toks)
        else:
            r.op("sync", lambda e: e.nop(), reads=r.phase_end())

        blk = es.enter_context(nc.Block())
        r.emit(blk, sems)

    return nc



def _host_inputs(cfg, x, qkv_w, qkv_b, out_w, out_b, router_w,
                 sh_gate, sh_up, sh_down, r_gate, r_up, r_down):
    C, T, OWN, A = cfg.C, cfg.T, cfg.OWN, cfg.A
    import ml_dtypes
    shared = {
        "qkvT": np.ascontiguousarray(np.asarray(qkv_w, np.float32).T),
        "qkvb_cols": np.ascontiguousarray(np.asarray(qkv_b, np.float32).reshape(3 * C // 128, 128).T),
        "qkvb_flat": np.ascontiguousarray(np.asarray(qkv_b, np.float32)),
        "outwT": np.ascontiguousarray(np.asarray(out_w, np.float32).T),
        "outb": np.ascontiguousarray(np.asarray(out_b, np.float32)),
        "gT": np.ascontiguousarray(np.asarray(sh_gate, np.float32).T),
        "uT": np.ascontiguousarray(np.asarray(sh_up, np.float32).T),
        "dT": np.ascontiguousarray(np.asarray(sh_down, np.float32).T),
        "rgT": np.ascontiguousarray(np.asarray(r_gate, np.float32).T),
        "ruT": np.ascontiguousarray(np.asarray(r_up, np.float32).T),
        "rdT": np.ascontiguousarray(np.asarray(r_down, np.float32).T),
        "routerT": np.ascontiguousarray(np.asarray(router_w, np.float32).T),
        "identin": np.eye(128, dtype=ml_dtypes.bfloat16),
    }
    masks_by_cls = [build_masks(cfg, 0), build_masks(cfg, 1)]
    B = x.shape[0]
    in_maps = []
    for c in range(2 * B):
        b, cls = c // 2, c % 2
        xs = np.ascontiguousarray(x[b])
        xo = np.ascontiguousarray(x[b].reshape(T // 128, 128, C)[cls::2].reshape(OWN, C))
        in_maps.append({"x_seq": xs, "x_own": xo, "masks": masks_by_cls[cls], **shared})
    return in_maps


def _scatter(cfg, outs, B):
    T, OWN, C = cfg.T, cfg.OWN, cfg.C
    full = np.empty((B, T, C), np.float32)
    for c, o in enumerate(outs):
        b, cls = c // 2, c % 2
        full[b].reshape(T // 128, 128, C)[cls::2] = o.reshape(OWN // 128, 128, C)
    return full


_NC_CACHE = {}


def kernel(L0_attn_x, L0_attn_qkv_w, L0_attn_qkv_b, L0_attn_out_w, L0_attn_out_b,
           L0_ffn_router_weight, L0_ffn_sh_gate, L0_ffn_sh_up, L0_ffn_sh_down,
           L0_ffn_r_gate, L0_ffn_r_up, L0_ffn_r_down):
    cfg = FULL
    x = np.asarray(L0_attn_x, np.float32)
    in_maps = _host_inputs(cfg, x, L0_attn_qkv_w, L0_attn_qkv_b, L0_attn_out_w,
                           L0_attn_out_b, L0_ffn_router_weight, L0_ffn_sh_gate,
                           L0_ffn_sh_up, L0_ffn_sh_down, L0_ffn_r_gate,
                           L0_ffn_r_up, L0_ffn_r_down)
    if "nc" not in _NC_CACHE:
        _NC_CACHE["nc"] = build(cfg)
    nc = _NC_CACHE["nc"]
    res = run_bass_kernel_spmd(nc, in_maps, core_ids=list(range(8)))
    outs = [r["out_own"] for r in res.results]
    return _scatter(cfg, outs, x.shape[0])
